# version 01
# baseline (speedup 1.0000x reference)
import contextlib
import numpy as np
import concourse.bass as bass
import concourse.mybir as mybir
from concourse.bass_utils import run_bass_kernel_spmd

F32 = mybir.dt.float32
BF16 = mybir.dt.bfloat16
I32 = mybir.dt.int32
AF = mybir.ActivationFunctionType
ALU = mybir.AluOpType

NCORE = 8
TOK = 2048
VD = 4096
DD = 512
FF = 2048
NL = 4
CK = 512
NCH = TOK // CK
NSLAB = 4
SLAB = TOK // NSLAB
NTT = 4
TT = 512
NGV = VD // 128
NGD = DD // 128
NGF = FF // 128
EPS = 1e-5
RSQRT_MAGIC = 1597463007.0

DEBUG = False
STAGES = 3


class VSafe:

    HORIZON = 4

    def __init__(self, e, state):
        self._e = e
        self._s = state

    @staticmethod
    def _names(aps):
        s = set()
        for a in aps:
            if a is None or isinstance(a, (int, float)):
                continue
            t = getattr(a, "tensor", None)
            if t is not None:
                s.add(t.name)
        return s

    def _do(self, ins, outs, call):
        ii = self._names(ins)
        oo = self._names(outs)
        for pi, po in self._s["hist"]:
            if (ii & po) or (oo & po) or (oo & pi):
                self._e.drain()
                self._s["hist"] = []
                break
        r = call()
        self._s["hist"].append((ii, oo))
        del self._s["hist"][:-self.HORIZON]
        return r

    def tensor_copy(self, out, in_):
        return self._do([in_], [out], lambda: self._e.tensor_copy(out=out, in_=in_))

    def memset(self, ap, c):
        r = self._e.memset(ap, c)
        nm = self._names([ap])
        if self._s["hist"]:
            pi, po = self._s["hist"][-1]
            self._s["hist"][-1] = (pi, po | nm)
        else:
            self._s["hist"].append((set(), nm))
        return r

    def tensor_scalar(self, out, in0, scalar1, scalar2, op0, op1=None):
        kw = {} if op1 is None else {"op1": op1}
        return self._do([in0, scalar1, scalar2], [out],
                        lambda: self._e.tensor_scalar(out=out, in0=in0, scalar1=scalar1,
                                                      scalar2=scalar2, op0=op0, **kw))

    def tensor_tensor(self, out, in0, in1, op):
        return self._do([in0, in1], [out],
                        lambda: self._e.tensor_tensor(out=out, in0=in0, in1=in1, op=op))

    def scalar_tensor_tensor(self, out, in0, scalar, in1, op0, op1):
        return self._do([in0, scalar, in1], [out],
                        lambda: self._e.scalar_tensor_tensor(out=out, in0=in0, scalar=scalar,
                                                             in1=in1, op0=op0, op1=op1))

    def tensor_tensor_scan(self, out, data0, data1, initial, op0, op1):
        return self._do([data0, data1, initial], [out],
                        lambda: self._e.tensor_tensor_scan(out=out, data0=data0, data1=data1,
                                                           initial=initial, op0=op0, op1=op1))

    def reciprocal(self, out, in_):
        return self._do([in_], [out], lambda: self._e.reciprocal(out=out, in_=in_))

    def __getattr__(self, name):
        return getattr(self._e, name)


class Sem:
    def __init__(self, es, nc, name):
        self.h = es.enter_context(nc.semaphore(name))
        self.n = 0

    def inc(self, k=1):
        self.n += k
        return self.n


def _rsqrt(e, out, in_, tmp_i, tmp_a):
    e.tensor_copy(out=tmp_a, in_=in_.bitcast(I32))
    e.tensor_scalar(out=tmp_a, in0=tmp_a, scalar1=-0.5 * 2.0 ** -8,
                    scalar2=RSQRT_MAGIC * 2.0 ** -8, op0=ALU.mult, op1=ALU.add)
    e.tensor_copy(out=tmp_i, in_=tmp_a)
    e.tensor_scalar(out=tmp_i, in0=tmp_i, scalar1=8, scalar2=None,
                    op0=ALU.arith_shift_left)
    e.tensor_copy(out=out, in_=tmp_i.bitcast(F32))
    for _ in range(3):
        e.tensor_tensor(out=tmp_a, in0=out, in1=out, op=ALU.mult)
        e.tensor_tensor(out=tmp_a, in0=tmp_a, in1=in_, op=ALU.mult)
        e.tensor_scalar(out=tmp_a, in0=tmp_a, scalar1=-0.5, scalar2=1.5,
                        op0=ALU.mult, op1=ALU.add)
        e.tensor_tensor(out=out, in0=out, in1=tmp_a, op=ALU.mult)


def build_nc(debug=False, stages=3):
    nc = bass.Bass("TRN2", target_bir_lowering=False)

    dxT = nc.dram_tensor("xT", [VD, TOK], F32, kind="ExternalInput").ap()
    dwin = nc.dram_tensor("winT", [VD, DD], F32, kind="ExternalInput").ap()
    dag = nc.dram_tensor("agT", [VD, 68], F32, kind="ExternalInput").ap()
    dw1 = nc.dram_tensor("w1T", [NL, DD, FF], F32, kind="ExternalInput").ap()
    dw2 = nc.dram_tensor("w2T", [NL, FF, DD], F32, kind="ExternalInput").ap()
    dwout = nc.dram_tensor("woutT", [DD, VD], F32, kind="ExternalInput").ap()
    dwff = nc.dram_tensor("wT", [VD, VD], F32, kind="ExternalInput").ap()
    dbT = nc.dram_tensor("bT", [64, VD], F32, kind="ExternalInput").ap()
    dbin = nc.dram_tensor("bin_r", [128, NGD], F32, kind="ExternalInput").ap()
    dln1w = nc.dram_tensor("ln1w_r", [NL, 128, NGD], F32, kind="ExternalInput").ap()
    dln1b = nc.dram_tensor("ln1b_r", [NL, 128, NGD], F32, kind="ExternalInput").ap()
    dln2w = nc.dram_tensor("ln2w_r", [NL, 128, NGD], F32, kind="ExternalInput").ap()
    dln2b = nc.dram_tensor("ln2b_r", [NL, 128, NGD], F32, kind="ExternalInput").ap()
    dscl = nc.dram_tensor("scl_r", [NL, 128, NGD], F32, kind="ExternalInput").ap()
    db2 = nc.dram_tensor("b2_r", [NL, 128, NGD], F32, kind="ExternalInput").ap()
    dphr = nc.dram_tensor("phzre_r", [NL, 128, NGD], F32, kind="ExternalInput").ap()
    dphi = nc.dram_tensor("phzim_r", [NL, 128, NGD], F32, kind="ExternalInput").ap()
    dlcr = nc.dram_tensor("lcre_r", [NL, 128, NGD], F32, kind="ExternalInput").ap()
    dlci = nc.dram_tensor("lcim_r", [NL, 128, NGD], F32, kind="ExternalInput").ap()
    db1 = nc.dram_tensor("b1_r", [NL, 128, NGF], F32, kind="ExternalInput").ap()
    dlnlw = nc.dram_tensor("lnlw_r", [128, NGD], F32, kind="ExternalInput").ap()
    dlnlb = nc.dram_tensor("lnlb_r", [128, NGD], F32, kind="ExternalInput").ap()
    dbout = nc.dram_tensor("bout_r", [128, NGV], F32, kind="ExternalInput").ap()
    dbff = nc.dram_tensor("bff_r", [128, NGV], F32, kind="ExternalInput").ap()
    dgateb = nc.dram_tensor("gateb_r", [1, 4], F32, kind="ExternalInput").ap()
    dmask8 = nc.dram_tensor("mask8", [1, 8], F32, kind="ExternalInput").ap()
    dpat = nc.dram_tensor("patm", [4, 64], BF16, kind="ExternalInput").ap()
    dout = nc.dram_tensor("outT", [VD, TOK], F32, kind="ExternalOutput").ap()

    dxn = nc.dram_tensor("xn_spill", [NGV, 128, TOK], BF16).ap()
    dlni = nc.dram_tensor("lni", [2 * TOK], F32).ap()
    dlno = nc.dram_tensor("lno", [2 * TOK], F32).ap()
    dgsc = nc.dram_tensor("gsc", [4, TOK], F32).ap()
    dgm = nc.dram_tensor("gm", [4, TOK], BF16).ap()
    hcc_i = [nc.dram_tensor(f"cc_i{l}", [128, 8], F32) for l in range(NL)]
    hcc_o = [nc.dram_tensor(f"cc_o{l}", [128 * 8, 8], F32) for l in range(NL)]

    dbg = {}
    if debug:
        dbg["h0"] = nc.dram_tensor("dbg_h0", [128, NGD, TOK], F32, kind="ExternalOutput").ap()
        dbg["xn"] = nc.dram_tensor("dbg_xn", [128, NGV, TOK], BF16, kind="ExternalOutput").ap()
        for l in range(NL):
            dbg[f"h{l+1}"] = nc.dram_tensor(f"dbg_h{l+1}", [128, NGD, TOK], F32, kind="ExternalOutput").ap()
        dbg["hn"] = nc.dram_tensor("dbg_hn", [128, NGD, TOK], BF16, kind="ExternalOutput").ap()
        dbg["um"] = nc.dram_tensor("dbg_um", [64, TOK], BF16, kind="ExternalOutput").ap()

    es = contextlib.ExitStack()
    with es:
        sb = lambda name, shape, dt: es.enter_context(nc.sbuf_tensor(name, shape, dt))
        es2 = contextlib.ExitStack()
        sbm = lambda name, shape, dt: es2.enter_context(nc.sbuf_tensor(name, shape, dt))
        scrbf = sb("scrbf", [128, NGD, TOK], BF16)
        umask = sb("umask", [64, TOK], BF16)
        onesb = sb("onesb", [128, 1], BF16)
        gbb = sb("gbb", [128, 4], F32)
        m8b = sb("m8b", [128, 8], F32)
        h = sbm("h", [128, NGD, TOK], F32)
        onesf = sbm("onesf", [128, CK], F32)
        binv = sbm("binv", [128, NGD], F32)
        lnlw = sbm("lnlw", [128, NGD], F32)
        lnlb = sbm("lnlb", [128, NGD], F32)
        ln1w = sbm("ln1w", [128, NGD], F32)
        ln1b = sbm("ln1b", [128, NGD], F32)
        ln2w = sbm("ln2w", [128, NGD], F32)
        ln2b = sbm("ln2b", [128, NGD], F32)
        sclv = sbm("sclv", [128, NGD], F32)
        b2v = sbm("b2v", [128, NGD], F32)
        b1v = sbm("b1v", [128, NGF], F32)
        phr = sbm("phr", [128, NGD], F32)
        phi = sbm("phi", [128, NGD], F32)
        rho4 = sbm("rho4", [128, NGD], F32)
        ure4 = sbm("ure4", [128, NGD], F32)
        uim4 = sbm("uim4", [128, NGD], F32)
        q4 = sbm("q4", [128, NGD, 2], F32)
        u5 = sbm("u5", [128, NGD, 2], F32)
        p2k = sbm("p2k", [128, NGD, 2], F32)
        pm1 = sbm("pm1", [128, NGD], F32)
        qpow = sbm("qpow", [128, NCH, NGD, 2], F32)
        iot = sbm("iot", [128, NGD, NCH, 2], F32)
        iosc = sbm("iosc", [128, NGD, NCH, 2], F32)
        Ev = sbm("Ev", [128, NGD, 2], F32)
        gat = sbm("gat", [128, 8, NGD, 2], F32)
        Hre = sbm("Hre", [128, NGD], F32)
        Him = sbm("Him", [128, NGD], F32)
        tA = sbm("tA", [128, NGD], F32)
        tB = sbm("tB", [128, NGD], F32)
        tC = sbm("tC", [128, NGD], F32)
        tD = sbm("tD", [128, NGD, 2], F32)
        tE = sbm("tE", [128, NGD, 2], F32)
        scs = sbm("scs", [128, 2], F32)
        sct = sbm("sct", [128, 4], F32)
        statin = sbm("statin", [1, 2 * TOK], F32)
        statq = sbm("statq", [128, 2, TOK // 128], F32)
        stato = sbm("stato", [128, 2, TOK // 128], F32)
        stati = sbm("stati", [128, TOK // 128], I32)
        stata = sbm("stata", [128, TOK // 128], F32)
        rstdb = sbm("rstdb", [128, TOK], F32)
        nub = sbm("nub", [128, TOK], F32)
        hbt = sbm("hbt", [128, TOK], BF16)
        h2t = sbm("h2t", [128, TOK], BF16)
        ytmp = sbm("ytmp", [128, TOK], F32)
        ps = [es.enter_context(nc.psum_tensor(f"ps{i}", [128, TT], F32)) for i in range(8)]

        LD = Sem(es, nc, "LD")
        GD = Sem(es, nc, "GD")
        SV = Sem(es, nc, "SV")
        SSm = Sem(es, nc, "SS")
        ST = Sem(es, nc, "ST")
        SG = Sem(es, nc, "SG")
        CC = Sem(es, nc, "CC")
        GW = [Sem(es, nc, "GW0"), Sem(es, nc, "GW1")]

        blk = es.enter_context(nc.Block())
        _vstate = {"hist": []}

        class _BlkWrap:
            def __init__(self, b):
                self._b = b

            def vector(self, f):
                return self._b.vector(lambda e: f(VSafe(e, _vstate)))

            def __getattr__(self, name):
                return getattr(self._b, name)

        blk = _BlkWrap(blk)

        def dma_w(e, S, out, in_):
            e.dma_start(out=out, in_=in_).then_inc(S.h, 16)
            e.wait_ge(S.h, S.inc(16))

        def wait_all(e):
            for s in (LD, GD, SV, SSm, ST, SG, CC, GW[0], GW[1]):
                if s.n:
                    e.wait_ge(s.h, s.n)

        OWN = {"sync": LD, "scalar": SSm, "vector": SV, "gpsimd": SG, "tensor": ST}

        def barrier(engines=("sync", "scalar", "vector", "gpsimd", "tensor")):
            for name in engines:
                def _g(e):
                    wait_all(e)
                getattr(blk, name)(_g)

        def _p0_sync(e):
            for dst, srcap in ((binv, dbin), (lnlw, dlnlw), (lnlb, dlnlb)):
                dma_w(e, LD, dst[:], srcap)
            dma_w(e, LD, gbb[:], dgateb.broadcast_to([128, 4]))
            dma_w(e, LD, m8b[:], dmask8.broadcast_to([128, 8]))
        blk.sync(_p0_sync)

        def _p0_vec(e):
            e.memset(onesb[:], 1.0)
            e.memset(onesf[:], 1.0).then_inc(SV.h, 1)
        blk.vector(_p0_vec)
        SV.inc(1)

        def ln_finish(width, inv_n, n_tiles, sig_banks, sq_banks):
            kq = width // 128

            def _evac(e):
                wait_all(e)
                for i in range(n_tiles):
                    e.activation(out=statin[0:1, i * TT:i * TT + TT], in_=sig_banks[i], func=AF.Identity)
                for i in range(n_tiles - 1):
                    e.activation(out=statin[0:1, width + i * TT:width + i * TT + TT],
                                 in_=sq_banks[i], func=AF.Identity)
                i = n_tiles - 1
                e.activation(out=statin[0:1, width + i * TT:width + i * TT + TT],
                             in_=sq_banks[i], func=AF.Identity).then_inc(SSm.h, 1)
            blk.scalar(_evac)
            SSm.inc(1)

            def _bounce1(e):
                e.wait_ge(SSm.h, SSm.n)
                dma_w(e, LD, dlni[0:2 * width].unsqueeze(0), statin[0:1, 0:2 * width])
                dma_w(e, LD, statq[:, :, 0:kq],
                      dlni[0:2 * width].rearrange("(a p k) -> p a k", p=128, a=2))
            blk.sync(_bounce1)

            def _finish(e):
                e.wait_ge(LD.h, LD.n)
                mu = stato[:, 0, 0:kq]
                var = stato[:, 1, 0:kq]
                e.tensor_scalar(out=mu, in0=statq[:, 0, 0:kq], scalar1=inv_n, scalar2=None, op0=ALU.mult)
                e.tensor_scalar(out=var, in0=statq[:, 1, 0:kq], scalar1=inv_n, scalar2=EPS,
                                op0=ALU.mult, op1=ALU.add)
                e.tensor_tensor(out=stata[:, 0:kq], in0=mu, in1=mu, op=ALU.mult)
                e.tensor_tensor(out=var, in0=var, in1=stata[:, 0:kq], op=ALU.subtract)
                _rsqrt(e, statq[:, 0, 0:kq], var, stati[:, 0:kq], stata[:, 0:kq])
                e.tensor_tensor(out=statq[:, 1, 0:kq], in0=mu, in1=statq[:, 0, 0:kq],
                                op=ALU.mult).then_inc(SV.h, 1)
            blk.vector(_finish)
            SV.inc(1)

            def _bounce2(e):
                e.wait_ge(SV.h, SV.n)
                dma_w(e, LD, dlno[0:2 * width].rearrange("(a p k) -> p a k", p=128, a=2),
                      statq[:, :, 0:kq])
                dma_w(e, LD, rstdb[:, 0:width],
                      dlno[0:width].unsqueeze(0).broadcast_to([128, width]))
                dma_w(e, LD, nub[:, 0:width],
                      dlno[width:2 * width].unsqueeze(0).broadcast_to([128, width]))
            blk.sync(_bounce2)

        def attn_ln_stats():
            for g in range(NGD):
                def _cst(e, g=g):
                    if g == 0:
                        wait_all(e)
                    else:
                        e.wait_ge(ST.h, ST.n)
                    e.tensor_copy(out=hbt[:, :], in_=h[:, g, :]).then_inc(SV.h, 1)
                blk.vector(_cst)
                SV.inc(1)

                def _sq(e):
                    e.wait_ge(SV.h, SV.n)
                    e.activation(out=h2t[:, :], in_=hbt[:, :], func=AF.Square).then_inc(SSm.h, 1)
                blk.scalar(_sq)
                SSm.inc(1)

                def _mm(e, g=g):
                    e.wait_ge(SSm.h, SSm.n)
                    e.wait_ge(SV.h, 1)
                    ins = None
                    for tt in range(NTT):
                        e.matmul(out=ps[tt][0:1, :], lhsT=onesb[:], rhs=hbt[:, tt * TT:tt * TT + TT],
                                 start=(g == 0), stop=(g == NGD - 1))
                        ins = e.matmul(out=ps[4 + tt][0:1, :], lhsT=onesb[:], rhs=h2t[:, tt * TT:tt * TT + TT],
                                       start=(g == 0), stop=(g == NGD - 1))
                    ins.then_inc(ST.h, 1)
                blk.tensor(_mm)
                ST.inc(1)
            ln_finish(TOK, 1.0 / DD, NTT, [ps[tt][0:1, :] for tt in range(NTT)],
                      [ps[4 + tt][0:1, :] for tt in range(NTT)])
            barrier(("vector", "scalar", "sync"))

        with (
            nc.sbuf_tensor("xslab", [128, NGV, SLAB], BF16) as xslab,
            nc.sbuf_tensor("x2t", [128, 2, SLAB], BF16) as x2t,
            nc.sbuf_tensor("winb", [128, NGV, DD], BF16) as winb,
            nc.sbuf_tensor("agb", [128, NGV, 68], BF16) as agb,
            nc.sbuf_tensor("usc", [128, SLAB], F32) as usc,
            nc.sbuf_tensor("mrows", [4, TOK], BF16) as mrows,
            nc.sbuf_tensor("scm", [128, 4, TOK // 128], F32) as scm,
            nc.sbuf_tensor("sc2", [128, 2, TOK // 128], F32) as sc2,
            nc.sbuf_tensor("mpk", [128, 4, TOK // 128], BF16) as mpk,
            nc.sbuf_tensor("pat", [4, 64], BF16) as pat,
        ):
            def _w1(e):
                dma_w(e, GD, winb[:], dwin.rearrange("(g p) d -> p g d", p=128))
                dma_w(e, GD, agb[:], dag.rearrange("(g p) d -> p g d", p=128))
            blk.gpsimd(_w1)

            for s in range(NSLAB):
                t0 = s * SLAB

                def _lx(e, t0=t0):
                    dma_w(e, GD, xslab[:],
                          dxT[:, t0:t0 + SLAB].rearrange("(g p) t -> p g t", p=128))
                blk.gpsimd(_lx)
                gd_x = GD.n

                st0 = ST.n
                sq0 = SSm.n

                def _sq1(e, gd_x=gd_x, st0=st0):
                    e.wait_ge(GD.h, gd_x)
                    e.wait_ge(SV.h, 1)
                    for g in range(NGV):
                        if g >= 2:
                            e.wait_ge(ST.h, st0 + g - 1)
                        e.activation(out=x2t[:, g % 2, :], in_=xslab[:, g, :],
                                     func=AF.Square).then_inc(SSm.h, 1)
                blk.scalar(_sq1)
                SSm.inc(NGV)

                def _mms(e, gd_x=gd_x, sq0=sq0):
                    e.wait_ge(GD.h, gd_x)
                    e.wait_ge(SV.h, 1)
                    for g in range(NGV):
                        e.wait_ge(SSm.h, sq0 + g + 1)
                        e.matmul(out=ps[0][0:1, :], lhsT=onesb[:], rhs=xslab[:, g, :],
                                 start=(g == 0), stop=(g == NGV - 1))
                        e.matmul(out=ps[1][0:1, :], lhsT=onesb[:], rhs=x2t[:, g % 2, :],
                                 start=(g == 0), stop=(g == NGV - 1)).then_inc(ST.h, 1)
                blk.tensor(_mms)
                ST.inc(NGV)

                ln_finish(SLAB, 1.0 / VD, 1, [ps[0][0:1, :]], [ps[1][0:1, :]])

                def _xn(e):
                    wait_all(e)
                    for g in range(NGV):
                        e.tensor_tensor(out=ytmp[:, 0:SLAB], in0=xslab[:, g, :],
                                        in1=rstdb[:, 0:SLAB], op=ALU.mult)
                        ins = e.tensor_tensor(out=xslab[:, g, :], in0=ytmp[:, 0:SLAB],
                                              in1=nub[:, 0:SLAB], op=ALU.subtract)
                    ins.then_inc(SV.h, 1)
                blk.vector(_xn)
                SV.inc(1)

                def _spill(e, t0=t0):
                    e.wait_ge(SV.h, SV.n)
                    dma_w(e, LD, dxn[:, :, t0:t0 + SLAB].rearrange("g p t -> p g t"), xslab[:])
                blk.sync(_spill)

                def _mmw(e):
                    e.wait_ge(SV.h, SV.n)
                    e.wait_ge(GD.h, GD.n)
                    e.wait_ge(SSm.h, SSm.n)
                    for g in range(NGV):
                        for m in range(NGD):
                            e.matmul(out=ps[2 + m][:, :], lhsT=winb[:, g, m * 128:m * 128 + 128],
                                     rhs=xslab[:, g, :], start=(g == 0), stop=(g == NGV - 1))
                        ins = e.matmul(out=ps[6][0:68, :], lhsT=agb[:, g, :], rhs=xslab[:, g, :],
                                       start=(g == 0), stop=(g == NGV - 1))
                    ins.then_inc(ST.h, 1)
                blk.tensor(_mmw)
                ST.inc(1)

                def _evh(e, t0=t0):
                    e.wait_ge(ST.h, ST.n)
                    for m in range(NGD):
                        e.activation(out=h[:, m, t0:t0 + SLAB], in_=ps[2 + m][:, :],
                                     func=AF.Identity, bias=binv[:, m:m + 1])
                    e.activation(out=umask[0:64, t0:t0 + SLAB], in_=ps[6][0:64, :], func=AF.Identity)
                    e.activation(out=usc[64:68, :], in_=ps[6][64:68, :],
                                 func=AF.Identity).then_inc(SSm.h, 1)
                blk.scalar(_evh)
                SSm.inc(1)

                def _ssc(e, t0=t0):
                    e.wait_ge(SSm.h, SSm.n)
                    dma_w(e, LD, dgsc[:, t0:t0 + SLAB], usc[64:68, :])
                blk.sync(_ssc)
                barrier()

            def _mld(e):
                wait_all(e)
                dma_w(e, LD, pat[:], dpat)
                for r in range(4):
                    dma_w(e, LD, scm[:, r, :], dgsc[r].rearrange("(p k) -> p k", p=128))
            blk.sync(_mld)

            def _mcalc(e):
                e.wait_ge(LD.h, LD.n)
                KQ = TOK // 128
                e.tensor_tensor(out=scm[:], in0=scm[:],
                                in1=gbb[:].unsqueeze(2).broadcast_to([128, 4, KQ]), op=ALU.add)
                e.tensor_tensor(out=sc2[:, 0, :], in0=scm[:, 0, :], in1=scm[:, 1, :], op=ALU.max)
                e.tensor_tensor(out=sc2[:, 1, :], in0=scm[:, 2, :], in1=scm[:, 3, :], op=ALU.max)
                e.tensor_tensor(out=sc2[:, 0, :], in0=sc2[:, 0, :], in1=sc2[:, 1, :], op=ALU.max)
                mx = sc2[:, 0, :]
                e.memset(mpk[:, 0, :], 1.0)
                e.tensor_tensor(out=sc2[:, 1, :], in0=scm[:, 0, :], in1=mx, op=ALU.is_lt)
                e.tensor_copy(out=mpk[:, 1, :], in_=sc2[:, 1, :])
                e.tensor_tensor(out=scm[:, 1, :], in0=scm[:, 1, :], in1=mx, op=ALU.is_lt)
                e.tensor_tensor(out=sc2[:, 1, :], in0=sc2[:, 1, :], in1=scm[:, 1, :], op=ALU.mult)
                e.tensor_copy(out=mpk[:, 2, :], in_=sc2[:, 1, :])
                e.tensor_tensor(out=scm[:, 2, :], in0=scm[:, 2, :], in1=mx, op=ALU.is_lt)
                e.tensor_tensor(out=sc2[:, 1, :], in0=sc2[:, 1, :], in1=scm[:, 2, :], op=ALU.mult)
                e.tensor_copy(out=mpk[:, 3, :], in_=sc2[:, 1, :]).then_inc(SV.h, 1)
            blk.vector(_mcalc)
            SV.inc(1)

            def _mst(e):
                e.wait_ge(SV.h, SV.n)
                for r in range(4):
                    dma_w(e, LD, dgm[r].rearrange("(p k) -> p k", p=128), mpk[:, r, :])
                dma_w(e, LD, mrows[:], dgm)
            blk.sync(_mst)

            for tt in range(NTT):
                sv_mm = SV.n

                def _mmm(e, tt=tt, sv_mm=sv_mm):
                    e.wait_ge(LD.h, LD.n)
                    e.wait_ge(SV.h, sv_mm)
                    e.matmul(out=ps[7][0:64, :], lhsT=pat[:], rhs=mrows[:, tt * TT:tt * TT + TT],
                             start=True, stop=True).then_inc(ST.h, 1)
                blk.tensor(_mmm)
                ST.inc(1)

                def _umul(e, tt=tt):
                    e.wait_ge(ST.h, ST.n)
                    e.tensor_tensor(out=umask[0:64, tt * TT:tt * TT + TT],
                                    in0=umask[0:64, tt * TT:tt * TT + TT],
                                    in1=ps[7][0:64, :], op=ALU.mult).then_inc(SV.h, 1)
                blk.vector(_umul)
                SV.inc(1)
            barrier()

            if debug:
                def _dbg1(e):
                    wait_all(e)
                    dma_w(e, LD, dbg["h0"], h[:])
                    dma_w(e, LD, dbg["um"], umask[:])
                blk.sync(_dbg1)
                barrier()

        if stages < 2:
            def _fin1(e):
                wait_all(e)
            for name in ("sync", "scalar", "vector", "gpsimd", "tensor"):
                getattr(blk, name)(_fin1)
            es2.close()
            return nc
        with (
            nc.sbuf_tensor("dw_", [128, NCH, CK, 2], F32) as dwv,
            nc.sbuf_tensor("midt", [128, NGF, TT], BF16) as midt,
            nc.sbuf_tensor("rint", [128, NGD, CK, 2], F32) as rint,
            nc.sbuf_tensor("dk4", [128, NGD, CK], F32) as dk4,
            nc.sbuf_tensor("w1b", [128, NGD, FF], BF16) as w1b,
            nc.sbuf_tensor("w2b", [128, NGF, DD], BF16) as w2b,
            nc.sbuf_tensor("rhot", [128, CK], F32) as rhot,
        ):
            for l in range(NL):
                def _lc(e, l=l):
                    for dst, srcap in ((ln1w, dln1w), (ln1b, dln1b), (ln2w, dln2w), (ln2b, dln2b),
                                       (sclv, dscl), (b2v, db2), (phr, dphr), (phi, dphi)):
                        dma_w(e, LD, dst[:], srcap[l])
                    dma_w(e, LD, b1v[:], db1[l])
                    dma_w(e, LD, Hre[:], dlcr[l])
                    dma_w(e, LD, Him[:], dlci[l])
                blk.sync(_lc)

                def _lw(e, l=l):
                    dma_w(e, GD, w1b[:], dw1[l].rearrange("(k p) f -> p k f", p=128))
                    dma_w(e, GD, w2b[:], dw2[l].rearrange("(k p) f -> p k f", p=128))
                blk.gpsimd(_lw)

                def _pc(e, l=l):
                    wait_all(e)
                    e.tensor_tensor(out=tA[:], in0=phr[:], in1=phr[:], op=ALU.mult)
                    e.tensor_tensor(out=tB[:], in0=phi[:], in1=phi[:], op=ALU.mult)
                    e.tensor_tensor(out=tA[:], in0=tA[:], in1=tB[:], op=ALU.add)
                    _rsqrt(e, tB[:], tA[:], stati[:, 0:NGD], sct[:])
                    e.tensor_tensor(out=tC[:], in0=tA[:], in1=tB[:], op=ALU.mult)
                    e.tensor_tensor(out=ure4[:], in0=phr[:], in1=tB[:], op=ALU.mult)
                    e.tensor_tensor(out=uim4[:], in0=phi[:], in1=tB[:], op=ALU.mult).then_inc(SV.h, 1)
                blk.vector(_pc)
                SV.inc(1)

                def _tanh(e):
                    e.wait_ge(SV.h, SV.n)
                    e.activation(out=rho4[:], in_=tC[:], func=AF.Tanh).then_inc(SSm.h, 1)
                blk.scalar(_tanh)
                SSm.inc(1)

                def _tbl(e):
                    e.wait_ge(SSm.h, SSm.n)
                    for g in range(NGD):
                        ri = rint[:, g]
                        dk = dk4[:, g]
                        e.memset(ri[:, 0:1, 0:1], 1.0)
                        e.memset(ri[:, 0:1, 1:2], 0.0)
                        e.tensor_copy(out=dk[:, 0:1], in_=rho4[:, g:g + 1])
                        e.tensor_copy(out=scs[:, 0:1], in_=ure4[:, g:g + 1])
                        e.tensor_copy(out=scs[:, 1:2], in_=uim4[:, g:g + 1])
                        n = 1
                        while n < CK:
                            m = min(n, CK - n)
                            e.tensor_scalar(out=ytmp[:, 0:m], in0=ri[:, 0:m, 1], scalar1=scs[:, 1:2],
                                            scalar2=None, op0=ALU.mult)
                            e.scalar_tensor_tensor(out=ri[:, n:n + m, 0], in0=ri[:, 0:m, 0],
                                                   scalar=scs[:, 0:1], in1=ytmp[:, 0:m],
                                                   op0=ALU.mult, op1=ALU.add)
                            e.tensor_scalar(out=ytmp[:, 0:m], in0=ri[:, 0:m, 0], scalar1=scs[:, 1:2],
                                            scalar2=None, op0=ALU.mult)
                            e.scalar_tensor_tensor(out=ri[:, n:n + m, 1], in0=ri[:, 0:m, 1],
                                                   scalar=scs[:, 0:1], in1=ytmp[:, 0:m],
                                                   op0=ALU.mult, op1=ALU.subtract)
                            e.tensor_scalar(out=dk[:, n:n + m], in0=dk[:, 0:m],
                                            scalar1=dk[:, n - 1:n], scalar2=None, op0=ALU.mult)
                            e.tensor_tensor(out=sct[:, 0:2], in0=scs[:, 0:2], in1=scs[:, 0:2], op=ALU.mult)
                            e.tensor_tensor(out=sct[:, 2:3], in0=scs[:, 0:1], in1=scs[:, 1:2], op=ALU.mult)
                            e.tensor_tensor(out=scs[:, 0:1], in0=sct[:, 0:1], in1=sct[:, 1:2], op=ALU.subtract)
                            e.tensor_scalar(out=scs[:, 1:2], in0=sct[:, 2:3], scalar1=2.0, scalar2=None,
                                            op0=ALU.mult)
                            n *= 2
                        e.tensor_copy(out=u5[:, g, 0:1], in_=scs[:, 0:1])
                        e.tensor_copy(out=u5[:, g, 1:2], in_=scs[:, 1:2])
                        e.tensor_scalar(out=q4[:, g, 0:1], in0=scs[:, 0:1], scalar1=dk[:, CK - 1:CK],
                                        scalar2=None, op0=ALU.mult)
                        e.tensor_scalar(out=q4[:, g, 1:2], in0=scs[:, 1:2], scalar1=dk[:, CK - 1:CK],
                                        scalar2=None, op0=ALU.mult)

                    def csq(dst, src):
                        e.tensor_tensor(out=tD[:, :, 0], in0=src[:, :, 0], in1=src[:, :, 0], op=ALU.mult)
                        e.tensor_tensor(out=tA[:], in0=src[:, :, 1], in1=src[:, :, 1], op=ALU.mult)
                        e.tensor_tensor(out=tD[:, :, 1], in0=src[:, :, 0], in1=src[:, :, 1], op=ALU.mult)
                        e.tensor_tensor(out=dst[:, :, 0], in0=tD[:, :, 0], in1=tA[:], op=ALU.subtract)
                        e.tensor_scalar(out=dst[:, :, 1], in0=tD[:, :, 1], scalar1=2.0, scalar2=None,
                                        op0=ALU.mult)
                    csq(tE, q4)
                    csq(p2k, tE)
                    e.tensor_scalar(out=pm1[:], in0=p2k[:, :, 0], scalar1=1.0, scalar2=-1.0,
                                    op0=ALU.mult, op1=ALU.add)
                    e.memset(qpow[:, 0, :, 0], 1.0)
                    e.memset(qpow[:, 0, :, 1], 0.0)
                    for k in range(1, NCH):
                        a = qpow[:, k - 1]
                        e.tensor_tensor(out=tD[:, :, 0], in0=a[:, :, 0], in1=q4[:, :, 0], op=ALU.mult)
                        e.tensor_tensor(out=tA[:], in0=a[:, :, 1], in1=q4[:, :, 1], op=ALU.mult)
                        e.tensor_tensor(out=qpow[:, k, :, 0], in0=tD[:, :, 0], in1=tA[:], op=ALU.subtract)
                        e.tensor_tensor(out=tD[:, :, 1], in0=a[:, :, 0], in1=q4[:, :, 1], op=ALU.mult)
                        e.tensor_tensor(out=tA[:], in0=a[:, :, 1], in1=q4[:, :, 0], op=ALU.mult)
                        e.tensor_tensor(out=qpow[:, k, :, 1], in0=tD[:, :, 1], in1=tA[:], op=ALU.add)
                    e.tensor_copy(out=sct[:, 0:1], in_=sct[:, 0:1]).then_inc(SV.h, 1)
                blk.vector(_tbl)
                SV.inc(1)
                barrier(("vector", "scalar", "tensor"))

                attn_ln_stats()

                for g in range(NGD):
                    def _nrm(e, g=g):
                        if g == 0:
                            wait_all(e)
                        else:
                            e.wait_ge(SSm.h, SSm.n)
                        e.tensor_tensor(out=ytmp[:], in0=h[:, g, :], in1=rstdb[:], op=ALU.mult)
                        e.tensor_tensor(out=ytmp[:], in0=ytmp[:], in1=nub[:],
                                        op=ALU.subtract).then_inc(SV.h, 1)
                    blk.vector(_nrm)
                    SV.inc(1)

                    def _silu(e, g=g):
                        e.wait_ge(SV.h, SV.n)
                        e.activation(out=ytmp[:], in_=ytmp[:], func=AF.Silu,
                                     scale=ln1w[:, g:g + 1], bias=ln1b[:, g:g + 1]).then_inc(SSm.h, 1)
                    blk.scalar(_silu)
                    SSm.inc(1)

                    def _scan(e, g=g):
                        e.wait_ge(SSm.h, SSm.n)
                        e.tensor_tensor(
                            out=dwv[:],
                            in0=ytmp[:].rearrange("p (c t) -> p c t", c=NCH).unsqueeze(3).broadcast_to([128, NCH, CK, 2]),
                            in1=rint[:, g].unsqueeze(1).broadcast_to([128, NCH, CK, 2]),
                            op=ALU.mult)
                        e.tensor_scalar(out=rhot[:], in0=onesf[:], scalar1=rho4[:, g:g + 1],
                                        scalar2=None, op0=ALU.mult)
                        for c in range(NCH):
                            for comp in range(2):
                                e.tensor_tensor_scan(out=dwv[:, c, :, comp], data0=rhot[:],
                                                     data1=dwv[:, c, :, comp], initial=0.0,
                                                     op0=ALU.mult, op1=ALU.add)
                        e.memset(iot[:, g, 0, :], 0.0)
                        for c in range(1, NCH):
                            pe = dwv[:, c - 1, CK - 1, :]
                            pi = iot[:, g, c - 1, :]
                            e.tensor_scalar(out=sct[:, 0:1], in0=pi[:, 1:2], scalar1=q4[:, g, 1:2],
                                            scalar2=None, op0=ALU.mult)
                            e.scalar_tensor_tensor(out=tD[:, 0, 0:1], in0=pi[:, 0:1], scalar=q4[:, g, 0:1],
                                                   in1=sct[:, 0:1], op0=ALU.mult, op1=ALU.subtract)
                            e.tensor_scalar(out=sct[:, 0:1], in0=pi[:, 0:1], scalar1=q4[:, g, 1:2],
                                            scalar2=None, op0=ALU.mult)
                            e.scalar_tensor_tensor(out=tD[:, 0, 1:2], in0=pi[:, 1:2], scalar=q4[:, g, 0:1],
                                                   in1=sct[:, 0:1], op0=ALU.mult, op1=ALU.add)
                            e.tensor_scalar(out=sct[:, 0:1], in0=pe[:, 1:2], scalar1=u5[:, g, 1:2],
                                            scalar2=None, op0=ALU.mult)
                            e.scalar_tensor_tensor(out=tD[:, 1, 0:1], in0=pe[:, 0:1], scalar=u5[:, g, 0:1],
                                                   in1=sct[:, 0:1], op0=ALU.mult, op1=ALU.subtract)
                            e.tensor_scalar(out=sct[:, 0:1], in0=pe[:, 0:1], scalar1=u5[:, g, 1:2],
                                            scalar2=None, op0=ALU.mult)
                            e.scalar_tensor_tensor(out=tD[:, 1, 1:2], in0=pe[:, 1:2], scalar=u5[:, g, 0:1],
                                                   in1=sct[:, 0:1], op0=ALU.mult, op1=ALU.add)
                            e.tensor_tensor(out=iot[:, g, c, :], in0=tD[:, 0, :], in1=tD[:, 1, :], op=ALU.add)
                        e.scalar_tensor_tensor(out=tD[:, 0, :], in0=iot[:, g, NCH - 1, :],
                                               scalar=dk4[:, g, CK - 1:CK], in1=dwv[:, NCH - 1, CK - 1, :],
                                               op0=ALU.mult, op1=ALU.add)
                        e.tensor_scalar(out=sct[:, 3:4], in0=rint[:, g, CK - 1, 1:2], scalar1=-1.0,
                                        scalar2=None, op0=ALU.mult)
                        e.tensor_scalar(out=sct[:, 0:1], in0=tD[:, 0, 1:2], scalar1=sct[:, 3:4],
                                        scalar2=None, op0=ALU.mult)
                        e.scalar_tensor_tensor(out=Ev[:, g, 0:1], in0=tD[:, 0, 0:1],
                                               scalar=rint[:, g, CK - 1, 0:1], in1=sct[:, 0:1],
                                               op0=ALU.mult, op1=ALU.subtract)
                        e.tensor_scalar(out=sct[:, 0:1], in0=tD[:, 0, 0:1], scalar1=sct[:, 3:4],
                                        scalar2=None, op0=ALU.mult)
                        e.scalar_tensor_tensor(out=Ev[:, g, 1:2], in0=tD[:, 0, 1:2],
                                               scalar=rint[:, g, CK - 1, 0:1], in1=sct[:, 0:1],
                                               op0=ALU.mult, op1=ALU.add)
                        e.tensor_tensor(out=dwv[:], in0=dwv[:],
                                        in1=rint[:, g].unsqueeze(1).broadcast_to([128, NCH, CK, 2]),
                                        op=ALU.mult)
                        hv = h[:, g, :].rearrange("p (c t) -> p c t", c=NCH)
                        e.scalar_tensor_tensor(out=hv, in0=dwv[:, :, :, 0], scalar=sclv[:, g:g + 1],
                                               in1=hv, op0=ALU.mult, op1=ALU.add)
                        e.scalar_tensor_tensor(out=hv, in0=dwv[:, :, :, 1], scalar=sclv[:, g:g + 1],
                                               in1=hv, op0=ALU.mult, op1=ALU.add).then_inc(SV.h, 1)
                    blk.vector(_scan)
                    SV.inc(1)

                def _pub(e, l=l):
                    e.wait_ge(SV.h, SV.n)
                    dma_w(e, GD, hcc_i[l].ap(), Ev[:].rearrange("p g c -> p (g c)"))
                    e.collective_compute("AllGather", ALU.bypass, replica_groups=[list(range(NCORE))],
                                         ins=[hcc_i[l].ap().opt()], outs=[hcc_o[l].ap().opt()]
                                         ).then_inc(CC.h, 1)
                    CC.inc(1)
                    e.wait_ge(CC.h, CC.n)
                    dma_w(e, GD, gat[:].rearrange("p s g c -> p s (g c)"),
                          hcc_o[l].ap().rearrange("(s p) c -> p s c", p=128))
                blk.gpsimd(_pub)

                def _comb(e):
                    e.wait_ge(GD.h, GD.n)
                    for j in range(NCORE):
                        mj = m8b[:, j:j + 1]
                        e.tensor_scalar(out=tA[:], in0=pm1[:], scalar1=mj, scalar2=1.0,
                                        op0=ALU.mult, op1=ALU.add)
                        e.tensor_scalar(out=tB[:], in0=p2k[:, :, 1], scalar1=mj, scalar2=None, op0=ALU.mult)
                        e.tensor_tensor(out=tC[:], in0=tB[:], in1=Him[:], op=ALU.mult)
                        e.tensor_tensor(out=tD[:, :, 0], in0=tA[:], in1=Hre[:], op=ALU.mult)
                        e.tensor_tensor(out=tD[:, :, 0], in0=tD[:, :, 0], in1=tC[:], op=ALU.subtract)
                        e.tensor_scalar(out=tC[:], in0=gat[:, j, :, 0], scalar1=mj, scalar2=None, op0=ALU.mult)
                        e.tensor_tensor(out=tD[:, :, 0], in0=tD[:, :, 0], in1=tC[:], op=ALU.add)
                        e.tensor_tensor(out=tC[:], in0=tB[:], in1=Hre[:], op=ALU.mult)
                        e.tensor_tensor(out=tD[:, :, 1], in0=tA[:], in1=Him[:], op=ALU.mult)
                        e.tensor_tensor(out=tD[:, :, 1], in0=tD[:, :, 1], in1=tC[:], op=ALU.add)
                        e.tensor_scalar(out=tC[:], in0=gat[:, j, :, 1], scalar1=mj, scalar2=None, op0=ALU.mult)
                        e.tensor_tensor(out=tD[:, :, 1], in0=tD[:, :, 1], in1=tC[:], op=ALU.add)
                        e.tensor_copy(out=Hre[:], in_=tD[:, :, 0])
                        e.tensor_copy(out=Him[:], in_=tD[:, :, 1])
                    e.tensor_tensor(out=tA[:], in0=ure4[:], in1=Hre[:], op=ALU.mult)
                    e.tensor_tensor(out=tB[:], in0=uim4[:], in1=Him[:], op=ALU.mult)
                    e.tensor_tensor(out=tA[:], in0=tA[:], in1=tB[:], op=ALU.subtract)
                    e.tensor_tensor(out=tB[:], in0=ure4[:], in1=Him[:], op=ALU.mult)
                    e.tensor_tensor(out=tC[:], in0=uim4[:], in1=Hre[:], op=ALU.mult)
                    e.tensor_tensor(out=tB[:], in0=tB[:], in1=tC[:], op=ALU.add)
                    for c in range(NCH):
                        qp = qpow[:, c]
                        e.tensor_tensor(out=tD[:, :, 0], in0=qp[:, :, 0], in1=tA[:], op=ALU.mult)
                        e.tensor_tensor(out=tC[:], in0=qp[:, :, 1], in1=tB[:], op=ALU.mult)
                        e.tensor_tensor(out=tD[:, :, 0], in0=tD[:, :, 0], in1=tC[:], op=ALU.subtract)
                        e.tensor_tensor(out=tD[:, :, 1], in0=qp[:, :, 0], in1=tB[:], op=ALU.mult)
                        e.tensor_tensor(out=tC[:], in0=qp[:, :, 1], in1=tA[:], op=ALU.mult)
                        e.tensor_tensor(out=tD[:, :, 1], in0=tD[:, :, 1], in1=tC[:], op=ALU.add)
                        for comp in range(2):
                            e.tensor_tensor(out=iot[:, :, c, comp], in0=iot[:, :, c, comp],
                                            in1=tD[:, :, comp], op=ALU.add)
                    e.tensor_tensor(out=iosc[:], in0=iot[:],
                                    in1=sclv[:].unsqueeze(2).unsqueeze(3).broadcast_to([128, NGD, NCH, 2]),
                                    op=ALU.mult).then_inc(SV.h, 1)
                blk.vector(_comb)
                SV.inc(1)

                def _ctr(e):
                    e.wait_ge(SV.h, SV.n)
                    t12 = ytmp[:, 0:CK * 2].rearrange("p (t x) -> p t x", x=2)
                    for g in range(NGD):
                        e.tensor_tensor(out=t12, in0=rint[:, g],
                                        in1=dk4[:, g].unsqueeze(2).broadcast_to([128, CK, 2]),
                                        op=ALU.mult)
                        for c in range(NCH):
                            hs = h[:, g, c * CK:c * CK + CK]
                            e.scalar_tensor_tensor(out=hs, in0=t12[:, :, 0], scalar=iosc[:, g, c, 0:1],
                                                   in1=hs, op0=ALU.mult, op1=ALU.add)
                            ins = e.scalar_tensor_tensor(out=hs, in0=t12[:, :, 1], scalar=iosc[:, g, c, 1:2],
                                                         in1=hs, op0=ALU.mult, op1=ALU.add)
                    ins.then_inc(SV.h, 1)
                blk.vector(_ctr)
                SV.inc(1)
                barrier(("vector", "scalar", "tensor"))

                attn_ln_stats()
                for g in range(NGD):
                    def _nr2(e, g=g):
                        if g == 0:
                            wait_all(e)
                        else:
                            e.wait_ge(SSm.h, SSm.n)
                        e.tensor_tensor(out=ytmp[:], in0=h[:, g, :], in1=rstdb[:], op=ALU.mult)
                        e.tensor_tensor(out=ytmp[:], in0=ytmp[:], in1=nub[:],
                                        op=ALU.subtract).then_inc(SV.h, 1)
                    blk.vector(_nr2)
                    SV.inc(1)

                    def _aff(e, g=g):
                        e.wait_ge(SV.h, SV.n)
                        e.activation(out=scrbf[:, g, :], in_=ytmp[:], func=AF.Identity,
                                     scale=ln2w[:, g:g + 1], bias=ln2b[:, g:g + 1]).then_inc(SSm.h, 1)
                    blk.scalar(_aff)
                    SSm.inc(1)
                barrier(("vector", "scalar", "tensor", "gpsimd"))

                for tt in range(NTT):
                    ss_f1 = SSm.n

                    def _mmf1(e, tt=tt, ss_f1=ss_f1):
                        if tt == 0:
                            wait_all(e)
                        else:
                            e.wait_ge(SSm.h, ss_f1)
                        for m in range(NGF):
                            bank = 2 * (m % 2)
                            if m >= 2:
                                e.wait_ge(SSm.h, ss_f1 + m - 1)
                            ins = None
                            for k in range(NGD):
                                ins = e.matmul(out=ps[bank][:, :], lhsT=w1b[:, k, m * 128:m * 128 + 128],
                                               rhs=scrbf[:, k, tt * TT:tt * TT + TT],
                                               start=(k == 0), stop=(k == NGD - 1))
                            ins.then_inc(ST.h, 1)
                    blk.tensor(_mmf1)
                    st_f1 = ST.n
                    ST.inc(NGF)

                    def _evf1(e, st_f1=st_f1):
                        for m in range(NGF):
                            bank = 2 * (m % 2)
                            e.wait_ge(ST.h, st_f1 + m + 1)
                            e.activation(out=midt[:, m, :], in_=ps[bank][:, :], func=AF.Silu,
                                         bias=b1v[:, m:m + 1]).then_inc(SSm.h, 1)
                    blk.scalar(_evf1)
                    SSm.inc(NGF)

                    sv_f2 = SV.n

                    def _mmf2(e, sv_f2=sv_f2):
                        e.wait_ge(SSm.h, SSm.n)
                        e.wait_ge(SV.h, sv_f2)
                        for m in range(NGD):
                            bank = 4 + 2 * (m % 2)
                            if m >= 2:
                                e.wait_ge(SV.h, sv_f2 + m - 1)
                            ins = None
                            for k in range(NGF):
                                ins = e.matmul(out=ps[bank][:, :], lhsT=w2b[:, k, m * 128:m * 128 + 128],
                                               rhs=midt[:, k, :], start=(k == 0), stop=(k == NGF - 1))
                            ins.then_inc(ST.h, 1)
                    blk.tensor(_mmf2)
                    st_f2 = ST.n
                    ST.inc(NGD)

                    def _evf2(e, tt=tt, st_f2=st_f2):
                        for m in range(NGD):
                            bank = 4 + 2 * (m % 2)
                            e.wait_ge(ST.h, st_f2 + m + 1)
                            hs = h[:, m, tt * TT:tt * TT + TT]
                            e.scalar_tensor_tensor(out=hs, in0=ps[bank][:, :], scalar=b2v[:, m:m + 1],
                                                   in1=hs, op0=ALU.add, op1=ALU.add).then_inc(SV.h, 1)
                    blk.vector(_evf2)
                    SV.inc(NGD)
                barrier()

                if debug:
                    def _dbg2(e, l=l):
                        wait_all(e)
                        dma_w(e, LD, dbg[f"h{l+1}"], h[:])
                    blk.sync(_dbg2)
                    barrier()

            attn_ln_stats()
            for g in range(NGD):
                def _nr4(e, g=g):
                    if g == 0:
                        wait_all(e)
                    else:
                        e.wait_ge(SSm.h, SSm.n)
                    e.tensor_tensor(out=ytmp[:], in0=h[:, g, :], in1=rstdb[:], op=ALU.mult)
                    e.tensor_tensor(out=ytmp[:], in0=ytmp[:], in1=nub[:],
                                    op=ALU.subtract).then_inc(SV.h, 1)
                blk.vector(_nr4)
                SV.inc(1)

                def _aff4(e, g=g):
                    e.wait_ge(SV.h, SV.n)
                    e.activation(out=scrbf[:, g, :], in_=ytmp[:], func=AF.Identity,
                                 scale=lnlw[:, g:g + 1], bias=lnlb[:, g:g + 1]).then_inc(SSm.h, 1)
                blk.scalar(_aff4)
                SSm.inc(1)
            barrier()

            if debug:
                def _dbg3(e):
                    wait_all(e)
                    dma_w(e, LD, dbg["hn"], scrbf[:])
                blk.sync(_dbg3)
                barrier()

        if stages < 3:
            def _fin2(e):
                wait_all(e)
            for name in ("sync", "scalar", "vector", "gpsimd", "tensor"):
                getattr(blk, name)(_fin2)
            es2.close()
            return nc
        es2.close()
        with (
            nc.sbuf_tensor("xnf", [128, NGV, TOK], BF16) as xnf,
            nc.sbuf_tensor("wtb", [128, 2, NGV, 128], BF16) as wtb,
            nc.sbuf_tensor("woutm", [128, 2, NGD, 128], BF16) as woutm,
            nc.sbuf_tensor("btb", [64, VD], BF16) as btb,
            nc.sbuf_tensor("xm", [128, TOK], F32) as xm,
            nc.sbuf_tensor("outst", [128, 2, TOK], F32) as outst,
            nc.sbuf_tensor("biast", [128, NGV], F32) as biast,
            nc.sbuf_tensor("bias2", [128, NGV], F32) as bias2,
        ):
            def _l3(e):
                for i in range(NGV // 8):
                    dma_w(e, LD, xnf[:, i * 8:(i + 1) * 8, :],
                          dxn[i * 8:(i + 1) * 8].rearrange("g p t -> p g t"))
                dma_w(e, LD, biast[:], dbout)
                dma_w(e, LD, bias2[:], dbff)
            blk.sync(_l3)

            def _l3g(e):
                dma_w(e, GD, btb[:], dbT)
            blk.gpsimd(_l3g)

            def _bsum(e):
                wait_all(e)
                e.tensor_tensor(out=biast[:], in0=biast[:], in1=bias2[:], op=ALU.add).then_inc(SV.h, 1)
            blk.vector(_bsum)
            SV.inc(1)
            barrier()

            st_mm = []
            for m in range(NGV):
                buf = m % 2
                ms = m * 128

                GWp = GW[buf]

                def _lw3(e, m=m, buf=buf, ms=ms, w=(st_mm[m - 2] if m >= 2 else 0), GWp=GWp):
                    if m >= 2:
                        e.wait_ge(ST.h, w)
                    e.dma_start(out=wtb[:, buf],
                                in_=dwff[:, ms:ms + 128].rearrange("(k p) f -> p k f", p=128)
                                ).then_inc(GWp.h, 16)
                    GWp.inc(16)
                    e.dma_start(out=woutm[:, buf],
                                in_=dwout[:, ms:ms + 128].rearrange("(k p) f -> p k f", p=128)
                                ).then_inc(GWp.h, 16)
                    GWp.inc(16)
                blk.gpsimd(_lw3)

                def _lx3(e, m=m, ms=ms):
                    if m >= 1:
                        e.wait_ge(SV.h, SV.n)
                    dma_w(e, LD, xm[:], dxT[ms:ms + 128, :])
                blk.sync(_lx3)

                gw_m = GW[buf].n
                sv_banks = SV.n

                def _mm5(e, m=m, buf=buf, gw_m=gw_m, sv_banks=sv_banks):
                    e.wait_ge(GW[buf].h, gw_m)
                    e.wait_ge(GD.h, GD.n)
                    if m >= 2:
                        e.wait_ge(SV.h, sv_banks - 1)
                    for tt in range(NTT):
                        bank = 4 * buf + tt
                        e.matmul(out=ps[bank][:, :], lhsT=btb[0:64, m * 128:m * 128 + 128],
                                 rhs=umask[0:64, tt * TT:tt * TT + TT], start=True, stop=False)
                        for k in range(NGD):
                            e.matmul(out=ps[bank][:, :], lhsT=woutm[:, buf, k, :],
                                     rhs=scrbf[:, k, tt * TT:tt * TT + TT], start=False, stop=False)
                        for k in range(NGV):
                            ins = e.matmul(out=ps[bank][:, :], lhsT=wtb[:, buf, k, :],
                                           rhs=xnf[:, k, tt * TT:tt * TT + TT], start=False,
                                           stop=(k == NGV - 1))
                    ins.then_inc(ST.h, 1)
                blk.tensor(_mm5)
                ST.inc(1)
                st_mm.append(ST.n)

                st_m = ST.n
                ld_m = LD.n

                def _ev5(e, m=m, buf=buf, st_m=st_m, ld_m=ld_m):
                    e.wait_ge(ST.h, st_m)
                    e.wait_ge(LD.h, ld_m)
                    ins = None
                    for tt in range(NTT):
                        bank = 4 * buf + tt
                        ins = e.scalar_tensor_tensor(out=outst[:, buf, tt * TT:tt * TT + TT],
                                                     in0=ps[bank][:, :], scalar=biast[:, m:m + 1],
                                                     in1=xm[:, tt * TT:tt * TT + TT],
                                                     op0=ALU.add, op1=ALU.add)
                    ins.then_inc(SV.h, 1)
                blk.vector(_ev5)
                SV.inc(1)

                sv_m = SV.n

                def _st5(e, buf=buf, ms=ms, sv_m=sv_m):
                    e.wait_ge(SV.h, sv_m)
                    dma_w(e, LD, dout[ms:ms + 128, :], outst[:, buf, :])
                blk.sync(_st5)

            def _fin(e):
                e.wait_ge(LD.h, LD.n)
            blk.sync(_fin)

            if debug:
                def _dbg4(e):
                    wait_all(e)
                    dma_w(e, LD, dbg["xn"], xnf[:])
                blk.sync(_dbg4)

    return nc


def _patm():
    import ml_dtypes
    p = np.zeros((4, 64), ml_dtypes.bfloat16)
    p[0, 0:8] = 1
    p[1, 8:16] = 1
    p[2, 16:32] = 1
    p[3, 32:64] = 1
    return p


def _prep(inputs):
    f = np.float32
    x = np.asarray(inputs["L0_attn_x"], f)[0]
    xT = np.ascontiguousarray(x.T)
    Win = np.asarray(inputs["L0_attn_Win"], f)
    A = np.asarray(inputs["L0_ffn_A"], f)
    gw = np.asarray(inputs["L0_ffn_gate_w"], f)
    com = {
        "winT": np.ascontiguousarray(Win.T),
        "agT": np.ascontiguousarray(np.concatenate([A.T, gw.T], axis=1)),
        "w1T": np.ascontiguousarray(np.asarray(inputs["L0_attn_W1"], f).transpose(0, 2, 1)),
        "w2T": np.ascontiguousarray(np.asarray(inputs["L0_attn_W2"], f).transpose(0, 2, 1)),
        "woutT": np.ascontiguousarray(np.asarray(inputs["L0_attn_Wout"], f).T),
        "wT": np.ascontiguousarray(np.asarray(inputs["L0_ffn_W"], f).T),
        "bT": np.ascontiguousarray(np.asarray(inputs["L0_ffn_B"], f).T),
        "bin_r": np.ascontiguousarray(np.asarray(inputs["L0_attn_bin"], f).reshape(NGD, 128).T),
        "b1_r": np.ascontiguousarray(np.asarray(inputs["L0_attn_b1"], f).reshape(NL, NGF, 128).transpose(0, 2, 1)),
        "lnlw_r": np.ascontiguousarray(np.asarray(inputs["L0_attn_lnl_w"], f).reshape(NGD, 128).T),
        "lnlb_r": np.ascontiguousarray(np.asarray(inputs["L0_attn_lnl_b"], f).reshape(NGD, 128).T),
        "bout_r": np.ascontiguousarray(np.asarray(inputs["L0_attn_bout"], f).reshape(NGV, 128).T),
        "bff_r": np.ascontiguousarray(np.asarray(inputs["L0_ffn_b"], f).reshape(NGV, 128).T),
        "gateb_r": np.ascontiguousarray(np.asarray(inputs["L0_ffn_gate_b"], f).reshape(1, 4)),
        "patm": _patm(),
    }
    for nm, key in (("ln1w_r", "L0_attn_ln1_w"), ("ln1b_r", "L0_attn_ln1_b"),
                    ("ln2w_r", "L0_attn_ln2_w"), ("ln2b_r", "L0_attn_ln2_b"),
                    ("scl_r", "L0_attn_sc_lin"), ("b2_r", "L0_attn_b2"),
                    ("phzre_r", "L0_attn_phz_re"), ("phzim_r", "L0_attn_phz_im"),
                    ("lcre_r", "L0_attn_lc_re"), ("lcim_r", "L0_attn_lc_im")):
        com[nm] = np.ascontiguousarray(
            np.asarray(inputs[key], f).reshape(NL, NGD, 128).transpose(0, 2, 1))
    in_maps = []
    for c in range(NCORE):
        mp = dict(com)
        mp["xT"] = np.ascontiguousarray(xT[:, c * TOK:(c + 1) * TOK])
        mp["mask8"] = np.ascontiguousarray(
            np.array([1.0] * c + [0.0] * (NCORE - c), f).reshape(1, 8))
        in_maps.append(mp)
    return in_maps


def kernel(**inputs):
    in_maps = _prep(inputs)
    nc = build_nc(debug=DEBUG, stages=STAGES)
    res = run_bass_kernel_spmd(nc, in_maps, core_ids=list(range(NCORE)))
    out = np.concatenate([np.asarray(res.results[c]["outT"]) for c in range(NCORE)], axis=1)
    if DEBUG:
        kernel._last = res
    return np.ascontiguousarray(out.T).reshape(1, 16384, 4096).astype(np.float32)


# version 02
# speedup vs baseline: 1.1031x; 1.1031x over previous
import contextlib
import numpy as np
import concourse.bass as bass
import concourse.mybir as mybir
from concourse.bass_utils import run_bass_kernel_spmd

F32 = mybir.dt.float32
BF16 = mybir.dt.bfloat16
I32 = mybir.dt.int32
AF = mybir.ActivationFunctionType
ALU = mybir.AluOpType

NCORE = 8
TOK = 2048
VD = 4096
DD = 512
FF = 2048
NL = 4
CK = 512
NCH = TOK // CK
NSLAB = 4
SLAB = TOK // NSLAB
NTT = 4
TT = 512
NGV = VD // 128
NGD = DD // 128
NGF = FF // 128
EPS = 1e-5
RSQRT_MAGIC = 1597463007.0

DEBUG = False
STAGES = 3


class VSafe:

    HORIZON = 4
    SHORIZON = 12

    def __init__(self, e, state):
        self._e = e
        self._s = state

    @staticmethod
    def _names(aps):
        out = []
        for a in aps:
            if a is None or isinstance(a, (int, float)):
                continue
            t = getattr(a, "tensor", None)
            if t is None:
                continue
            try:
                off = a.offset
                ext = 1
                for step, cnt in list(a.ap)[1:]:
                    ext += (cnt - 1) * abs(step)
            except Exception:
                off, ext = 0, 1 << 40
            out.append((t.name, off, off + ext))
        return out

    @staticmethod
    def _hit(a, b):
        for n1, s1, e1 in a:
            for n2, s2, e2 in b:
                if n1 == n2 and s1 < e2 and s2 < e1:
                    return True
        return False

    def _do(self, ins, outs, call, scal=()):
        ii = self._names(ins)
        oo = self._names(outs)
        ss = self._names(scal)
        hist = self._s["hist"]
        need = False
        for pi, po in hist[-self.HORIZON:]:
            if self._hit(ii, po) or self._hit(oo, po) or self._hit(oo, pi):
                need = True
                break
        if not need and ss:
            for pi, po in hist[-self.SHORIZON:]:
                if self._hit(ss, po):
                    need = True
                    break
        if need:
            self._e.drain()
            self._s["hist"] = []
        r = call()
        self._s["hist"].append((ii + ss, oo))
        del self._s["hist"][:-self.SHORIZON]
        return r

    def tensor_copy(self, out, in_):
        return self._do([in_], [out], lambda: self._e.tensor_copy(out=out, in_=in_))

    def memset(self, ap, c):
        r = self._e.memset(ap, c)
        nm = self._names([ap])
        if self._s["hist"]:
            pi, po = self._s["hist"][-1]
            self._s["hist"][-1] = (pi, po + nm)
        else:
            self._s["hist"].append(([], nm))
        return r

    def tensor_scalar(self, out, in0, scalar1, scalar2, op0, op1=None):
        kw = {} if op1 is None else {"op1": op1}
        return self._do([in0], [out],
                        lambda: self._e.tensor_scalar(out=out, in0=in0, scalar1=scalar1,
                                                      scalar2=scalar2, op0=op0, **kw),
                        scal=[scalar1, scalar2])

    def tensor_tensor(self, out, in0, in1, op):
        return self._do([in0, in1], [out],
                        lambda: self._e.tensor_tensor(out=out, in0=in0, in1=in1, op=op))

    def scalar_tensor_tensor(self, out, in0, scalar, in1, op0, op1):
        return self._do([in0, in1], [out],
                        lambda: self._e.scalar_tensor_tensor(out=out, in0=in0, scalar=scalar,
                                                             in1=in1, op0=op0, op1=op1),
                        scal=[scalar])

    def tensor_tensor_scan(self, out, data0, data1, initial, op0, op1):
        return self._do([data0, data1], [out],
                        lambda: self._e.tensor_tensor_scan(out=out, data0=data0, data1=data1,
                                                           initial=initial, op0=op0, op1=op1),
                        scal=[initial] if not isinstance(initial, float) else ())

    def reciprocal(self, out, in_):
        return self._do([in_], [out], lambda: self._e.reciprocal(out=out, in_=in_))

    def __getattr__(self, name):
        return getattr(self._e, name)


class Sem:
    def __init__(self, es, nc, name):
        self.h = es.enter_context(nc.semaphore(name))
        self.n = 0

    def inc(self, k=1):
        self.n += k
        return self.n


def _rsqrt(e, out, in_, tmp_i, tmp_a):
    e.tensor_copy(out=tmp_a, in_=in_.bitcast(I32))
    e.tensor_scalar(out=tmp_a, in0=tmp_a, scalar1=-0.5 * 2.0 ** -8,
                    scalar2=RSQRT_MAGIC * 2.0 ** -8, op0=ALU.mult, op1=ALU.add)
    e.tensor_copy(out=tmp_i, in_=tmp_a)
    e.tensor_scalar(out=tmp_i, in0=tmp_i, scalar1=8, scalar2=None,
                    op0=ALU.arith_shift_left)
    e.tensor_copy(out=out, in_=tmp_i.bitcast(F32))
    for _ in range(3):
        e.tensor_tensor(out=tmp_a, in0=out, in1=out, op=ALU.mult)
        e.tensor_tensor(out=tmp_a, in0=tmp_a, in1=in_, op=ALU.mult)
        e.tensor_scalar(out=tmp_a, in0=tmp_a, scalar1=-0.5, scalar2=1.5,
                        op0=ALU.mult, op1=ALU.add)
        e.tensor_tensor(out=out, in0=out, in1=tmp_a, op=ALU.mult)


def build_nc(debug=False, stages=3):
    nc = bass.Bass("TRN2", target_bir_lowering=False)

    dxT = nc.dram_tensor("xT", [VD, TOK], F32, kind="ExternalInput").ap()
    dwin = nc.dram_tensor("winT", [VD, DD], F32, kind="ExternalInput").ap()
    dag = nc.dram_tensor("agT", [VD, 68], F32, kind="ExternalInput").ap()
    dw1 = nc.dram_tensor("w1T", [NL, DD, FF], F32, kind="ExternalInput").ap()
    dw2 = nc.dram_tensor("w2T", [NL, FF, DD], F32, kind="ExternalInput").ap()
    dwout = nc.dram_tensor("woutT", [DD, VD], F32, kind="ExternalInput").ap()
    dwff = nc.dram_tensor("wT", [VD, VD], F32, kind="ExternalInput").ap()
    dbT = nc.dram_tensor("bT", [64, VD], F32, kind="ExternalInput").ap()
    dbin = nc.dram_tensor("bin_r", [128, NGD], F32, kind="ExternalInput").ap()
    dln1w = nc.dram_tensor("ln1w_r", [NL, 128, NGD], F32, kind="ExternalInput").ap()
    dln1b = nc.dram_tensor("ln1b_r", [NL, 128, NGD], F32, kind="ExternalInput").ap()
    dln2w = nc.dram_tensor("ln2w_r", [NL, 128, NGD], F32, kind="ExternalInput").ap()
    dln2b = nc.dram_tensor("ln2b_r", [NL, 128, NGD], F32, kind="ExternalInput").ap()
    dscl = nc.dram_tensor("scl_r", [NL, 128, NGD], F32, kind="ExternalInput").ap()
    db2 = nc.dram_tensor("b2_r", [NL, 128, NGD], F32, kind="ExternalInput").ap()
    dphr = nc.dram_tensor("phzre_r", [NL, 128, NGD], F32, kind="ExternalInput").ap()
    dphi = nc.dram_tensor("phzim_r", [NL, 128, NGD], F32, kind="ExternalInput").ap()
    dlcr = nc.dram_tensor("lcre_r", [NL, 128, NGD], F32, kind="ExternalInput").ap()
    dlci = nc.dram_tensor("lcim_r", [NL, 128, NGD], F32, kind="ExternalInput").ap()
    db1 = nc.dram_tensor("b1_r", [NL, 128, NGF], F32, kind="ExternalInput").ap()
    dlnlw = nc.dram_tensor("lnlw_r", [128, NGD], F32, kind="ExternalInput").ap()
    dlnlb = nc.dram_tensor("lnlb_r", [128, NGD], F32, kind="ExternalInput").ap()
    dbout = nc.dram_tensor("bout_r", [128, NGV], F32, kind="ExternalInput").ap()
    dbff = nc.dram_tensor("bff_r", [128, NGV], F32, kind="ExternalInput").ap()
    dgateb = nc.dram_tensor("gateb_r", [1, 4], F32, kind="ExternalInput").ap()
    dmask8 = nc.dram_tensor("mask8", [1, 8], F32, kind="ExternalInput").ap()
    dpat = nc.dram_tensor("patm", [4, 64], BF16, kind="ExternalInput").ap()
    dout = nc.dram_tensor("outT", [VD, TOK], F32, kind="ExternalOutput").ap()

    dxn = nc.dram_tensor("xn_spill", [NGV, 128, TOK], BF16).ap()
    dlni = nc.dram_tensor("lni", [2 * TOK], F32).ap()
    dlno = nc.dram_tensor("lno", [2 * TOK], F32).ap()
    dgsc = nc.dram_tensor("gsc", [4, TOK], F32).ap()
    dgm = nc.dram_tensor("gm", [4, TOK], BF16).ap()
    hcc_i = [nc.dram_tensor(f"cc_i{l}", [128, 8], F32) for l in range(NL)]
    hcc_o = [nc.dram_tensor(f"cc_o{l}", [128 * 8, 8], F32) for l in range(NL)]

    dbg = {}
    if debug:
        dbg["h0"] = nc.dram_tensor("dbg_h0", [128, NGD, TOK], F32, kind="ExternalOutput").ap()
        dbg["xn"] = nc.dram_tensor("dbg_xn", [128, NGV, TOK], BF16, kind="ExternalOutput").ap()
        for l in range(NL):
            dbg[f"h{l+1}"] = nc.dram_tensor(f"dbg_h{l+1}", [128, NGD, TOK], F32, kind="ExternalOutput").ap()
        dbg["hn"] = nc.dram_tensor("dbg_hn", [128, NGD, TOK], BF16, kind="ExternalOutput").ap()
        dbg["um"] = nc.dram_tensor("dbg_um", [64, TOK], BF16, kind="ExternalOutput").ap()

    es = contextlib.ExitStack()
    with es:
        sb = lambda name, shape, dt: es.enter_context(nc.sbuf_tensor(name, shape, dt))
        es2 = contextlib.ExitStack()
        sbm = lambda name, shape, dt: es2.enter_context(nc.sbuf_tensor(name, shape, dt))
        scrbf = sb("scrbf", [128, NGD, TOK], BF16)
        umask = sb("umask", [64, TOK], BF16)
        onesb = sb("onesb", [128, 1], BF16)
        gbb = sb("gbb", [128, 4], F32)
        m8b = sb("m8b", [128, 8], F32)
        h = sbm("h", [128, NGD, TOK], F32)
        onesf = sbm("onesf", [128, CK], F32)
        binv = sbm("binv", [128, NGD], F32)
        lnlw = sbm("lnlw", [128, NGD], F32)
        lnlb = sbm("lnlb", [128, NGD], F32)
        ln1w = sbm("ln1w", [128, NGD], F32)
        ln1b = sbm("ln1b", [128, NGD], F32)
        ln2w = sbm("ln2w", [128, NGD], F32)
        ln2b = sbm("ln2b", [128, NGD], F32)
        sclv = sbm("sclv", [128, NGD], F32)
        b2v = sbm("b2v", [128, NGD], F32)
        b1v = sbm("b1v", [128, NGF], F32)
        phr = sbm("phr", [128, NGD], F32)
        phi = sbm("phi", [128, NGD], F32)
        rho4 = sbm("rho4", [128, NGD], F32)
        ure4 = sbm("ure4", [128, NGD], F32)
        uim4 = sbm("uim4", [128, NGD], F32)
        q4 = sbm("q4", [128, NGD, 2], F32)
        u5 = sbm("u5", [128, NGD, 2], F32)
        p2k = sbm("p2k", [128, NGD, 2], F32)
        pm1 = sbm("pm1", [128, NGD], F32)
        qpow = sbm("qpow", [128, NCH, NGD, 2], F32)
        iot = sbm("iot", [128, NGD, NCH, 2], F32)
        iosc = sbm("iosc", [128, NGD, NCH, 2], F32)
        Ev = sbm("Ev", [128, NGD, 2], F32)
        gat = sbm("gat", [128, 8, NGD, 2], F32)
        Hre = sbm("Hre", [128, NGD], F32)
        Him = sbm("Him", [128, NGD], F32)
        tA = sbm("tA", [128, NGD], F32)
        tB = sbm("tB", [128, NGD], F32)
        tC = sbm("tC", [128, NGD], F32)
        tD = sbm("tD", [128, NGD, 2], F32)
        tE = sbm("tE", [128, NGD, 2], F32)
        scs = sbm("scs", [128, 2], F32)
        sct = sbm("sct", [128, 4], F32)
        scs4 = sbm("scs4", [128, NGD, 2], F32)
        sct4 = sbm("sct4", [128, NGD, 4], F32)
        ends4 = sbm("ends4", [128, NGD, NCH, 2], F32)
        tF = sbm("tF", [128, NGD, 2], F32)
        statin = sbm("statin", [1, 2 * TOK], F32)
        statq = sbm("statq", [128, 2, TOK // 128], F32)
        stato = sbm("stato", [128, 2, TOK // 128], F32)
        stati = sbm("stati", [128, TOK // 128], I32)
        stata = sbm("stata", [128, TOK // 128], F32)
        rstdb = sbm("rstdb", [128, TOK], F32)
        nub = sbm("nub", [128, TOK], F32)
        hbt = sbm("hbt", [128, TOK], BF16)
        h2t = sbm("h2t", [128, TOK], BF16)
        ytmp = sbm("ytmp", [128, TOK], F32)
        ps = [es.enter_context(nc.psum_tensor(f"ps{i}", [128, TT], F32)) for i in range(8)]

        LD = Sem(es, nc, "LD")
        GD = Sem(es, nc, "GD")
        SV = Sem(es, nc, "SV")
        SSm = Sem(es, nc, "SS")
        ST = Sem(es, nc, "ST")
        SG = Sem(es, nc, "SG")
        CC = Sem(es, nc, "CC")
        GW = [Sem(es, nc, "GW0"), Sem(es, nc, "GW1")]

        blk = es.enter_context(nc.Block())
        _vstate = {"hist": []}

        class _BlkWrap:
            def __init__(self, b):
                self._b = b

            def vector(self, f):
                return self._b.vector(lambda e: f(VSafe(e, _vstate)))

            def __getattr__(self, name):
                return getattr(self._b, name)

        blk = _BlkWrap(blk)

        def dma_w(e, S, out, in_):
            e.dma_start(out=out, in_=in_).then_inc(S.h, 16)
            e.wait_ge(S.h, S.inc(16))

        def wait_all(e):
            for s in (LD, GD, SV, SSm, ST, SG, CC, GW[0], GW[1]):
                if s.n:
                    e.wait_ge(s.h, s.n)

        OWN = {"sync": LD, "scalar": SSm, "vector": SV, "gpsimd": SG, "tensor": ST}

        def barrier(engines=("sync", "scalar", "vector", "gpsimd", "tensor")):
            for name in engines:
                def _g(e):
                    wait_all(e)
                getattr(blk, name)(_g)

        def _p0_sync(e):
            for dst, srcap in ((binv, dbin), (lnlw, dlnlw), (lnlb, dlnlb)):
                dma_w(e, LD, dst[:], srcap)
            dma_w(e, LD, gbb[:], dgateb.broadcast_to([128, 4]))
            dma_w(e, LD, m8b[:], dmask8.broadcast_to([128, 8]))
        blk.sync(_p0_sync)

        def _p0_scal(e):
            e.preload_act_table(AF.Silu)
        blk.scalar(_p0_scal)

        def _p0_vec(e):
            e.memset(onesb[:], 1.0)
            e.memset(onesf[:], 1.0).then_inc(SV.h, 1)
        blk.vector(_p0_vec)
        SV.inc(1)

        def ln_finish(width, inv_n, n_tiles, sig_banks, sq_banks):
            kq = width // 128

            def _evac(e):
                wait_all(e)
                for i in range(n_tiles):
                    e.activation(out=statin[0:1, i * TT:i * TT + TT], in_=sig_banks[i], func=AF.Identity)
                for i in range(n_tiles - 1):
                    e.activation(out=statin[0:1, width + i * TT:width + i * TT + TT],
                                 in_=sq_banks[i], func=AF.Identity)
                i = n_tiles - 1
                e.activation(out=statin[0:1, width + i * TT:width + i * TT + TT],
                             in_=sq_banks[i], func=AF.Identity).then_inc(SSm.h, 1)
            blk.scalar(_evac)
            SSm.inc(1)

            def _bounce1(e):
                e.wait_ge(SSm.h, SSm.n)
                dma_w(e, LD, dlni[0:2 * width].unsqueeze(0), statin[0:1, 0:2 * width])
                dma_w(e, LD, statq[:, :, 0:kq],
                      dlni[0:2 * width].rearrange("(a p k) -> p a k", p=128, a=2))
            blk.sync(_bounce1)

            def _finish(e):
                e.wait_ge(LD.h, LD.n)
                mu = stato[:, 0, 0:kq]
                var = stato[:, 1, 0:kq]
                e.tensor_scalar(out=mu, in0=statq[:, 0, 0:kq], scalar1=inv_n, scalar2=None, op0=ALU.mult)
                e.tensor_scalar(out=var, in0=statq[:, 1, 0:kq], scalar1=inv_n, scalar2=EPS,
                                op0=ALU.mult, op1=ALU.add)
                e.tensor_tensor(out=stata[:, 0:kq], in0=mu, in1=mu, op=ALU.mult)
                e.tensor_tensor(out=var, in0=var, in1=stata[:, 0:kq], op=ALU.subtract)
                _rsqrt(e, statq[:, 0, 0:kq], var, stati[:, 0:kq], stata[:, 0:kq])
                e.tensor_tensor(out=statq[:, 1, 0:kq], in0=mu, in1=statq[:, 0, 0:kq],
                                op=ALU.mult).then_inc(SV.h, 1)
            blk.vector(_finish)
            SV.inc(1)

            def _bounce2(e):
                e.wait_ge(SV.h, SV.n)
                dma_w(e, LD, dlno[0:2 * width].rearrange("(a p k) -> p a k", p=128, a=2),
                      statq[:, :, 0:kq])
                dma_w(e, LD, rstdb[:, 0:width],
                      dlno[0:width].unsqueeze(0).broadcast_to([128, width]))
                dma_w(e, LD, nub[:, 0:width],
                      dlno[width:2 * width].unsqueeze(0).broadcast_to([128, width]))
            blk.sync(_bounce2)

        def attn_ln_stats():
            for g in range(NGD):
                def _cst(e, g=g):
                    if g == 0:
                        wait_all(e)
                    else:
                        e.wait_ge(ST.h, ST.n)
                    e.tensor_copy(out=hbt[:, :], in_=h[:, g, :]).then_inc(SV.h, 1)
                blk.vector(_cst)
                SV.inc(1)

                def _sq(e):
                    e.wait_ge(SV.h, SV.n)
                    e.activation(out=h2t[:, :], in_=hbt[:, :], func=AF.Square).then_inc(SSm.h, 1)
                blk.scalar(_sq)
                SSm.inc(1)

                def _mm(e, g=g):
                    e.wait_ge(SSm.h, SSm.n)
                    e.wait_ge(SV.h, 1)
                    ins = None
                    for tt in range(NTT):
                        e.matmul(out=ps[tt][0:1, :], lhsT=onesb[:], rhs=hbt[:, tt * TT:tt * TT + TT],
                                 start=(g == 0), stop=(g == NGD - 1))
                        ins = e.matmul(out=ps[4 + tt][0:1, :], lhsT=onesb[:], rhs=h2t[:, tt * TT:tt * TT + TT],
                                       start=(g == 0), stop=(g == NGD - 1))
                    ins.then_inc(ST.h, 1)
                blk.tensor(_mm)
                ST.inc(1)
            ln_finish(TOK, 1.0 / DD, NTT, [ps[tt][0:1, :] for tt in range(NTT)],
                      [ps[4 + tt][0:1, :] for tt in range(NTT)])
            barrier(("vector", "scalar", "sync"))

        with (
            nc.sbuf_tensor("xslab", [128, NGV, SLAB], BF16) as xslab,
            nc.sbuf_tensor("x2t", [128, 2, SLAB], BF16) as x2t,
            nc.sbuf_tensor("winb", [128, NGV, DD], BF16) as winb,
            nc.sbuf_tensor("agb", [128, NGV, 68], BF16) as agb,
            nc.sbuf_tensor("usc", [128, SLAB], F32) as usc,
            nc.sbuf_tensor("mrows", [4, TOK], BF16) as mrows,
            nc.sbuf_tensor("scm", [128, 4, TOK // 128], F32) as scm,
            nc.sbuf_tensor("sc2", [128, 2, TOK // 128], F32) as sc2,
            nc.sbuf_tensor("mpk", [128, 4, TOK // 128], BF16) as mpk,
            nc.sbuf_tensor("pat", [4, 64], BF16) as pat,
        ):
            def _w1(e):
                dma_w(e, GD, winb[:], dwin.rearrange("(g p) d -> p g d", p=128))
                dma_w(e, GD, agb[:], dag.rearrange("(g p) d -> p g d", p=128))
            blk.gpsimd(_w1)

            for s in range(NSLAB):
                t0 = s * SLAB

                def _lx(e, t0=t0):
                    dma_w(e, GD, xslab[:],
                          dxT[:, t0:t0 + SLAB].rearrange("(g p) t -> p g t", p=128))
                blk.gpsimd(_lx)
                gd_x = GD.n

                st0 = ST.n
                sq0 = SSm.n

                def _sq1(e, gd_x=gd_x, st0=st0):
                    e.wait_ge(GD.h, gd_x)
                    e.wait_ge(SV.h, 1)
                    for g in range(NGV):
                        if g >= 2:
                            e.wait_ge(ST.h, st0 + g - 1)
                        e.activation(out=x2t[:, g % 2, :], in_=xslab[:, g, :],
                                     func=AF.Square).then_inc(SSm.h, 1)
                blk.scalar(_sq1)
                SSm.inc(NGV)

                def _mms(e, gd_x=gd_x, sq0=sq0):
                    e.wait_ge(GD.h, gd_x)
                    e.wait_ge(SV.h, 1)
                    for g in range(NGV):
                        e.wait_ge(SSm.h, sq0 + g + 1)
                        e.matmul(out=ps[0][0:1, :], lhsT=onesb[:], rhs=xslab[:, g, :],
                                 start=(g == 0), stop=(g == NGV - 1))
                        e.matmul(out=ps[1][0:1, :], lhsT=onesb[:], rhs=x2t[:, g % 2, :],
                                 start=(g == 0), stop=(g == NGV - 1)).then_inc(ST.h, 1)
                blk.tensor(_mms)
                ST.inc(NGV)

                ln_finish(SLAB, 1.0 / VD, 1, [ps[0][0:1, :]], [ps[1][0:1, :]])

                def _xn(e):
                    wait_all(e)
                    for g in range(NGV):
                        e.tensor_tensor(out=ytmp[:, 0:SLAB], in0=xslab[:, g, :],
                                        in1=rstdb[:, 0:SLAB], op=ALU.mult)
                        ins = e.tensor_tensor(out=xslab[:, g, :], in0=ytmp[:, 0:SLAB],
                                              in1=nub[:, 0:SLAB], op=ALU.subtract)
                    ins.then_inc(SV.h, 1)
                blk.vector(_xn)
                SV.inc(1)

                def _spill(e, t0=t0):
                    e.wait_ge(SV.h, SV.n)
                    dma_w(e, LD, dxn[:, :, t0:t0 + SLAB].rearrange("g p t -> p g t"), xslab[:])
                blk.sync(_spill)

                def _mmw(e):
                    e.wait_ge(SV.h, SV.n)
                    e.wait_ge(GD.h, GD.n)
                    e.wait_ge(SSm.h, SSm.n)
                    for g in range(NGV):
                        for m in range(NGD):
                            e.matmul(out=ps[2 + m][:, :], lhsT=winb[:, g, m * 128:m * 128 + 128],
                                     rhs=xslab[:, g, :], start=(g == 0), stop=(g == NGV - 1))
                        ins = e.matmul(out=ps[6][0:68, :], lhsT=agb[:, g, :], rhs=xslab[:, g, :],
                                       start=(g == 0), stop=(g == NGV - 1))
                    ins.then_inc(ST.h, 1)
                blk.tensor(_mmw)
                ST.inc(1)

                def _evh(e, t0=t0):
                    e.wait_ge(ST.h, ST.n)
                    for m in range(NGD):
                        e.activation(out=h[:, m, t0:t0 + SLAB], in_=ps[2 + m][:, :],
                                     func=AF.Identity, bias=binv[:, m:m + 1])
                    e.activation(out=umask[0:64, t0:t0 + SLAB], in_=ps[6][0:64, :], func=AF.Identity)
                    e.activation(out=usc[64:68, :], in_=ps[6][64:68, :],
                                 func=AF.Identity).then_inc(SSm.h, 1)
                blk.scalar(_evh)
                SSm.inc(1)

                def _ssc(e, t0=t0):
                    e.wait_ge(SSm.h, SSm.n)
                    dma_w(e, LD, dgsc[:, t0:t0 + SLAB], usc[64:68, :])
                blk.sync(_ssc)
                barrier()

            def _mld(e):
                wait_all(e)
                dma_w(e, LD, pat[:], dpat)
                for r in range(4):
                    dma_w(e, LD, scm[:, r, :], dgsc[r].rearrange("(p k) -> p k", p=128))
            blk.sync(_mld)

            def _mcalc(e):
                e.wait_ge(LD.h, LD.n)
                KQ = TOK // 128
                e.tensor_tensor(out=scm[:], in0=scm[:],
                                in1=gbb[:].unsqueeze(2).broadcast_to([128, 4, KQ]), op=ALU.add)
                e.tensor_tensor(out=sc2[:, 0, :], in0=scm[:, 0, :], in1=scm[:, 1, :], op=ALU.max)
                e.tensor_tensor(out=sc2[:, 1, :], in0=scm[:, 2, :], in1=scm[:, 3, :], op=ALU.max)
                e.tensor_tensor(out=sc2[:, 0, :], in0=sc2[:, 0, :], in1=sc2[:, 1, :], op=ALU.max)
                mx = sc2[:, 0, :]
                e.memset(mpk[:, 0, :], 1.0)
                e.tensor_tensor(out=sc2[:, 1, :], in0=scm[:, 0, :], in1=mx, op=ALU.is_lt)
                e.tensor_copy(out=mpk[:, 1, :], in_=sc2[:, 1, :])
                e.tensor_tensor(out=scm[:, 1, :], in0=scm[:, 1, :], in1=mx, op=ALU.is_lt)
                e.tensor_tensor(out=sc2[:, 1, :], in0=sc2[:, 1, :], in1=scm[:, 1, :], op=ALU.mult)
                e.tensor_copy(out=mpk[:, 2, :], in_=sc2[:, 1, :])
                e.tensor_tensor(out=scm[:, 2, :], in0=scm[:, 2, :], in1=mx, op=ALU.is_lt)
                e.tensor_tensor(out=sc2[:, 1, :], in0=sc2[:, 1, :], in1=scm[:, 2, :], op=ALU.mult)
                e.tensor_copy(out=mpk[:, 3, :], in_=sc2[:, 1, :]).then_inc(SV.h, 1)
            blk.vector(_mcalc)
            SV.inc(1)

            def _mst(e):
                e.wait_ge(SV.h, SV.n)
                for r in range(4):
                    dma_w(e, LD, dgm[r].rearrange("(p k) -> p k", p=128), mpk[:, r, :])
                dma_w(e, LD, mrows[:], dgm)
            blk.sync(_mst)

            for tt in range(NTT):
                sv_mm = SV.n

                def _mmm(e, tt=tt, sv_mm=sv_mm):
                    e.wait_ge(LD.h, LD.n)
                    e.wait_ge(SV.h, sv_mm)
                    e.matmul(out=ps[7][0:64, :], lhsT=pat[:], rhs=mrows[:, tt * TT:tt * TT + TT],
                             start=True, stop=True).then_inc(ST.h, 1)
                blk.tensor(_mmm)
                ST.inc(1)

                def _umul(e, tt=tt):
                    e.wait_ge(ST.h, ST.n)
                    e.tensor_tensor(out=umask[0:64, tt * TT:tt * TT + TT],
                                    in0=umask[0:64, tt * TT:tt * TT + TT],
                                    in1=ps[7][0:64, :], op=ALU.mult).then_inc(SV.h, 1)
                blk.vector(_umul)
                SV.inc(1)
            barrier()

            if debug:
                def _dbg1(e):
                    wait_all(e)
                    dma_w(e, LD, dbg["h0"], h[:])
                    dma_w(e, LD, dbg["um"], umask[:])
                blk.sync(_dbg1)
                barrier()

        if stages < 2:
            def _fin1(e):
                wait_all(e)
            for name in ("sync", "scalar", "vector", "gpsimd", "tensor"):
                getattr(blk, name)(_fin1)
            es2.close()
            return nc
        with (
            nc.sbuf_tensor("dw_", [128, NCH, CK, 2], F32) as dwv,
            nc.sbuf_tensor("midt", [128, NGF, TT], BF16) as midt,
            nc.sbuf_tensor("rint", [128, NGD, CK, 2], F32) as rint,
            nc.sbuf_tensor("dk4", [128, NGD, CK], F32) as dk4,
            nc.sbuf_tensor("w1b", [128, NGD, FF], BF16) as w1b,
            nc.sbuf_tensor("w2b", [128, NGF, DD], BF16) as w2b,
            nc.sbuf_tensor("rhot", [128, CK], F32) as rhot,
        ):
            for l in range(NL):
                def _lc(e, l=l):
                    for dst, srcap in ((ln1w, dln1w), (ln1b, dln1b), (ln2w, dln2w), (ln2b, dln2b),
                                       (sclv, dscl), (b2v, db2), (phr, dphr), (phi, dphi)):
                        dma_w(e, LD, dst[:], srcap[l])
                    dma_w(e, LD, b1v[:], db1[l])
                    dma_w(e, LD, Hre[:], dlcr[l])
                    dma_w(e, LD, Him[:], dlci[l])
                blk.sync(_lc)

                def _lw(e, l=l):
                    dma_w(e, GD, w1b[:], dw1[l].rearrange("(k p) f -> p k f", p=128))
                    dma_w(e, GD, w2b[:], dw2[l].rearrange("(k p) f -> p k f", p=128))
                blk.gpsimd(_lw)

                def _pc(e, l=l):
                    wait_all(e)
                    e.tensor_tensor(out=tA[:], in0=phr[:], in1=phr[:], op=ALU.mult)
                    e.tensor_tensor(out=tB[:], in0=phi[:], in1=phi[:], op=ALU.mult)
                    e.tensor_tensor(out=tA[:], in0=tA[:], in1=tB[:], op=ALU.add)
                    _rsqrt(e, tB[:], tA[:], stati[:, 0:NGD], sct[:])
                    e.tensor_tensor(out=tC[:], in0=tA[:], in1=tB[:], op=ALU.mult)
                    e.tensor_tensor(out=ure4[:], in0=phr[:], in1=tB[:], op=ALU.mult)
                    e.tensor_tensor(out=uim4[:], in0=phi[:], in1=tB[:], op=ALU.mult).then_inc(SV.h, 1)
                blk.vector(_pc)
                SV.inc(1)

                def _tanh(e):
                    e.wait_ge(SV.h, SV.n)
                    e.activation(out=rho4[:], in_=tC[:], func=AF.Tanh).then_inc(SSm.h, 1)
                blk.scalar(_tanh)
                SSm.inc(1)

                def _tbl(e):
                    e.wait_ge(SSm.h, SSm.n)
                    for g in range(NGD):
                        ri = rint[:, g]
                        e.memset(ri[:, 0:1, 0:1], 1.0)
                        e.memset(ri[:, 0:1, 1:2], 0.0)
                        e.tensor_copy(out=dk4[:, g, 0:1], in_=rho4[:, g:g + 1])
                        e.tensor_copy(out=scs4[:, g, 0:1], in_=ure4[:, g:g + 1])
                        e.tensor_copy(out=scs4[:, g, 1:2], in_=uim4[:, g:g + 1])
                    n = 1
                    while n < CK:
                        m = min(n, CK - n)
                        for g in range(NGD):
                            e.tensor_scalar(out=ytmp[:, g * 512:g * 512 + m], in0=rint[:, g, 0:m, 1],
                                            scalar1=scs4[:, g, 1:2], scalar2=None, op0=ALU.mult)
                        for g in range(NGD):
                            e.scalar_tensor_tensor(out=rint[:, g, n:n + m, 0], in0=rint[:, g, 0:m, 0],
                                                   scalar=scs4[:, g, 0:1], in1=ytmp[:, g * 512:g * 512 + m],
                                                   op0=ALU.mult, op1=ALU.add)
                        for g in range(NGD):
                            e.tensor_scalar(out=ytmp[:, g * 512:g * 512 + m], in0=rint[:, g, 0:m, 0],
                                            scalar1=scs4[:, g, 1:2], scalar2=None, op0=ALU.mult)
                        for g in range(NGD):
                            e.scalar_tensor_tensor(out=rint[:, g, n:n + m, 1], in0=rint[:, g, 0:m, 1],
                                                   scalar=scs4[:, g, 0:1], in1=ytmp[:, g * 512:g * 512 + m],
                                                   op0=ALU.mult, op1=ALU.subtract)
                        for g in range(NGD):
                            e.tensor_scalar(out=dk4[:, g, n:n + m], in0=dk4[:, g, 0:m],
                                            scalar1=dk4[:, g, n - 1:n], scalar2=None, op0=ALU.mult)
                        for g in range(NGD):
                            e.tensor_tensor(out=sct4[:, g, 0:2], in0=scs4[:, g, 0:2],
                                            in1=scs4[:, g, 0:2], op=ALU.mult)
                        for g in range(NGD):
                            e.tensor_tensor(out=sct4[:, g, 2:3], in0=scs4[:, g, 0:1],
                                            in1=scs4[:, g, 1:2], op=ALU.mult)
                        for g in range(NGD):
                            e.tensor_tensor(out=scs4[:, g, 0:1], in0=sct4[:, g, 0:1],
                                            in1=sct4[:, g, 1:2], op=ALU.subtract)
                        for g in range(NGD):
                            e.tensor_scalar(out=scs4[:, g, 1:2], in0=sct4[:, g, 2:3], scalar1=2.0,
                                            scalar2=None, op0=ALU.mult)
                        n *= 2
                    e.tensor_copy(out=u5[:], in_=scs4[:])
                    e.tensor_tensor(out=q4[:, :, 0], in0=scs4[:, :, 0],
                                    in1=dk4[:, :, CK - 1], op=ALU.mult)
                    e.tensor_tensor(out=q4[:, :, 1], in0=scs4[:, :, 1],
                                    in1=dk4[:, :, CK - 1], op=ALU.mult)

                    def csq(dst, src_):
                        e.tensor_tensor(out=tD[:, :, 0], in0=src_[:, :, 0], in1=src_[:, :, 0], op=ALU.mult)
                        e.tensor_tensor(out=tA[:], in0=src_[:, :, 1], in1=src_[:, :, 1], op=ALU.mult)
                        e.tensor_tensor(out=tD[:, :, 1], in0=src_[:, :, 0], in1=src_[:, :, 1], op=ALU.mult)
                        e.tensor_tensor(out=dst[:, :, 0], in0=tD[:, :, 0], in1=tA[:], op=ALU.subtract)
                        e.tensor_scalar(out=dst[:, :, 1], in0=tD[:, :, 1], scalar1=2.0, scalar2=None,
                                        op0=ALU.mult)
                    csq(tE, q4)
                    csq(p2k, tE)
                    e.tensor_scalar(out=pm1[:], in0=p2k[:, :, 0], scalar1=1.0, scalar2=-1.0,
                                    op0=ALU.mult, op1=ALU.add)
                    e.memset(qpow[:, 0, :, 0], 1.0)
                    e.memset(qpow[:, 0, :, 1], 0.0)
                    for k in range(1, NCH):
                        a = qpow[:, k - 1]
                        e.tensor_tensor(out=tD[:, :, 0], in0=a[:, :, 0], in1=q4[:, :, 0], op=ALU.mult)
                        e.tensor_tensor(out=tA[:], in0=a[:, :, 1], in1=q4[:, :, 1], op=ALU.mult)
                        e.tensor_tensor(out=tD[:, :, 1], in0=a[:, :, 0], in1=q4[:, :, 1], op=ALU.mult)
                        e.tensor_tensor(out=tB[:], in0=a[:, :, 1], in1=q4[:, :, 0], op=ALU.mult)
                        e.tensor_tensor(out=qpow[:, k, :, 0], in0=tD[:, :, 0], in1=tA[:], op=ALU.subtract)
                        e.tensor_tensor(out=qpow[:, k, :, 1], in0=tD[:, :, 1], in1=tB[:], op=ALU.add)
                    e.tensor_copy(out=sct[:, 0:1], in_=sct[:, 0:1]).then_inc(SV.h, 1)
                blk.vector(_tbl)
                SV.inc(1)
                barrier(("vector", "scalar", "tensor"))

                attn_ln_stats()

                for g in range(NGD):
                    def _nrm(e, g=g):
                        if g == 0:
                            wait_all(e)
                        else:
                            e.wait_ge(SSm.h, SSm.n)
                        e.tensor_tensor(out=ytmp[:], in0=h[:, g, :], in1=rstdb[:], op=ALU.mult)
                        e.tensor_tensor(out=ytmp[:], in0=ytmp[:], in1=nub[:],
                                        op=ALU.subtract).then_inc(SV.h, 1)
                    blk.vector(_nrm)
                    SV.inc(1)

                    def _silu(e, g=g):
                        e.wait_ge(SV.h, SV.n)
                        e.activation(out=ytmp[:], in_=ytmp[:], func=AF.Silu,
                                     scale=ln1w[:, g:g + 1], bias=ln1b[:, g:g + 1]).then_inc(SSm.h, 1)
                    blk.scalar(_silu)
                    SSm.inc(1)

                    def _scan(e, g=g):
                        e.wait_ge(SSm.h, SSm.n)
                        e.tensor_tensor(
                            out=dwv[:],
                            in0=ytmp[:].rearrange("p (c t) -> p c t", c=NCH).unsqueeze(3).broadcast_to([128, NCH, CK, 2]),
                            in1=rint[:, g].unsqueeze(1).broadcast_to([128, NCH, CK, 2]),
                            op=ALU.mult)
                        e.tensor_scalar(out=rhot[:], in0=onesf[:], scalar1=rho4[:, g:g + 1],
                                        scalar2=None, op0=ALU.mult)
                        for comp in range(2):
                            for c in range(NCH):
                                e.tensor_tensor_scan(out=dwv[:, c, :, comp], data0=rhot[:],
                                                     data1=dwv[:, c, :, comp], initial=0.0,
                                                     op0=ALU.mult, op1=ALU.add)
                        e.tensor_copy(out=ends4[:, g], in_=dwv[:, :, CK - 1, :])
                        e.tensor_tensor(out=dwv[:], in0=dwv[:],
                                        in1=rint[:, g].unsqueeze(1).broadcast_to([128, NCH, CK, 2]),
                                        op=ALU.mult)
                        hv = h[:, g, :].rearrange("p (c t) -> p c t", c=NCH)
                        e.scalar_tensor_tensor(out=hv, in0=dwv[:, :, :, 0], scalar=sclv[:, g:g + 1],
                                               in1=hv, op0=ALU.mult, op1=ALU.add)
                        e.scalar_tensor_tensor(out=hv, in0=dwv[:, :, :, 1], scalar=sclv[:, g:g + 1],
                                               in1=hv, op0=ALU.mult, op1=ALU.add).then_inc(SV.h, 1)
                    blk.vector(_scan)
                    SV.inc(1)

                def _chains(e):
                    e.wait_ge(SV.h, SV.n)
                    e.memset(iot[:, :, 0, :], 0.0)
                    for c in range(1, NCH):
                        pi = iot[:, :, c - 1, :]
                        pe = ends4[:, :, c - 1, :]
                        e.tensor_tensor(out=tD[:, :, 0], in0=q4[:, :, 0], in1=pi[:, :, 0], op=ALU.mult)
                        e.tensor_tensor(out=tE[:, :, 0], in0=u5[:, :, 0], in1=pe[:, :, 0], op=ALU.mult)
                        e.tensor_tensor(out=tD[:, :, 1], in0=q4[:, :, 0], in1=pi[:, :, 1], op=ALU.mult)
                        e.tensor_tensor(out=tE[:, :, 1], in0=u5[:, :, 0], in1=pe[:, :, 1], op=ALU.mult)
                        e.tensor_tensor(out=tA[:], in0=q4[:, :, 1], in1=pi[:, :, 1], op=ALU.mult)
                        e.tensor_tensor(out=tB[:], in0=u5[:, :, 1], in1=pe[:, :, 1], op=ALU.mult)
                        e.tensor_tensor(out=tD[:, :, 0], in0=tD[:, :, 0], in1=tA[:], op=ALU.subtract)
                        e.tensor_tensor(out=tE[:, :, 0], in0=tE[:, :, 0], in1=tB[:], op=ALU.subtract)
                        e.tensor_tensor(out=tA[:], in0=q4[:, :, 1], in1=pi[:, :, 0], op=ALU.mult)
                        e.tensor_tensor(out=tB[:], in0=u5[:, :, 1], in1=pe[:, :, 0], op=ALU.mult)
                        e.tensor_tensor(out=tD[:, :, 1], in0=tD[:, :, 1], in1=tA[:], op=ALU.add)
                        e.tensor_tensor(out=tE[:, :, 1], in0=tE[:, :, 1], in1=tB[:], op=ALU.add)
                        e.tensor_tensor(out=iot[:, :, c, :], in0=tD[:, :, :], in1=tE[:, :, :], op=ALU.add)
                    e.tensor_tensor(out=tD[:, :, 0], in0=iot[:, :, NCH - 1, 0],
                                    in1=dk4[:, :, CK - 1], op=ALU.mult)
                    e.tensor_tensor(out=tD[:, :, 1], in0=iot[:, :, NCH - 1, 1],
                                    in1=dk4[:, :, CK - 1], op=ALU.mult)
                    e.tensor_tensor(out=tD[:, :, :], in0=tD[:, :, :],
                                    in1=ends4[:, :, NCH - 1, :], op=ALU.add)
                    e.tensor_copy(out=tF[:, :, 0], in_=rint[:, :, CK - 1, 0])
                    e.tensor_scalar(out=tF[:, :, 1], in0=rint[:, :, CK - 1, 1], scalar1=-1.0,
                                    scalar2=None, op0=ALU.mult)
                    e.tensor_tensor(out=tA[:], in0=tF[:, :, 0], in1=tD[:, :, 0], op=ALU.mult)
                    e.tensor_tensor(out=tB[:], in0=tF[:, :, 1], in1=tD[:, :, 1], op=ALU.mult)
                    e.tensor_tensor(out=tC[:], in0=tF[:, :, 1], in1=tD[:, :, 0], op=ALU.mult)
                    e.tensor_tensor(out=tE[:, :, 1], in0=tF[:, :, 0], in1=tD[:, :, 1], op=ALU.mult)
                    e.tensor_tensor(out=Ev[:, :, 0], in0=tA[:], in1=tB[:], op=ALU.subtract)
                    e.tensor_tensor(out=Ev[:, :, 1], in0=tE[:, :, 1], in1=tC[:], op=ALU.add).then_inc(SV.h, 1)
                blk.vector(_chains)
                SV.inc(1)

                def _pub(e, l=l):
                    e.wait_ge(SV.h, SV.n)
                    dma_w(e, GD, hcc_i[l].ap(), Ev[:].rearrange("p g c -> p (g c)"))
                    e.collective_compute("AllGather", ALU.bypass, replica_groups=[list(range(NCORE))],
                                         ins=[hcc_i[l].ap().opt()], outs=[hcc_o[l].ap().opt()]
                                         ).then_inc(CC.h, 1)
                    CC.inc(1)
                    e.wait_ge(CC.h, CC.n)
                    dma_w(e, GD, gat[:].rearrange("p s g c -> p s (g c)"),
                          hcc_o[l].ap().rearrange("(s p) c -> p s c", p=128))
                blk.gpsimd(_pub)

                def _comb(e):
                    e.wait_ge(GD.h, GD.n)
                    for j in range(NCORE):
                        mj = m8b[:, j:j + 1]
                        e.tensor_scalar(out=tA[:], in0=pm1[:], scalar1=mj, scalar2=1.0,
                                        op0=ALU.mult, op1=ALU.add)
                        e.tensor_scalar(out=tB[:], in0=p2k[:, :, 1], scalar1=mj, scalar2=None, op0=ALU.mult)
                        e.tensor_tensor(out=tC[:], in0=tB[:], in1=Him[:], op=ALU.mult)
                        e.tensor_tensor(out=tD[:, :, 0], in0=tA[:], in1=Hre[:], op=ALU.mult)
                        e.tensor_tensor(out=tD[:, :, 0], in0=tD[:, :, 0], in1=tC[:], op=ALU.subtract)
                        e.tensor_scalar(out=tC[:], in0=gat[:, j, :, 0], scalar1=mj, scalar2=None, op0=ALU.mult)
                        e.tensor_tensor(out=tD[:, :, 0], in0=tD[:, :, 0], in1=tC[:], op=ALU.add)
                        e.tensor_tensor(out=tC[:], in0=tB[:], in1=Hre[:], op=ALU.mult)
                        e.tensor_tensor(out=tD[:, :, 1], in0=tA[:], in1=Him[:], op=ALU.mult)
                        e.tensor_tensor(out=tD[:, :, 1], in0=tD[:, :, 1], in1=tC[:], op=ALU.add)
                        e.tensor_scalar(out=tC[:], in0=gat[:, j, :, 1], scalar1=mj, scalar2=None, op0=ALU.mult)
                        e.tensor_tensor(out=tD[:, :, 1], in0=tD[:, :, 1], in1=tC[:], op=ALU.add)
                        e.tensor_copy(out=Hre[:], in_=tD[:, :, 0])
                        e.tensor_copy(out=Him[:], in_=tD[:, :, 1])
                    e.tensor_tensor(out=tA[:], in0=ure4[:], in1=Hre[:], op=ALU.mult)
                    e.tensor_tensor(out=tB[:], in0=uim4[:], in1=Him[:], op=ALU.mult)
                    e.tensor_tensor(out=tA[:], in0=tA[:], in1=tB[:], op=ALU.subtract)
                    e.tensor_tensor(out=tB[:], in0=ure4[:], in1=Him[:], op=ALU.mult)
                    e.tensor_tensor(out=tC[:], in0=uim4[:], in1=Hre[:], op=ALU.mult)
                    e.tensor_tensor(out=tB[:], in0=tB[:], in1=tC[:], op=ALU.add)
                    for c in range(NCH):
                        qp = qpow[:, c]
                        e.tensor_tensor(out=tD[:, :, 0], in0=qp[:, :, 0], in1=tA[:], op=ALU.mult)
                        e.tensor_tensor(out=tC[:], in0=qp[:, :, 1], in1=tB[:], op=ALU.mult)
                        e.tensor_tensor(out=tD[:, :, 0], in0=tD[:, :, 0], in1=tC[:], op=ALU.subtract)
                        e.tensor_tensor(out=tD[:, :, 1], in0=qp[:, :, 0], in1=tB[:], op=ALU.mult)
                        e.tensor_tensor(out=tC[:], in0=qp[:, :, 1], in1=tA[:], op=ALU.mult)
                        e.tensor_tensor(out=tD[:, :, 1], in0=tD[:, :, 1], in1=tC[:], op=ALU.add)
                        for comp in range(2):
                            e.tensor_tensor(out=iot[:, :, c, comp], in0=iot[:, :, c, comp],
                                            in1=tD[:, :, comp], op=ALU.add)
                    e.tensor_tensor(out=iosc[:], in0=iot[:],
                                    in1=sclv[:].unsqueeze(2).unsqueeze(3).broadcast_to([128, NGD, NCH, 2]),
                                    op=ALU.mult).then_inc(SV.h, 1)
                blk.vector(_comb)
                SV.inc(1)

                def _ctr(e):
                    e.wait_ge(SV.h, SV.n)
                    t12s = [ytmp[:, 0:CK * 2].rearrange("p (t x) -> p t x", x=2),
                            ytmp[:, CK * 2:CK * 4].rearrange("p (t x) -> p t x", x=2)]

                    def build(g):
                        e.tensor_tensor(out=t12s[g % 2], in0=rint[:, g],
                                        in1=dk4[:, g].unsqueeze(2).broadcast_to([128, CK, 2]),
                                        op=ALU.mult)

                    def stts(g):
                        for comp in range(2):
                            for c in range(NCH):
                                hs = h[:, g, c * CK:c * CK + CK]
                                e.scalar_tensor_tensor(out=hs, in0=t12s[g % 2][:, :, comp],
                                                       scalar=iosc[:, g, c, comp:comp + 1],
                                                       in1=hs, op0=ALU.mult, op1=ALU.add)
                    build(0)
                    build(1)
                    stts(0)
                    build(2)
                    stts(1)
                    build(3)
                    stts(2)
                    stts(3)
                    e.tensor_copy(out=sct[:, 0:1], in_=sct[:, 0:1]).then_inc(SV.h, 1)
                blk.vector(_ctr)
                SV.inc(1)
                barrier(("vector", "scalar", "tensor"))

                attn_ln_stats()
                for g in range(NGD):
                    def _nr2(e, g=g):
                        if g == 0:
                            wait_all(e)
                        else:
                            e.wait_ge(SSm.h, SSm.n)
                        e.tensor_tensor(out=ytmp[:], in0=h[:, g, :], in1=rstdb[:], op=ALU.mult)
                        e.tensor_tensor(out=ytmp[:], in0=ytmp[:], in1=nub[:],
                                        op=ALU.subtract).then_inc(SV.h, 1)
                    blk.vector(_nr2)
                    SV.inc(1)

                    def _aff(e, g=g):
                        e.wait_ge(SV.h, SV.n)
                        e.activation(out=scrbf[:, g, :], in_=ytmp[:], func=AF.Identity,
                                     scale=ln2w[:, g:g + 1], bias=ln2b[:, g:g + 1]).then_inc(SSm.h, 1)
                    blk.scalar(_aff)
                    SSm.inc(1)
                barrier(("vector", "scalar", "tensor", "gpsimd"))

                for tt in range(NTT):
                    ss_f1 = SSm.n

                    def _mmf1(e, tt=tt, ss_f1=ss_f1):
                        if tt == 0:
                            wait_all(e)
                        else:
                            e.wait_ge(SSm.h, ss_f1)
                        for m in range(NGF):
                            bank = 2 * (m % 2)
                            if m >= 2:
                                e.wait_ge(SSm.h, ss_f1 + m - 1)
                            ins = None
                            for k in range(NGD):
                                ins = e.matmul(out=ps[bank][:, :], lhsT=w1b[:, k, m * 128:m * 128 + 128],
                                               rhs=scrbf[:, k, tt * TT:tt * TT + TT],
                                               start=(k == 0), stop=(k == NGD - 1))
                            ins.then_inc(ST.h, 1)
                    blk.tensor(_mmf1)
                    st_f1 = ST.n
                    ST.inc(NGF)

                    def _evf1(e, st_f1=st_f1):
                        for m in range(NGF):
                            bank = 2 * (m % 2)
                            e.wait_ge(ST.h, st_f1 + m + 1)
                            e.activation(out=midt[:, m, :], in_=ps[bank][:, :], func=AF.Silu,
                                         bias=b1v[:, m:m + 1]).then_inc(SSm.h, 1)
                    blk.scalar(_evf1)
                    SSm.inc(NGF)

                    sv_f2 = SV.n

                    def _mmf2(e, sv_f2=sv_f2):
                        e.wait_ge(SSm.h, SSm.n)
                        e.wait_ge(SV.h, sv_f2)
                        for m in range(NGD):
                            bank = 4 + 2 * (m % 2)
                            if m >= 2:
                                e.wait_ge(SV.h, sv_f2 + m - 1)
                            ins = None
                            for k in range(NGF):
                                ins = e.matmul(out=ps[bank][:, :], lhsT=w2b[:, k, m * 128:m * 128 + 128],
                                               rhs=midt[:, k, :], start=(k == 0), stop=(k == NGF - 1))
                            ins.then_inc(ST.h, 1)
                    blk.tensor(_mmf2)
                    st_f2 = ST.n
                    ST.inc(NGD)

                    def _evf2(e, tt=tt, st_f2=st_f2):
                        for m in range(NGD):
                            bank = 4 + 2 * (m % 2)
                            e.wait_ge(ST.h, st_f2 + m + 1)
                            hs = h[:, m, tt * TT:tt * TT + TT]
                            e.scalar_tensor_tensor(out=hs, in0=ps[bank][:, :], scalar=b2v[:, m:m + 1],
                                                   in1=hs, op0=ALU.add, op1=ALU.add).then_inc(SV.h, 1)
                    blk.vector(_evf2)
                    SV.inc(NGD)
                barrier()

                if debug:
                    def _dbg2(e, l=l):
                        wait_all(e)
                        dma_w(e, LD, dbg[f"h{l+1}"], h[:])
                    blk.sync(_dbg2)
                    barrier()

            attn_ln_stats()
            for g in range(NGD):
                def _nr4(e, g=g):
                    if g == 0:
                        wait_all(e)
                    else:
                        e.wait_ge(SSm.h, SSm.n)
                    e.tensor_tensor(out=ytmp[:], in0=h[:, g, :], in1=rstdb[:], op=ALU.mult)
                    e.tensor_tensor(out=ytmp[:], in0=ytmp[:], in1=nub[:],
                                    op=ALU.subtract).then_inc(SV.h, 1)
                blk.vector(_nr4)
                SV.inc(1)

                def _aff4(e, g=g):
                    e.wait_ge(SV.h, SV.n)
                    e.activation(out=scrbf[:, g, :], in_=ytmp[:], func=AF.Identity,
                                 scale=lnlw[:, g:g + 1], bias=lnlb[:, g:g + 1]).then_inc(SSm.h, 1)
                blk.scalar(_aff4)
                SSm.inc(1)
            barrier()

            if debug:
                def _dbg3(e):
                    wait_all(e)
                    dma_w(e, LD, dbg["hn"], scrbf[:])
                blk.sync(_dbg3)
                barrier()

        if stages < 3:
            def _fin2(e):
                wait_all(e)
            for name in ("sync", "scalar", "vector", "gpsimd", "tensor"):
                getattr(blk, name)(_fin2)
            es2.close()
            return nc
        es2.close()
        with (
            nc.sbuf_tensor("xnf", [128, NGV, TOK], BF16) as xnf,
            nc.sbuf_tensor("wtb", [128, 2, NGV, 128], BF16) as wtb,
            nc.sbuf_tensor("woutm", [128, 2, NGD, 128], BF16) as woutm,
            nc.sbuf_tensor("btb", [64, VD], BF16) as btb,
            nc.sbuf_tensor("xm", [128, TOK], F32) as xm,
            nc.sbuf_tensor("outst", [128, 2, TOK], F32) as outst,
            nc.sbuf_tensor("biast", [128, NGV], F32) as biast,
            nc.sbuf_tensor("bias2", [128, NGV], F32) as bias2,
        ):
            def _l3(e):
                for i in range(NGV // 8):
                    dma_w(e, LD, xnf[:, i * 8:(i + 1) * 8, :],
                          dxn[i * 8:(i + 1) * 8].rearrange("g p t -> p g t"))
                dma_w(e, LD, biast[:], dbout)
                dma_w(e, LD, bias2[:], dbff)
            blk.sync(_l3)

            def _l3g(e):
                dma_w(e, GD, btb[:], dbT)
            blk.gpsimd(_l3g)

            def _bsum(e):
                wait_all(e)
                e.tensor_tensor(out=biast[:], in0=biast[:], in1=bias2[:], op=ALU.add).then_inc(SV.h, 1)
            blk.vector(_bsum)
            SV.inc(1)
            barrier()

            st_mm = []
            for m in range(NGV):
                buf = m % 2
                ms = m * 128

                GWp = GW[buf]

                def _lw3(e, m=m, buf=buf, ms=ms, w=(st_mm[m - 2] if m >= 2 else 0), GWp=GWp):
                    if m >= 2:
                        e.wait_ge(ST.h, w)
                    e.dma_start(out=wtb[:, buf],
                                in_=dwff[:, ms:ms + 128].rearrange("(k p) f -> p k f", p=128)
                                ).then_inc(GWp.h, 16)
                    GWp.inc(16)
                    e.dma_start(out=woutm[:, buf],
                                in_=dwout[:, ms:ms + 128].rearrange("(k p) f -> p k f", p=128)
                                ).then_inc(GWp.h, 16)
                    GWp.inc(16)
                blk.gpsimd(_lw3)

                def _lx3(e, m=m, ms=ms):
                    if m >= 1:
                        e.wait_ge(SV.h, SV.n)
                    dma_w(e, LD, xm[:], dxT[ms:ms + 128, :])
                blk.sync(_lx3)

                gw_m = GW[buf].n
                sv_banks = SV.n

                def _mm5(e, m=m, buf=buf, gw_m=gw_m, sv_banks=sv_banks):
                    e.wait_ge(GW[buf].h, gw_m)
                    e.wait_ge(GD.h, GD.n)
                    if m >= 2:
                        e.wait_ge(SV.h, sv_banks - 1)
                    for tt in range(NTT):
                        bank = 4 * buf + tt
                        e.matmul(out=ps[bank][:, :], lhsT=btb[0:64, m * 128:m * 128 + 128],
                                 rhs=umask[0:64, tt * TT:tt * TT + TT], start=True, stop=False)
                        for k in range(NGD):
                            e.matmul(out=ps[bank][:, :], lhsT=woutm[:, buf, k, :],
                                     rhs=scrbf[:, k, tt * TT:tt * TT + TT], start=False, stop=False)
                        for k in range(NGV):
                            ins = e.matmul(out=ps[bank][:, :], lhsT=wtb[:, buf, k, :],
                                           rhs=xnf[:, k, tt * TT:tt * TT + TT], start=False,
                                           stop=(k == NGV - 1))
                    ins.then_inc(ST.h, 1)
                blk.tensor(_mm5)
                ST.inc(1)
                st_mm.append(ST.n)

                st_m = ST.n
                ld_m = LD.n

                def _ev5(e, m=m, buf=buf, st_m=st_m, ld_m=ld_m):
                    e.wait_ge(ST.h, st_m)
                    e.wait_ge(LD.h, ld_m)
                    ins = None
                    for tt in range(NTT):
                        bank = 4 * buf + tt
                        ins = e.scalar_tensor_tensor(out=outst[:, buf, tt * TT:tt * TT + TT],
                                                     in0=ps[bank][:, :], scalar=biast[:, m:m + 1],
                                                     in1=xm[:, tt * TT:tt * TT + TT],
                                                     op0=ALU.add, op1=ALU.add)
                    ins.then_inc(SV.h, 1)
                blk.vector(_ev5)
                SV.inc(1)

                sv_m = SV.n

                def _st5(e, buf=buf, ms=ms, sv_m=sv_m):
                    e.wait_ge(SV.h, sv_m)
                    dma_w(e, LD, dout[ms:ms + 128, :], outst[:, buf, :])
                blk.sync(_st5)

            def _fin(e):
                e.wait_ge(LD.h, LD.n)
            blk.sync(_fin)

            if debug:
                def _dbg4(e):
                    wait_all(e)
                    dma_w(e, LD, dbg["xn"], xnf[:])
                blk.sync(_dbg4)

    return nc


def _patm():
    import ml_dtypes
    p = np.zeros((4, 64), ml_dtypes.bfloat16)
    p[0, 0:8] = 1
    p[1, 8:16] = 1
    p[2, 16:32] = 1
    p[3, 32:64] = 1
    return p


def _prep(inputs):
    f = np.float32
    x = np.asarray(inputs["L0_attn_x"], f)[0]
    xT = np.ascontiguousarray(x.T)
    Win = np.asarray(inputs["L0_attn_Win"], f)
    A = np.asarray(inputs["L0_ffn_A"], f)
    gw = np.asarray(inputs["L0_ffn_gate_w"], f)
    com = {
        "winT": np.ascontiguousarray(Win.T),
        "agT": np.ascontiguousarray(np.concatenate([A.T, gw.T], axis=1)),
        "w1T": np.ascontiguousarray(np.asarray(inputs["L0_attn_W1"], f).transpose(0, 2, 1)),
        "w2T": np.ascontiguousarray(np.asarray(inputs["L0_attn_W2"], f).transpose(0, 2, 1)),
        "woutT": np.ascontiguousarray(np.asarray(inputs["L0_attn_Wout"], f).T),
        "wT": np.ascontiguousarray(np.asarray(inputs["L0_ffn_W"], f).T),
        "bT": np.ascontiguousarray(np.asarray(inputs["L0_ffn_B"], f).T),
        "bin_r": np.ascontiguousarray(np.asarray(inputs["L0_attn_bin"], f).reshape(NGD, 128).T),
        "b1_r": np.ascontiguousarray(np.asarray(inputs["L0_attn_b1"], f).reshape(NL, NGF, 128).transpose(0, 2, 1)),
        "lnlw_r": np.ascontiguousarray(np.asarray(inputs["L0_attn_lnl_w"], f).reshape(NGD, 128).T),
        "lnlb_r": np.ascontiguousarray(np.asarray(inputs["L0_attn_lnl_b"], f).reshape(NGD, 128).T),
        "bout_r": np.ascontiguousarray(np.asarray(inputs["L0_attn_bout"], f).reshape(NGV, 128).T),
        "bff_r": np.ascontiguousarray(np.asarray(inputs["L0_ffn_b"], f).reshape(NGV, 128).T),
        "gateb_r": np.ascontiguousarray(np.asarray(inputs["L0_ffn_gate_b"], f).reshape(1, 4)),
        "patm": _patm(),
    }
    for nm, key in (("ln1w_r", "L0_attn_ln1_w"), ("ln1b_r", "L0_attn_ln1_b"),
                    ("ln2w_r", "L0_attn_ln2_w"), ("ln2b_r", "L0_attn_ln2_b"),
                    ("scl_r", "L0_attn_sc_lin"), ("b2_r", "L0_attn_b2"),
                    ("phzre_r", "L0_attn_phz_re"), ("phzim_r", "L0_attn_phz_im"),
                    ("lcre_r", "L0_attn_lc_re"), ("lcim_r", "L0_attn_lc_im")):
        com[nm] = np.ascontiguousarray(
            np.asarray(inputs[key], f).reshape(NL, NGD, 128).transpose(0, 2, 1))
    in_maps = []
    for c in range(NCORE):
        mp = dict(com)
        mp["xT"] = np.ascontiguousarray(xT[:, c * TOK:(c + 1) * TOK])
        mp["mask8"] = np.ascontiguousarray(
            np.array([1.0] * c + [0.0] * (NCORE - c), f).reshape(1, 8))
        in_maps.append(mp)
    return in_maps


def kernel(**inputs):
    in_maps = _prep(inputs)
    nc = build_nc(debug=DEBUG, stages=STAGES)
    res = run_bass_kernel_spmd(nc, in_maps, core_ids=list(range(NCORE)))
    out = np.concatenate([np.asarray(res.results[c]["outT"]) for c in range(NCORE)], axis=1)
    if DEBUG:
        kernel._last = res
    return np.ascontiguousarray(out.T).reshape(1, 16384, 4096).astype(np.float32)


# version 03
# speedup vs baseline: 1.1349x; 1.0288x over previous
import contextlib
import numpy as np
import concourse.bass as bass
import concourse.mybir as mybir
from concourse.bass_utils import run_bass_kernel_spmd

F32 = mybir.dt.float32
BF16 = mybir.dt.bfloat16
I32 = mybir.dt.int32
AF = mybir.ActivationFunctionType
ALU = mybir.AluOpType

NCORE = 8
TOK = 2048
VD = 4096
DD = 512
FF = 2048
NL = 4
CK = 512
NCH = TOK // CK
NSLAB = 4
SLAB = TOK // NSLAB
NTT = 4
TT = 512
NGV = VD // 128
NGD = DD // 128
NGF = FF // 128
EPS = 1e-5
RSQRT_MAGIC = 1597463007.0

DEBUG = False
STAGES = 3


class VSafe:

    def __init__(self, e, state):
        self._e = e
        self._s = state

    @staticmethod
    def _names(aps):
        out = []
        for a in aps:
            if a is None or isinstance(a, (int, float)):
                continue
            t = getattr(a, "tensor", None)
            if t is None:
                continue
            try:
                off = a.offset
                ext = 1
                for step, cnt in list(a.ap)[1:]:
                    ext += (cnt - 1) * abs(step)
            except Exception:
                off, ext = 0, 1 << 40
            out.append((t.name, off, off + ext))
        return out

    @staticmethod
    def _hit(a, b):
        for n1, s1, e1 in a:
            for n2, s2, e2 in b:
                if n1 == n2 and s1 < e2 and s2 < e1:
                    return True
        return False

    def _do(self, ins, outs, call, scal=()):
        ii = self._names(ins) + self._names(scal)
        oo = self._names(outs)
        for pi, po in self._s["hist"]:
            if self._hit(ii, po) or self._hit(oo, po) or self._hit(oo, pi):
                self._e.drain()
                self._s["hist"] = []
                break
        r = call()
        self._s["hist"].append((ii, oo))
        return r

    def tensor_copy(self, out, in_):
        return self._do([in_], [out], lambda: self._e.tensor_copy(out=out, in_=in_))

    def memset(self, ap, c):
        return self._do([], [ap], lambda: self._e.memset(ap, c))

    def tensor_scalar(self, out, in0, scalar1, scalar2, op0, op1=None):
        kw = {} if op1 is None else {"op1": op1}
        return self._do([in0], [out],
                        lambda: self._e.tensor_scalar(out=out, in0=in0, scalar1=scalar1,
                                                      scalar2=scalar2, op0=op0, **kw),
                        scal=[scalar1, scalar2])

    def tensor_tensor(self, out, in0, in1, op):
        return self._do([in0, in1], [out],
                        lambda: self._e.tensor_tensor(out=out, in0=in0, in1=in1, op=op))

    def scalar_tensor_tensor(self, out, in0, scalar, in1, op0, op1):
        return self._do([in0, in1], [out],
                        lambda: self._e.scalar_tensor_tensor(out=out, in0=in0, scalar=scalar,
                                                             in1=in1, op0=op0, op1=op1),
                        scal=[scalar])

    def tensor_tensor_scan(self, out, data0, data1, initial, op0, op1):
        return self._do([data0, data1], [out],
                        lambda: self._e.tensor_tensor_scan(out=out, data0=data0, data1=data1,
                                                           initial=initial, op0=op0, op1=op1),
                        scal=[initial] if not isinstance(initial, float) else ())

    def reciprocal(self, out, in_):
        return self._do([in_], [out], lambda: self._e.reciprocal(out=out, in_=in_))

    def __getattr__(self, name):
        return getattr(self._e, name)


class Sem:
    def __init__(self, es, nc, name):
        self.h = es.enter_context(nc.semaphore(name))
        self.n = 0

    def inc(self, k=1):
        self.n += k
        return self.n


def _rsqrt(e, out, in_, tmp_i, tmp_a):
    n = in_.shape[-1]
    q = max(1, n // 4)
    sl = []
    lo = 0
    while lo < n:
        hi = min(n, lo + q)
        sl.append((lo, hi))
        lo = hi
    def S(ap, s):
        return ap[..., s[0]:s[1]]
    for s in sl:
        e.tensor_copy(out=S(tmp_a, s), in_=S(in_, s).bitcast(I32))
    for s in sl:
        e.tensor_scalar(out=S(tmp_a, s), in0=S(tmp_a, s), scalar1=-0.5 * 2.0 ** -8,
                        scalar2=RSQRT_MAGIC * 2.0 ** -8, op0=ALU.mult, op1=ALU.add)
    for s in sl:
        e.tensor_copy(out=S(tmp_i, s), in_=S(tmp_a, s))
    for s in sl:
        e.tensor_scalar(out=S(tmp_i, s), in0=S(tmp_i, s), scalar1=8, scalar2=None,
                        op0=ALU.arith_shift_left)
    for s in sl:
        e.tensor_copy(out=S(out, s), in_=S(tmp_i, s).bitcast(F32))
    for _ in range(3):
        for s in sl:
            e.tensor_tensor(out=S(tmp_a, s), in0=S(out, s), in1=S(out, s), op=ALU.mult)
        for s in sl:
            e.tensor_tensor(out=S(tmp_a, s), in0=S(tmp_a, s), in1=S(in_, s), op=ALU.mult)
        for s in sl:
            e.tensor_scalar(out=S(tmp_a, s), in0=S(tmp_a, s), scalar1=-0.5, scalar2=1.5,
                            op0=ALU.mult, op1=ALU.add)
        for s in sl:
            e.tensor_tensor(out=S(out, s), in0=S(out, s), in1=S(tmp_a, s), op=ALU.mult)


def build_nc(debug=False, stages=3):
    nc = bass.Bass("TRN2", target_bir_lowering=False)

    dxT = nc.dram_tensor("xT", [VD, TOK], F32, kind="ExternalInput").ap()
    dwin = nc.dram_tensor("winT", [VD, DD], F32, kind="ExternalInput").ap()
    dag = nc.dram_tensor("agT", [VD, 68], F32, kind="ExternalInput").ap()
    dw1 = nc.dram_tensor("w1T", [NL, DD, FF], F32, kind="ExternalInput").ap()
    dw2 = nc.dram_tensor("w2T", [NL, FF, DD], F32, kind="ExternalInput").ap()
    dwout = nc.dram_tensor("woutT", [DD, VD], F32, kind="ExternalInput").ap()
    dwff = nc.dram_tensor("wT", [VD, VD], F32, kind="ExternalInput").ap()
    dbT = nc.dram_tensor("bT", [64, VD], F32, kind="ExternalInput").ap()
    dbin = nc.dram_tensor("bin_r", [128, NGD], F32, kind="ExternalInput").ap()
    dln1w = nc.dram_tensor("ln1w_r", [NL, 128, NGD], F32, kind="ExternalInput").ap()
    dln1b = nc.dram_tensor("ln1b_r", [NL, 128, NGD], F32, kind="ExternalInput").ap()
    dln2w = nc.dram_tensor("ln2w_r", [NL, 128, NGD], F32, kind="ExternalInput").ap()
    dln2b = nc.dram_tensor("ln2b_r", [NL, 128, NGD], F32, kind="ExternalInput").ap()
    dscl = nc.dram_tensor("scl_r", [NL, 128, NGD], F32, kind="ExternalInput").ap()
    db2 = nc.dram_tensor("b2_r", [NL, 128, NGD], F32, kind="ExternalInput").ap()
    dphr = nc.dram_tensor("phzre_r", [NL, 128, NGD], F32, kind="ExternalInput").ap()
    dphi = nc.dram_tensor("phzim_r", [NL, 128, NGD], F32, kind="ExternalInput").ap()
    dlcr = nc.dram_tensor("lcre_r", [NL, 128, NGD], F32, kind="ExternalInput").ap()
    dlci = nc.dram_tensor("lcim_r", [NL, 128, NGD], F32, kind="ExternalInput").ap()
    db1 = nc.dram_tensor("b1_r", [NL, 128, NGF], F32, kind="ExternalInput").ap()
    dlnlw = nc.dram_tensor("lnlw_r", [128, NGD], F32, kind="ExternalInput").ap()
    dlnlb = nc.dram_tensor("lnlb_r", [128, NGD], F32, kind="ExternalInput").ap()
    dbout = nc.dram_tensor("bout_r", [128, NGV], F32, kind="ExternalInput").ap()
    dbff = nc.dram_tensor("bff_r", [128, NGV], F32, kind="ExternalInput").ap()
    dgateb = nc.dram_tensor("gateb_r", [1, 4], F32, kind="ExternalInput").ap()
    dmask8 = nc.dram_tensor("mask8", [1, 8], F32, kind="ExternalInput").ap()
    dpat = nc.dram_tensor("patm", [4, 64], BF16, kind="ExternalInput").ap()
    dout = nc.dram_tensor("outT", [VD, TOK], F32, kind="ExternalOutput").ap()

    dxn = nc.dram_tensor("xn_spill", [NGV, 128, TOK], BF16).ap()
    dlni = nc.dram_tensor("lni", [2 * TOK], F32).ap()
    dlno = nc.dram_tensor("lno", [2 * TOK], F32).ap()
    dgsc = nc.dram_tensor("gsc", [4, TOK], F32).ap()
    dgm = nc.dram_tensor("gm", [4, TOK], BF16).ap()
    hcc_i = [nc.dram_tensor(f"cc_i{l}", [128, 8], F32) for l in range(NL)]
    hcc_o = [nc.dram_tensor(f"cc_o{l}", [128 * 8, 8], F32) for l in range(NL)]

    dbg = {}
    if debug:
        dbg["h0"] = nc.dram_tensor("dbg_h0", [128, NGD, TOK], F32, kind="ExternalOutput").ap()
        dbg["xn"] = nc.dram_tensor("dbg_xn", [128, NGV, TOK], BF16, kind="ExternalOutput").ap()
        for l in range(NL):
            dbg[f"h{l+1}"] = nc.dram_tensor(f"dbg_h{l+1}", [128, NGD, TOK], F32, kind="ExternalOutput").ap()
        dbg["hn"] = nc.dram_tensor("dbg_hn", [128, NGD, TOK], BF16, kind="ExternalOutput").ap()
        dbg["um"] = nc.dram_tensor("dbg_um", [64, TOK], BF16, kind="ExternalOutput").ap()

    es = contextlib.ExitStack()
    with es:
        sb = lambda name, shape, dt: es.enter_context(nc.sbuf_tensor(name, shape, dt))
        es2 = contextlib.ExitStack()
        sbm = lambda name, shape, dt: es2.enter_context(nc.sbuf_tensor(name, shape, dt))
        scrbf = sb("scrbf", [128, NGD, TOK], BF16)
        umask = sb("umask", [64, TOK], BF16)
        onesb = sb("onesb", [128, 1], BF16)
        gbb = sb("gbb", [128, 4], F32)
        m8b = sb("m8b", [128, 8], F32)
        h = sbm("h", [128, NGD, TOK], F32)
        onesf = sbm("onesf", [128, CK], F32)
        binv = sbm("binv", [128, NGD], F32)
        lnlw = sbm("lnlw", [128, NGD], F32)
        lnlb = sbm("lnlb", [128, NGD], F32)
        ln1w = sbm("ln1w", [128, NGD], F32)
        ln1b = sbm("ln1b", [128, NGD], F32)
        ln2w = sbm("ln2w", [128, NGD], F32)
        ln2b = sbm("ln2b", [128, NGD], F32)
        sclv = sbm("sclv", [128, NGD], F32)
        b2v = sbm("b2v", [128, NGD], F32)
        b1v = sbm("b1v", [128, NGF], F32)
        phr = sbm("phr", [128, NGD], F32)
        phi = sbm("phi", [128, NGD], F32)
        rho4 = sbm("rho4", [128, NGD], F32)
        ure4 = sbm("ure4", [128, NGD], F32)
        uim4 = sbm("uim4", [128, NGD], F32)
        q4 = sbm("q4", [128, NGD, 2], F32)
        u5 = sbm("u5", [128, NGD, 2], F32)
        p2k = sbm("p2k", [128, NGD, 2], F32)
        pm1 = sbm("pm1", [128, NGD], F32)
        qpow = sbm("qpow", [128, NCH, NGD, 2], F32)
        iot = sbm("iot", [128, NGD, NCH, 2], F32)
        iosc = sbm("iosc", [128, NGD, NCH, 2], F32)
        Ev = sbm("Ev", [128, NGD, 2], F32)
        gat = sbm("gat", [128, 8, NGD, 2], F32)
        Hre = sbm("Hre", [128, NGD], F32)
        Him = sbm("Him", [128, NGD], F32)
        tA = sbm("tA", [128, NGD], F32)
        tB = sbm("tB", [128, NGD], F32)
        tC = sbm("tC", [128, NGD], F32)
        tD = sbm("tD", [128, NGD, 2], F32)
        tE = sbm("tE", [128, NGD, 2], F32)
        scs = sbm("scs", [128, 2], F32)
        sct = sbm("sct", [128, 4], F32)
        scs4 = sbm("scs4", [128, NGD, 2], F32)
        sct4 = sbm("sct4", [128, NGD, 4], F32)
        ends4 = sbm("ends4", [128, NGD, NCH, 2], F32)
        tF = sbm("tF", [128, NGD, 2], F32)
        statin = sbm("statin", [1, 2 * TOK], F32)
        statq = sbm("statq", [128, 2, TOK // 128], F32)
        stato = sbm("stato", [128, 2, TOK // 128], F32)
        stati = sbm("stati", [128, TOK // 128], I32)
        stata = sbm("stata", [128, TOK // 128], F32)
        rstdb = sbm("rstdb", [128, TOK], F32)
        nub = sbm("nub", [128, TOK], F32)
        hbt = sbm("hbt", [128, 2, TOK], BF16)
        h2t = sbm("h2t", [128, 2, TOK], BF16)
        ytmp = sbm("ytmp", [128, TOK], F32)
        ps = [es.enter_context(nc.psum_tensor(f"ps{i}", [128, TT], F32)) for i in range(8)]

        LD = Sem(es, nc, "LD")
        GD = Sem(es, nc, "GD")
        SV = Sem(es, nc, "SV")
        SSm = Sem(es, nc, "SS")
        ST = Sem(es, nc, "ST")
        SG = Sem(es, nc, "SG")
        CC = Sem(es, nc, "CC")
        GW = [Sem(es, nc, "GW0"), Sem(es, nc, "GW1")]

        blk = es.enter_context(nc.Block())
        _vstate = {"hist": []}

        class _BlkWrap:
            def __init__(self, b):
                self._b = b

            def vector(self, f):
                return self._b.vector(lambda e: f(VSafe(e, _vstate)))

            def __getattr__(self, name):
                return getattr(self._b, name)

        blk = _BlkWrap(blk)

        def dma_w(e, S, out, in_):
            e.dma_start(out=out, in_=in_).then_inc(S.h, 16)
            e.wait_ge(S.h, S.inc(16))

        def wait_all(e):
            for s in (LD, GD, SV, SSm, ST, SG, CC, GW[0], GW[1]):
                if s.n:
                    e.wait_ge(s.h, s.n)

        OWN = {"sync": LD, "scalar": SSm, "vector": SV, "gpsimd": SG, "tensor": ST}

        def barrier(engines=("sync", "scalar", "vector", "gpsimd", "tensor")):
            for name in engines:
                def _g(e):
                    wait_all(e)
                getattr(blk, name)(_g)

        def _p0_sync(e):
            for dst, srcap in ((binv, dbin), (lnlw, dlnlw), (lnlb, dlnlb)):
                dma_w(e, LD, dst[:], srcap)
            dma_w(e, LD, gbb[:], dgateb.broadcast_to([128, 4]))
            dma_w(e, LD, m8b[:], dmask8.broadcast_to([128, 8]))
        blk.sync(_p0_sync)

        def _p0_scal(e):
            e.preload_act_table(AF.Silu)
        blk.scalar(_p0_scal)

        def _p0_vec(e):
            e.memset(onesb[:], 1.0)
            e.memset(onesf[:], 1.0).then_inc(SV.h, 1)
        blk.vector(_p0_vec)
        SV.inc(1)

        def ln_finish(width, inv_n, n_tiles, sig_banks, sq_banks):
            kq = width // 128

            def _evac(e):
                wait_all(e)
                for i in range(n_tiles):
                    e.activation(out=statin[0:1, i * TT:i * TT + TT], in_=sig_banks[i], func=AF.Identity)
                for i in range(n_tiles - 1):
                    e.activation(out=statin[0:1, width + i * TT:width + i * TT + TT],
                                 in_=sq_banks[i], func=AF.Identity)
                i = n_tiles - 1
                e.activation(out=statin[0:1, width + i * TT:width + i * TT + TT],
                             in_=sq_banks[i], func=AF.Identity).then_inc(SSm.h, 1)
            blk.scalar(_evac)
            SSm.inc(1)

            def _bounce1(e):
                e.wait_ge(SSm.h, SSm.n)
                dma_w(e, LD, dlni[0:2 * width].unsqueeze(0), statin[0:1, 0:2 * width])
                dma_w(e, LD, statq[:, :, 0:kq],
                      dlni[0:2 * width].rearrange("(a p k) -> p a k", p=128, a=2))
            blk.sync(_bounce1)

            def _finish(e):
                e.wait_ge(LD.h, LD.n)
                mu = stato[:, 0, 0:kq]
                var = stato[:, 1, 0:kq]
                e.tensor_scalar(out=mu, in0=statq[:, 0, 0:kq], scalar1=inv_n, scalar2=None, op0=ALU.mult)
                e.tensor_scalar(out=var, in0=statq[:, 1, 0:kq], scalar1=inv_n, scalar2=EPS,
                                op0=ALU.mult, op1=ALU.add)
                e.tensor_tensor(out=stata[:, 0:kq], in0=mu, in1=mu, op=ALU.mult)
                e.tensor_tensor(out=var, in0=var, in1=stata[:, 0:kq], op=ALU.subtract)
                _rsqrt(e, statq[:, 0, 0:kq], var, stati[:, 0:kq], stata[:, 0:kq])
                e.tensor_tensor(out=statq[:, 1, 0:kq], in0=mu, in1=statq[:, 0, 0:kq],
                                op=ALU.mult).then_inc(SV.h, 1)
            blk.vector(_finish)
            SV.inc(1)

            def _bounce2(e):
                e.wait_ge(SV.h, SV.n)
                dma_w(e, LD, dlno[0:2 * width].rearrange("(a p k) -> p a k", p=128, a=2),
                      statq[:, :, 0:kq])
                dma_w(e, LD, rstdb[:, 0:width],
                      dlno[0:width].unsqueeze(0).broadcast_to([128, width]))
                dma_w(e, LD, nub[:, 0:width],
                      dlno[width:2 * width].unsqueeze(0).broadcast_to([128, width]))
            blk.sync(_bounce2)

        def attn_ln_stats():
            st0 = ST.n
            ss0 = SSm.n
            sv0 = SV.n

            def _cst(e, st0=st0):
                wait_all(e)
                for g in range(NGD):
                    if g >= 2:
                        e.wait_ge(ST.h, st0 + g - 1)
                    e.tensor_copy(out=hbt[:, g % 2, :], in_=h[:, g, :]).then_inc(SV.h, 1)
            blk.vector(_cst)
            SV.inc(NGD)

            def _sq(e, sv0=sv0, st0=st0):
                for g in range(NGD):
                    e.wait_ge(SV.h, sv0 + g + 1)
                    if g >= 2:
                        e.wait_ge(ST.h, st0 + g - 1)
                    e.activation(out=h2t[:, g % 2, :], in_=hbt[:, g % 2, :],
                                 func=AF.Square).then_inc(SSm.h, 1)
            blk.scalar(_sq)
            SSm.inc(NGD)

            def _mm(e, ss0=ss0):
                e.wait_ge(SV.h, 1)
                for g in range(NGD):
                    e.wait_ge(SSm.h, ss0 + g + 1)
                    ins = None
                    for tt in range(NTT):
                        e.matmul(out=ps[tt][0:1, :], lhsT=onesb[:],
                                 rhs=hbt[:, g % 2, tt * TT:tt * TT + TT],
                                 start=(g == 0), stop=(g == NGD - 1))
                        ins = e.matmul(out=ps[4 + tt][0:1, :], lhsT=onesb[:],
                                       rhs=h2t[:, g % 2, tt * TT:tt * TT + TT],
                                       start=(g == 0), stop=(g == NGD - 1))
                    ins.then_inc(ST.h, 1)
            blk.tensor(_mm)
            ST.inc(NGD)
            ln_finish(TOK, 1.0 / DD, NTT, [ps[tt][0:1, :] for tt in range(NTT)],
                      [ps[4 + tt][0:1, :] for tt in range(NTT)])
            barrier(("vector", "scalar", "sync"))

        with (
            nc.sbuf_tensor("xslab", [128, NGV, SLAB], BF16) as xslab,
            nc.sbuf_tensor("x2t", [128, 2, SLAB], BF16) as x2t,
            nc.sbuf_tensor("winb", [128, NGV, DD], BF16) as winb,
            nc.sbuf_tensor("agb", [128, NGV, 68], BF16) as agb,
            nc.sbuf_tensor("usc", [128, SLAB], F32) as usc,
            nc.sbuf_tensor("mrows", [4, TOK], BF16) as mrows,
            nc.sbuf_tensor("scm", [128, 4, TOK // 128], F32) as scm,
            nc.sbuf_tensor("sc2", [128, 2, TOK // 128], F32) as sc2,
            nc.sbuf_tensor("mpk", [128, 4, TOK // 128], BF16) as mpk,
            nc.sbuf_tensor("pat", [4, 64], BF16) as pat,
        ):
            def _w1(e):
                dma_w(e, GD, winb[:], dwin.rearrange("(g p) d -> p g d", p=128))
                dma_w(e, GD, agb[:], dag.rearrange("(g p) d -> p g d", p=128))
            blk.gpsimd(_w1)

            for s in range(NSLAB):
                t0 = s * SLAB

                def _lx(e, t0=t0):
                    dma_w(e, GD, xslab[:],
                          dxT[:, t0:t0 + SLAB].rearrange("(g p) t -> p g t", p=128))
                blk.gpsimd(_lx)
                gd_x = GD.n

                st0 = ST.n
                sq0 = SSm.n

                def _sq1(e, gd_x=gd_x, st0=st0):
                    e.wait_ge(GD.h, gd_x)
                    e.wait_ge(SV.h, 1)
                    for g in range(NGV):
                        if g >= 2:
                            e.wait_ge(ST.h, st0 + g - 1)
                        e.activation(out=x2t[:, g % 2, :], in_=xslab[:, g, :],
                                     func=AF.Square).then_inc(SSm.h, 1)
                blk.scalar(_sq1)
                SSm.inc(NGV)

                def _mms(e, gd_x=gd_x, sq0=sq0):
                    e.wait_ge(GD.h, gd_x)
                    e.wait_ge(SV.h, 1)
                    for g in range(NGV):
                        e.wait_ge(SSm.h, sq0 + g + 1)
                        e.matmul(out=ps[0][0:1, :], lhsT=onesb[:], rhs=xslab[:, g, :],
                                 start=(g == 0), stop=(g == NGV - 1))
                        e.matmul(out=ps[1][0:1, :], lhsT=onesb[:], rhs=x2t[:, g % 2, :],
                                 start=(g == 0), stop=(g == NGV - 1)).then_inc(ST.h, 1)
                blk.tensor(_mms)
                ST.inc(NGV)

                ln_finish(SLAB, 1.0 / VD, 1, [ps[0][0:1, :]], [ps[1][0:1, :]])

                def _xn(e):
                    wait_all(e)
                    ins = None
                    for g4 in range(0, NGV, 4):
                        for j in range(4):
                            e.tensor_tensor(out=ytmp[:, j * SLAB:(j + 1) * SLAB],
                                            in0=xslab[:, g4 + j, :],
                                            in1=rstdb[:, 0:SLAB], op=ALU.mult)
                        for j in range(4):
                            ins = e.tensor_tensor(out=xslab[:, g4 + j, :],
                                                  in0=ytmp[:, j * SLAB:(j + 1) * SLAB],
                                                  in1=nub[:, 0:SLAB], op=ALU.subtract)
                    ins.then_inc(SV.h, 1)
                blk.vector(_xn)
                SV.inc(1)

                def _spill(e, t0=t0):
                    e.wait_ge(SV.h, SV.n)
                    dma_w(e, LD, dxn[:, :, t0:t0 + SLAB].rearrange("g p t -> p g t"), xslab[:])
                blk.sync(_spill)

                def _mmw(e):
                    e.wait_ge(SV.h, SV.n)
                    e.wait_ge(GD.h, GD.n)
                    e.wait_ge(SSm.h, SSm.n)
                    for g in range(NGV):
                        for m in range(NGD):
                            e.matmul(out=ps[2 + m][:, :], lhsT=winb[:, g, m * 128:m * 128 + 128],
                                     rhs=xslab[:, g, :], start=(g == 0), stop=(g == NGV - 1))
                        ins = e.matmul(out=ps[6][0:68, :], lhsT=agb[:, g, :], rhs=xslab[:, g, :],
                                       start=(g == 0), stop=(g == NGV - 1))
                    ins.then_inc(ST.h, 1)
                blk.tensor(_mmw)
                ST.inc(1)

                def _evh(e, t0=t0):
                    e.wait_ge(ST.h, ST.n)
                    for m in range(NGD):
                        e.activation(out=h[:, m, t0:t0 + SLAB], in_=ps[2 + m][:, :],
                                     func=AF.Identity, bias=binv[:, m:m + 1])
                    e.activation(out=umask[0:64, t0:t0 + SLAB], in_=ps[6][0:64, :], func=AF.Identity)
                    e.activation(out=usc[64:68, :], in_=ps[6][64:68, :],
                                 func=AF.Identity).then_inc(SSm.h, 1)
                blk.scalar(_evh)
                SSm.inc(1)

                def _ssc(e, t0=t0):
                    e.wait_ge(SSm.h, SSm.n)
                    dma_w(e, LD, dgsc[:, t0:t0 + SLAB], usc[64:68, :])
                blk.sync(_ssc)
                barrier()

            def _mld(e):
                wait_all(e)
                dma_w(e, LD, pat[:], dpat)
                for r in range(4):
                    dma_w(e, LD, scm[:, r, :], dgsc[r].rearrange("(p k) -> p k", p=128))
            blk.sync(_mld)

            def _mcalc(e):
                e.wait_ge(LD.h, LD.n)
                KQ = TOK // 128
                e.tensor_tensor(out=scm[:], in0=scm[:],
                                in1=gbb[:].unsqueeze(2).broadcast_to([128, 4, KQ]), op=ALU.add)
                e.tensor_tensor(out=sc2[:, 0, :], in0=scm[:, 0, :], in1=scm[:, 1, :], op=ALU.max)
                e.tensor_tensor(out=sc2[:, 1, :], in0=scm[:, 2, :], in1=scm[:, 3, :], op=ALU.max)
                e.tensor_tensor(out=sc2[:, 0, :], in0=sc2[:, 0, :], in1=sc2[:, 1, :], op=ALU.max)
                mx = sc2[:, 0, :]
                e.memset(mpk[:, 0, :], 1.0)
                e.tensor_tensor(out=sc2[:, 1, :], in0=scm[:, 0, :], in1=mx, op=ALU.is_lt)
                e.tensor_copy(out=mpk[:, 1, :], in_=sc2[:, 1, :])
                e.tensor_tensor(out=scm[:, 1, :], in0=scm[:, 1, :], in1=mx, op=ALU.is_lt)
                e.tensor_tensor(out=sc2[:, 1, :], in0=sc2[:, 1, :], in1=scm[:, 1, :], op=ALU.mult)
                e.tensor_copy(out=mpk[:, 2, :], in_=sc2[:, 1, :])
                e.tensor_tensor(out=scm[:, 2, :], in0=scm[:, 2, :], in1=mx, op=ALU.is_lt)
                e.tensor_tensor(out=sc2[:, 1, :], in0=sc2[:, 1, :], in1=scm[:, 2, :], op=ALU.mult)
                e.tensor_copy(out=mpk[:, 3, :], in_=sc2[:, 1, :]).then_inc(SV.h, 1)
            blk.vector(_mcalc)
            SV.inc(1)

            def _mst(e):
                e.wait_ge(SV.h, SV.n)
                for r in range(4):
                    dma_w(e, LD, dgm[r].rearrange("(p k) -> p k", p=128), mpk[:, r, :])
                dma_w(e, LD, mrows[:], dgm)
            blk.sync(_mst)

            for tt in range(NTT):
                sv_mm = SV.n

                def _mmm(e, tt=tt, sv_mm=sv_mm):
                    e.wait_ge(LD.h, LD.n)
                    e.wait_ge(SV.h, sv_mm)
                    e.matmul(out=ps[7][0:64, :], lhsT=pat[:], rhs=mrows[:, tt * TT:tt * TT + TT],
                             start=True, stop=True).then_inc(ST.h, 1)
                blk.tensor(_mmm)
                ST.inc(1)

                def _umul(e, tt=tt):
                    e.wait_ge(ST.h, ST.n)
                    e.tensor_tensor(out=umask[0:64, tt * TT:tt * TT + TT],
                                    in0=umask[0:64, tt * TT:tt * TT + TT],
                                    in1=ps[7][0:64, :], op=ALU.mult).then_inc(SV.h, 1)
                blk.vector(_umul)
                SV.inc(1)
            barrier()

            if debug:
                def _dbg1(e):
                    wait_all(e)
                    dma_w(e, LD, dbg["h0"], h[:])
                    dma_w(e, LD, dbg["um"], umask[:])
                blk.sync(_dbg1)
                barrier()

        if stages < 2:
            def _fin1(e):
                wait_all(e)
            for name in ("sync", "scalar", "vector", "gpsimd", "tensor"):
                getattr(blk, name)(_fin1)
            es2.close()
            return nc
        with (
            nc.sbuf_tensor("dw_", [128, NCH, CK, 2], F32) as dwv,
            nc.sbuf_tensor("midt", [128, NGF, TT], BF16) as midt,
            nc.sbuf_tensor("rint", [128, NGD, CK, 2], F32) as rint,
            nc.sbuf_tensor("dk4", [128, NGD, CK], F32) as dk4,
            nc.sbuf_tensor("w1b", [128, NGD, FF], BF16) as w1b,
            nc.sbuf_tensor("w2b", [128, NGF, DD], BF16) as w2b,
            nc.sbuf_tensor("rhot", [128, CK], F32) as rhot,
        ):
            for l in range(NL):
                def _lc(e, l=l):
                    for dst, srcap in ((ln1w, dln1w), (ln1b, dln1b), (ln2w, dln2w), (ln2b, dln2b),
                                       (sclv, dscl), (b2v, db2), (phr, dphr), (phi, dphi)):
                        dma_w(e, LD, dst[:], srcap[l])
                    dma_w(e, LD, b1v[:], db1[l])
                    dma_w(e, LD, Hre[:], dlcr[l])
                    dma_w(e, LD, Him[:], dlci[l])
                blk.sync(_lc)

                def _lw(e, l=l):
                    dma_w(e, GD, w1b[:], dw1[l].rearrange("(k p) f -> p k f", p=128))
                    dma_w(e, GD, w2b[:], dw2[l].rearrange("(k p) f -> p k f", p=128))
                blk.gpsimd(_lw)

                def _pc(e, l=l):
                    wait_all(e)
                    e.tensor_tensor(out=tA[:], in0=phr[:], in1=phr[:], op=ALU.mult)
                    e.tensor_tensor(out=tB[:], in0=phi[:], in1=phi[:], op=ALU.mult)
                    e.tensor_tensor(out=tA[:], in0=tA[:], in1=tB[:], op=ALU.add)
                    _rsqrt(e, tB[:], tA[:], stati[:, 0:NGD], sct[:])
                    e.tensor_tensor(out=tC[:], in0=tA[:], in1=tB[:], op=ALU.mult)
                    e.tensor_tensor(out=ure4[:], in0=phr[:], in1=tB[:], op=ALU.mult)
                    e.tensor_tensor(out=uim4[:], in0=phi[:], in1=tB[:], op=ALU.mult).then_inc(SV.h, 1)
                blk.vector(_pc)
                SV.inc(1)

                def _tanh(e):
                    e.wait_ge(SV.h, SV.n)
                    e.activation(out=rho4[:], in_=tC[:], func=AF.Tanh).then_inc(SSm.h, 1)
                blk.scalar(_tanh)
                SSm.inc(1)

                def _tbl(e):
                    e.wait_ge(SSm.h, SSm.n)
                    for g in range(NGD):
                        ri = rint[:, g]
                        e.memset(ri[:, 0:1, 0:1], 1.0)
                        e.memset(ri[:, 0:1, 1:2], 0.0)
                        e.tensor_copy(out=dk4[:, g, 0:1], in_=rho4[:, g:g + 1])
                        e.tensor_copy(out=scs4[:, g, 0:1], in_=ure4[:, g:g + 1])
                        e.tensor_copy(out=scs4[:, g, 1:2], in_=uim4[:, g:g + 1])
                    n = 1
                    while n < CK:
                        m = min(n, CK - n)
                        for g in range(NGD):
                            e.tensor_scalar(out=ytmp[:, g * 512:g * 512 + m], in0=rint[:, g, 0:m, 1],
                                            scalar1=scs4[:, g, 1:2], scalar2=None, op0=ALU.mult)
                        for g in range(NGD):
                            e.scalar_tensor_tensor(out=rint[:, g, n:n + m, 0], in0=rint[:, g, 0:m, 0],
                                                   scalar=scs4[:, g, 0:1], in1=ytmp[:, g * 512:g * 512 + m],
                                                   op0=ALU.mult, op1=ALU.add)
                        for g in range(NGD):
                            e.tensor_scalar(out=ytmp[:, g * 512:g * 512 + m], in0=rint[:, g, 0:m, 0],
                                            scalar1=scs4[:, g, 1:2], scalar2=None, op0=ALU.mult)
                        for g in range(NGD):
                            e.scalar_tensor_tensor(out=rint[:, g, n:n + m, 1], in0=rint[:, g, 0:m, 1],
                                                   scalar=scs4[:, g, 0:1], in1=ytmp[:, g * 512:g * 512 + m],
                                                   op0=ALU.mult, op1=ALU.subtract)
                        for g in range(NGD):
                            e.tensor_scalar(out=dk4[:, g, n:n + m], in0=dk4[:, g, 0:m],
                                            scalar1=dk4[:, g, n - 1:n], scalar2=None, op0=ALU.mult)
                        for g in range(NGD):
                            e.tensor_tensor(out=sct4[:, g, 0:2], in0=scs4[:, g, 0:2],
                                            in1=scs4[:, g, 0:2], op=ALU.mult)
                        for g in range(NGD):
                            e.tensor_tensor(out=sct4[:, g, 2:3], in0=scs4[:, g, 0:1],
                                            in1=scs4[:, g, 1:2], op=ALU.mult)
                        for g in range(NGD):
                            e.tensor_tensor(out=scs4[:, g, 0:1], in0=sct4[:, g, 0:1],
                                            in1=sct4[:, g, 1:2], op=ALU.subtract)
                        for g in range(NGD):
                            e.tensor_scalar(out=scs4[:, g, 1:2], in0=sct4[:, g, 2:3], scalar1=2.0,
                                            scalar2=None, op0=ALU.mult)
                        n *= 2
                    e.tensor_copy(out=u5[:], in_=scs4[:])
                    e.tensor_tensor(out=q4[:, :, 0], in0=scs4[:, :, 0],
                                    in1=dk4[:, :, CK - 1], op=ALU.mult)
                    e.tensor_tensor(out=q4[:, :, 1], in0=scs4[:, :, 1],
                                    in1=dk4[:, :, CK - 1], op=ALU.mult)

                    def csq(dst, src_):
                        e.tensor_tensor(out=tD[:, :, 0], in0=src_[:, :, 0], in1=src_[:, :, 0], op=ALU.mult)
                        e.tensor_tensor(out=tA[:], in0=src_[:, :, 1], in1=src_[:, :, 1], op=ALU.mult)
                        e.tensor_tensor(out=tD[:, :, 1], in0=src_[:, :, 0], in1=src_[:, :, 1], op=ALU.mult)
                        e.tensor_tensor(out=dst[:, :, 0], in0=tD[:, :, 0], in1=tA[:], op=ALU.subtract)
                        e.tensor_scalar(out=dst[:, :, 1], in0=tD[:, :, 1], scalar1=2.0, scalar2=None,
                                        op0=ALU.mult)
                    csq(tE, q4)
                    csq(p2k, tE)
                    e.tensor_scalar(out=pm1[:], in0=p2k[:, :, 0], scalar1=1.0, scalar2=-1.0,
                                    op0=ALU.mult, op1=ALU.add)
                    e.memset(qpow[:, 0, :, 0], 1.0)
                    e.memset(qpow[:, 0, :, 1], 0.0)
                    for k in range(1, NCH):
                        a = qpow[:, k - 1]
                        e.tensor_tensor(out=tD[:, :, 0], in0=a[:, :, 0], in1=q4[:, :, 0], op=ALU.mult)
                        e.tensor_tensor(out=tA[:], in0=a[:, :, 1], in1=q4[:, :, 1], op=ALU.mult)
                        e.tensor_tensor(out=tD[:, :, 1], in0=a[:, :, 0], in1=q4[:, :, 1], op=ALU.mult)
                        e.tensor_tensor(out=tB[:], in0=a[:, :, 1], in1=q4[:, :, 0], op=ALU.mult)
                        e.tensor_tensor(out=qpow[:, k, :, 0], in0=tD[:, :, 0], in1=tA[:], op=ALU.subtract)
                        e.tensor_tensor(out=qpow[:, k, :, 1], in0=tD[:, :, 1], in1=tB[:], op=ALU.add)
                    e.tensor_copy(out=sct[:, 0:1], in_=sct[:, 0:1]).then_inc(SV.h, 1)
                blk.vector(_tbl)
                SV.inc(1)
                barrier(("vector", "scalar", "tensor"))

                attn_ln_stats()

                for g in range(NGD):
                    def _nrm(e, g=g):
                        if g == 0:
                            wait_all(e)
                        else:
                            e.wait_ge(SSm.h, SSm.n)
                        e.tensor_tensor(out=ytmp[:], in0=h[:, g, :], in1=rstdb[:], op=ALU.mult)
                        e.tensor_tensor(out=ytmp[:], in0=ytmp[:], in1=nub[:],
                                        op=ALU.subtract).then_inc(SV.h, 1)
                    blk.vector(_nrm)
                    SV.inc(1)

                    def _silu(e, g=g):
                        e.wait_ge(SV.h, SV.n)
                        e.activation(out=ytmp[:], in_=ytmp[:], func=AF.Silu,
                                     scale=ln1w[:, g:g + 1], bias=ln1b[:, g:g + 1]).then_inc(SSm.h, 1)
                    blk.scalar(_silu)
                    SSm.inc(1)

                    def _scan(e, g=g):
                        e.wait_ge(SSm.h, SSm.n)
                        e.tensor_tensor(
                            out=dwv[:],
                            in0=ytmp[:].rearrange("p (c t) -> p c t", c=NCH).unsqueeze(3).broadcast_to([128, NCH, CK, 2]),
                            in1=rint[:, g].unsqueeze(1).broadcast_to([128, NCH, CK, 2]),
                            op=ALU.mult)
                        e.tensor_scalar(out=rhot[:], in0=onesf[:], scalar1=rho4[:, g:g + 1],
                                        scalar2=None, op0=ALU.mult)
                        for comp in range(2):
                            for c in range(NCH):
                                e.tensor_tensor_scan(out=dwv[:, c, :, comp], data0=rhot[:],
                                                     data1=dwv[:, c, :, comp], initial=0.0,
                                                     op0=ALU.mult, op1=ALU.add)
                        e.tensor_copy(out=ends4[:, g], in_=dwv[:, :, CK - 1, :])
                        e.tensor_tensor(out=dwv[:], in0=dwv[:],
                                        in1=rint[:, g].unsqueeze(1).broadcast_to([128, NCH, CK, 2]),
                                        op=ALU.mult)
                        hv = h[:, g, :].rearrange("p (c t) -> p c t", c=NCH)
                        e.scalar_tensor_tensor(out=hv, in0=dwv[:, :, :, 0], scalar=sclv[:, g:g + 1],
                                               in1=hv, op0=ALU.mult, op1=ALU.add)
                        e.scalar_tensor_tensor(out=hv, in0=dwv[:, :, :, 1], scalar=sclv[:, g:g + 1],
                                               in1=hv, op0=ALU.mult, op1=ALU.add).then_inc(SV.h, 1)
                    blk.vector(_scan)
                    SV.inc(1)

                def _chains(e):
                    e.wait_ge(SV.h, SV.n)
                    e.memset(iot[:, :, 0, :], 0.0)
                    for c in range(1, NCH):
                        pi = iot[:, :, c - 1, :]
                        pe = ends4[:, :, c - 1, :]
                        e.tensor_tensor(out=tD[:, :, 0], in0=q4[:, :, 0], in1=pi[:, :, 0], op=ALU.mult)
                        e.tensor_tensor(out=tE[:, :, 0], in0=u5[:, :, 0], in1=pe[:, :, 0], op=ALU.mult)
                        e.tensor_tensor(out=tD[:, :, 1], in0=q4[:, :, 0], in1=pi[:, :, 1], op=ALU.mult)
                        e.tensor_tensor(out=tE[:, :, 1], in0=u5[:, :, 0], in1=pe[:, :, 1], op=ALU.mult)
                        e.tensor_tensor(out=tA[:], in0=q4[:, :, 1], in1=pi[:, :, 1], op=ALU.mult)
                        e.tensor_tensor(out=tB[:], in0=u5[:, :, 1], in1=pe[:, :, 1], op=ALU.mult)
                        e.tensor_tensor(out=tD[:, :, 0], in0=tD[:, :, 0], in1=tA[:], op=ALU.subtract)
                        e.tensor_tensor(out=tE[:, :, 0], in0=tE[:, :, 0], in1=tB[:], op=ALU.subtract)
                        e.tensor_tensor(out=tA[:], in0=q4[:, :, 1], in1=pi[:, :, 0], op=ALU.mult)
                        e.tensor_tensor(out=tB[:], in0=u5[:, :, 1], in1=pe[:, :, 0], op=ALU.mult)
                        e.tensor_tensor(out=tD[:, :, 1], in0=tD[:, :, 1], in1=tA[:], op=ALU.add)
                        e.tensor_tensor(out=tE[:, :, 1], in0=tE[:, :, 1], in1=tB[:], op=ALU.add)
                        e.tensor_tensor(out=iot[:, :, c, :], in0=tD[:, :, :], in1=tE[:, :, :], op=ALU.add)
                    e.tensor_tensor(out=tD[:, :, 0], in0=iot[:, :, NCH - 1, 0],
                                    in1=dk4[:, :, CK - 1], op=ALU.mult)
                    e.tensor_tensor(out=tD[:, :, 1], in0=iot[:, :, NCH - 1, 1],
                                    in1=dk4[:, :, CK - 1], op=ALU.mult)
                    e.tensor_tensor(out=tD[:, :, :], in0=tD[:, :, :],
                                    in1=ends4[:, :, NCH - 1, :], op=ALU.add)
                    e.tensor_copy(out=tF[:, :, 0], in_=rint[:, :, CK - 1, 0])
                    e.tensor_scalar(out=tF[:, :, 1], in0=rint[:, :, CK - 1, 1], scalar1=-1.0,
                                    scalar2=None, op0=ALU.mult)
                    e.tensor_tensor(out=tA[:], in0=tF[:, :, 0], in1=tD[:, :, 0], op=ALU.mult)
                    e.tensor_tensor(out=tB[:], in0=tF[:, :, 1], in1=tD[:, :, 1], op=ALU.mult)
                    e.tensor_tensor(out=tC[:], in0=tF[:, :, 1], in1=tD[:, :, 0], op=ALU.mult)
                    e.tensor_tensor(out=tE[:, :, 1], in0=tF[:, :, 0], in1=tD[:, :, 1], op=ALU.mult)
                    e.tensor_tensor(out=Ev[:, :, 0], in0=tA[:], in1=tB[:], op=ALU.subtract)
                    e.tensor_tensor(out=Ev[:, :, 1], in0=tE[:, :, 1], in1=tC[:], op=ALU.add).then_inc(SV.h, 1)
                blk.vector(_chains)
                SV.inc(1)

                def _pub(e, l=l):
                    e.wait_ge(SV.h, SV.n)
                    dma_w(e, GD, hcc_i[l].ap(), Ev[:].rearrange("p g c -> p (g c)"))
                    e.collective_compute("AllGather", ALU.bypass, replica_groups=[list(range(NCORE))],
                                         ins=[hcc_i[l].ap().opt()], outs=[hcc_o[l].ap().opt()]
                                         ).then_inc(CC.h, 1)
                    CC.inc(1)
                    e.wait_ge(CC.h, CC.n)
                    dma_w(e, GD, gat[:].rearrange("p s g c -> p s (g c)"),
                          hcc_o[l].ap().rearrange("(s p) c -> p s c", p=128))
                blk.gpsimd(_pub)

                def _comb(e):
                    e.wait_ge(GD.h, GD.n)
                    hbufs = [(Hre[:], Him[:]), (tE[:, :, 0], tE[:, :, 1])]
                    for j in range(NCORE):
                        mj = m8b[:, j:j + 1]
                        hr, hi = hbufs[j % 2]
                        nr, ni = hbufs[(j + 1) % 2]
                        e.tensor_scalar(out=tA[:], in0=pm1[:], scalar1=mj, scalar2=1.0,
                                        op0=ALU.mult, op1=ALU.add)
                        e.tensor_scalar(out=tB[:], in0=p2k[:, :, 1], scalar1=mj, scalar2=None, op0=ALU.mult)
                        e.tensor_scalar(out=tD[:, :, 0], in0=gat[:, j, :, 0], scalar1=mj, scalar2=None,
                                        op0=ALU.mult)
                        e.tensor_scalar(out=tD[:, :, 1], in0=gat[:, j, :, 1], scalar1=mj, scalar2=None,
                                        op0=ALU.mult)
                        e.tensor_tensor(out=tC[:], in0=tA[:], in1=hr, op=ALU.mult)
                        e.tensor_tensor(out=tF[:, :, 0], in0=tB[:], in1=hi, op=ALU.mult)
                        e.tensor_tensor(out=tF[:, :, 1], in0=tA[:], in1=hi, op=ALU.mult)
                        e.tensor_tensor(out=tB[:], in0=tB[:], in1=hr, op=ALU.mult)
                        e.tensor_tensor(out=tD[:, :, 0], in0=tD[:, :, 0], in1=tC[:], op=ALU.add)
                        e.tensor_tensor(out=tD[:, :, 1], in0=tD[:, :, 1], in1=tF[:, :, 1], op=ALU.add)
                        e.tensor_tensor(out=nr, in0=tD[:, :, 0], in1=tF[:, :, 0], op=ALU.subtract)
                        e.tensor_tensor(out=ni, in0=tD[:, :, 1], in1=tB[:], op=ALU.add)
                    hr, hi = hbufs[NCORE % 2]
                    e.tensor_tensor(out=tA[:], in0=ure4[:], in1=hr, op=ALU.mult)
                    e.tensor_tensor(out=tB[:], in0=uim4[:], in1=hi, op=ALU.mult)
                    e.tensor_tensor(out=tC[:], in0=ure4[:], in1=hi, op=ALU.mult)
                    e.tensor_tensor(out=tD[:, :, 0], in0=uim4[:], in1=hr, op=ALU.mult)
                    e.tensor_tensor(out=tA[:], in0=tA[:], in1=tB[:], op=ALU.subtract)
                    e.tensor_tensor(out=tB[:], in0=tC[:], in1=tD[:, :, 0], op=ALU.add)
                    for c in range(NCH):
                        qp = qpow[:, c]
                        e.tensor_tensor(out=tD[:, :, 0], in0=qp[:, :, 0], in1=tA[:], op=ALU.mult)
                        e.tensor_tensor(out=tC[:], in0=qp[:, :, 1], in1=tB[:], op=ALU.mult)
                        e.tensor_tensor(out=tD[:, :, 1], in0=qp[:, :, 0], in1=tB[:], op=ALU.mult)
                        e.tensor_tensor(out=tF[:, :, 0], in0=qp[:, :, 1], in1=tA[:], op=ALU.mult)
                        e.tensor_tensor(out=tD[:, :, 0], in0=tD[:, :, 0], in1=tC[:], op=ALU.subtract)
                        e.tensor_tensor(out=tD[:, :, 1], in0=tD[:, :, 1], in1=tF[:, :, 0], op=ALU.add)
                        for comp in range(2):
                            e.tensor_tensor(out=iot[:, :, c, comp], in0=iot[:, :, c, comp],
                                            in1=tD[:, :, comp], op=ALU.add)
                    e.tensor_tensor(out=iosc[:], in0=iot[:],
                                    in1=sclv[:].unsqueeze(2).unsqueeze(3).broadcast_to([128, NGD, NCH, 2]),
                                    op=ALU.mult).then_inc(SV.h, 1)
                blk.vector(_comb)
                SV.inc(1)

                def _ctr(e):
                    e.wait_ge(SV.h, SV.n)
                    t12s = [ytmp[:, 0:CK * 2].rearrange("p (t x) -> p t x", x=2),
                            ytmp[:, CK * 2:CK * 4].rearrange("p (t x) -> p t x", x=2)]

                    def build(g):
                        e.tensor_tensor(out=t12s[g % 2], in0=rint[:, g],
                                        in1=dk4[:, g].unsqueeze(2).broadcast_to([128, CK, 2]),
                                        op=ALU.mult)

                    def stts(g):
                        for comp in range(2):
                            for c in range(NCH):
                                hs = h[:, g, c * CK:c * CK + CK]
                                e.scalar_tensor_tensor(out=hs, in0=t12s[g % 2][:, :, comp],
                                                       scalar=iosc[:, g, c, comp:comp + 1],
                                                       in1=hs, op0=ALU.mult, op1=ALU.add)
                    build(0)
                    build(1)
                    stts(0)
                    build(2)
                    stts(1)
                    build(3)
                    stts(2)
                    stts(3)
                    e.tensor_copy(out=sct[:, 0:1], in_=sct[:, 0:1]).then_inc(SV.h, 1)
                blk.vector(_ctr)
                SV.inc(1)
                barrier(("vector", "scalar", "tensor"))

                attn_ln_stats()
                for g in range(NGD):
                    def _nr2(e, g=g):
                        if g == 0:
                            wait_all(e)
                        else:
                            e.wait_ge(SSm.h, SSm.n)
                        e.tensor_tensor(out=ytmp[:], in0=h[:, g, :], in1=rstdb[:], op=ALU.mult)
                        e.tensor_tensor(out=ytmp[:], in0=ytmp[:], in1=nub[:],
                                        op=ALU.subtract).then_inc(SV.h, 1)
                    blk.vector(_nr2)
                    SV.inc(1)

                    def _aff(e, g=g):
                        e.wait_ge(SV.h, SV.n)
                        e.activation(out=scrbf[:, g, :], in_=ytmp[:], func=AF.Identity,
                                     scale=ln2w[:, g:g + 1], bias=ln2b[:, g:g + 1]).then_inc(SSm.h, 1)
                    blk.scalar(_aff)
                    SSm.inc(1)
                barrier(("vector", "scalar", "tensor", "gpsimd"))

                for tt in range(NTT):
                    ss_f1 = SSm.n

                    def _mmf1(e, tt=tt, ss_f1=ss_f1):
                        if tt == 0:
                            wait_all(e)
                        else:
                            e.wait_ge(SSm.h, ss_f1)
                        for m in range(NGF):
                            bank = 2 * (m % 2)
                            if m >= 2:
                                e.wait_ge(SSm.h, ss_f1 + m - 1)
                            ins = None
                            for k in range(NGD):
                                ins = e.matmul(out=ps[bank][:, :], lhsT=w1b[:, k, m * 128:m * 128 + 128],
                                               rhs=scrbf[:, k, tt * TT:tt * TT + TT],
                                               start=(k == 0), stop=(k == NGD - 1))
                            ins.then_inc(ST.h, 1)
                    blk.tensor(_mmf1)
                    st_f1 = ST.n
                    ST.inc(NGF)

                    def _evf1(e, st_f1=st_f1):
                        for m in range(NGF):
                            bank = 2 * (m % 2)
                            e.wait_ge(ST.h, st_f1 + m + 1)
                            e.activation(out=midt[:, m, :], in_=ps[bank][:, :], func=AF.Silu,
                                         bias=b1v[:, m:m + 1]).then_inc(SSm.h, 1)
                    blk.scalar(_evf1)
                    SSm.inc(NGF)

                    sv_f2 = SV.n

                    def _mmf2(e, sv_f2=sv_f2):
                        e.wait_ge(SSm.h, SSm.n)
                        e.wait_ge(SV.h, sv_f2)
                        for m in range(NGD):
                            bank = 4 + 2 * (m % 2)
                            if m >= 2:
                                e.wait_ge(SV.h, sv_f2 + m - 1)
                            ins = None
                            for k in range(NGF):
                                ins = e.matmul(out=ps[bank][:, :], lhsT=w2b[:, k, m * 128:m * 128 + 128],
                                               rhs=midt[:, k, :], start=(k == 0), stop=(k == NGF - 1))
                            ins.then_inc(ST.h, 1)
                    blk.tensor(_mmf2)
                    st_f2 = ST.n
                    ST.inc(NGD)

                    def _evf2(e, tt=tt, st_f2=st_f2):
                        for m in range(NGD):
                            bank = 4 + 2 * (m % 2)
                            e.wait_ge(ST.h, st_f2 + m + 1)
                            hs = h[:, m, tt * TT:tt * TT + TT]
                            e.scalar_tensor_tensor(out=hs, in0=ps[bank][:, :], scalar=b2v[:, m:m + 1],
                                                   in1=hs, op0=ALU.add, op1=ALU.add).then_inc(SV.h, 1)
                    blk.vector(_evf2)
                    SV.inc(NGD)
                barrier()

                if debug:
                    def _dbg2(e, l=l):
                        wait_all(e)
                        dma_w(e, LD, dbg[f"h{l+1}"], h[:])
                    blk.sync(_dbg2)
                    barrier()

            attn_ln_stats()
            for g in range(NGD):
                def _nr4(e, g=g):
                    if g == 0:
                        wait_all(e)
                    else:
                        e.wait_ge(SSm.h, SSm.n)
                    e.tensor_tensor(out=ytmp[:], in0=h[:, g, :], in1=rstdb[:], op=ALU.mult)
                    e.tensor_tensor(out=ytmp[:], in0=ytmp[:], in1=nub[:],
                                    op=ALU.subtract).then_inc(SV.h, 1)
                blk.vector(_nr4)
                SV.inc(1)

                def _aff4(e, g=g):
                    e.wait_ge(SV.h, SV.n)
                    e.activation(out=scrbf[:, g, :], in_=ytmp[:], func=AF.Identity,
                                 scale=lnlw[:, g:g + 1], bias=lnlb[:, g:g + 1]).then_inc(SSm.h, 1)
                blk.scalar(_aff4)
                SSm.inc(1)
            barrier()

            if debug:
                def _dbg3(e):
                    wait_all(e)
                    dma_w(e, LD, dbg["hn"], scrbf[:])
                blk.sync(_dbg3)
                barrier()

        if stages < 3:
            def _fin2(e):
                wait_all(e)
            for name in ("sync", "scalar", "vector", "gpsimd", "tensor"):
                getattr(blk, name)(_fin2)
            es2.close()
            return nc
        es2.close()
        with (
            nc.sbuf_tensor("xnf", [128, NGV, TOK], BF16) as xnf,
            nc.sbuf_tensor("wtb", [128, 2, NGV, 128], BF16) as wtb,
            nc.sbuf_tensor("woutm", [128, 2, NGD, 128], BF16) as woutm,
            nc.sbuf_tensor("btb", [64, VD], BF16) as btb,
            nc.sbuf_tensor("xm", [128, TOK], F32) as xm,
            nc.sbuf_tensor("outst", [128, 2, TOK], F32) as outst,
            nc.sbuf_tensor("biast", [128, NGV], F32) as biast,
            nc.sbuf_tensor("bias2", [128, NGV], F32) as bias2,
        ):
            def _l3(e):
                for i in range(NGV // 8):
                    dma_w(e, LD, xnf[:, i * 8:(i + 1) * 8, :],
                          dxn[i * 8:(i + 1) * 8].rearrange("g p t -> p g t"))
                dma_w(e, LD, biast[:], dbout)
                dma_w(e, LD, bias2[:], dbff)
            blk.sync(_l3)

            def _l3g(e):
                dma_w(e, GD, btb[:], dbT)
            blk.gpsimd(_l3g)

            def _bsum(e):
                wait_all(e)
                e.tensor_tensor(out=biast[:], in0=biast[:], in1=bias2[:], op=ALU.add).then_inc(SV.h, 1)
            blk.vector(_bsum)
            SV.inc(1)
            barrier()

            st_mm = []
            for m in range(NGV):
                buf = m % 2
                ms = m * 128

                GWp = GW[buf]

                def _lw3(e, m=m, buf=buf, ms=ms, w=(st_mm[m - 2] if m >= 2 else 0), GWp=GWp):
                    if m >= 2:
                        e.wait_ge(ST.h, w)
                    e.dma_start(out=wtb[:, buf],
                                in_=dwff[:, ms:ms + 128].rearrange("(k p) f -> p k f", p=128)
                                ).then_inc(GWp.h, 16)
                    GWp.inc(16)
                    e.dma_start(out=woutm[:, buf],
                                in_=dwout[:, ms:ms + 128].rearrange("(k p) f -> p k f", p=128)
                                ).then_inc(GWp.h, 16)
                    GWp.inc(16)
                blk.gpsimd(_lw3)

                def _lx3(e, m=m, ms=ms):
                    if m >= 1:
                        e.wait_ge(SV.h, SV.n)
                    dma_w(e, LD, xm[:], dxT[ms:ms + 128, :])
                blk.sync(_lx3)

                gw_m = GW[buf].n
                sv_banks = SV.n

                def _mm5(e, m=m, buf=buf, gw_m=gw_m, sv_banks=sv_banks):
                    e.wait_ge(GW[buf].h, gw_m)
                    e.wait_ge(GD.h, GD.n)
                    if m >= 2:
                        e.wait_ge(SV.h, sv_banks - 1)
                    ins = None
                    for tt in range(NTT):
                        e.matmul(out=ps[4 * buf + tt][:, :], lhsT=btb[0:64, m * 128:m * 128 + 128],
                                 rhs=umask[0:64, tt * TT:tt * TT + TT], start=True, stop=False)
                    for k in range(NGD):
                        for tt in range(NTT):
                            e.matmul(out=ps[4 * buf + tt][:, :], lhsT=woutm[:, buf, k, :],
                                     rhs=scrbf[:, k, tt * TT:tt * TT + TT], start=False, stop=False)
                    for k in range(NGV):
                        for tt in range(NTT):
                            ins = e.matmul(out=ps[4 * buf + tt][:, :], lhsT=wtb[:, buf, k, :],
                                           rhs=xnf[:, k, tt * TT:tt * TT + TT], start=False,
                                           stop=(k == NGV - 1))
                    ins.then_inc(ST.h, 1)
                blk.tensor(_mm5)
                ST.inc(1)
                st_mm.append(ST.n)

                st_m = ST.n
                ld_m = LD.n

                def _ev5(e, m=m, buf=buf, st_m=st_m, ld_m=ld_m):
                    e.wait_ge(ST.h, st_m)
                    e.wait_ge(LD.h, ld_m)
                    ins = None
                    for tt in range(NTT):
                        bank = 4 * buf + tt
                        ins = e.scalar_tensor_tensor(out=outst[:, buf, tt * TT:tt * TT + TT],
                                                     in0=ps[bank][:, :], scalar=biast[:, m:m + 1],
                                                     in1=xm[:, tt * TT:tt * TT + TT],
                                                     op0=ALU.add, op1=ALU.add)
                    ins.then_inc(SV.h, 1)
                blk.vector(_ev5)
                SV.inc(1)

                sv_m = SV.n

                def _st5(e, buf=buf, ms=ms, sv_m=sv_m):
                    e.wait_ge(SV.h, sv_m)
                    dma_w(e, LD, dout[ms:ms + 128, :], outst[:, buf, :])
                blk.sync(_st5)

            def _fin(e):
                e.wait_ge(LD.h, LD.n)
            blk.sync(_fin)

            if debug:
                def _dbg4(e):
                    wait_all(e)
                    dma_w(e, LD, dbg["xn"], xnf[:])
                blk.sync(_dbg4)

    return nc


def _patm():
    import ml_dtypes
    p = np.zeros((4, 64), ml_dtypes.bfloat16)
    p[0, 0:8] = 1
    p[1, 8:16] = 1
    p[2, 16:32] = 1
    p[3, 32:64] = 1
    return p


def _prep(inputs):
    f = np.float32
    x = np.asarray(inputs["L0_attn_x"], f)[0]
    xT = np.ascontiguousarray(x.T)
    Win = np.asarray(inputs["L0_attn_Win"], f)
    A = np.asarray(inputs["L0_ffn_A"], f)
    gw = np.asarray(inputs["L0_ffn_gate_w"], f)
    com = {
        "winT": np.ascontiguousarray(Win.T),
        "agT": np.ascontiguousarray(np.concatenate([A.T, gw.T], axis=1)),
        "w1T": np.ascontiguousarray(np.asarray(inputs["L0_attn_W1"], f).transpose(0, 2, 1)),
        "w2T": np.ascontiguousarray(np.asarray(inputs["L0_attn_W2"], f).transpose(0, 2, 1)),
        "woutT": np.ascontiguousarray(np.asarray(inputs["L0_attn_Wout"], f).T),
        "wT": np.ascontiguousarray(np.asarray(inputs["L0_ffn_W"], f).T),
        "bT": np.ascontiguousarray(np.asarray(inputs["L0_ffn_B"], f).T),
        "bin_r": np.ascontiguousarray(np.asarray(inputs["L0_attn_bin"], f).reshape(NGD, 128).T),
        "b1_r": np.ascontiguousarray(np.asarray(inputs["L0_attn_b1"], f).reshape(NL, NGF, 128).transpose(0, 2, 1)),
        "lnlw_r": np.ascontiguousarray(np.asarray(inputs["L0_attn_lnl_w"], f).reshape(NGD, 128).T),
        "lnlb_r": np.ascontiguousarray(np.asarray(inputs["L0_attn_lnl_b"], f).reshape(NGD, 128).T),
        "bout_r": np.ascontiguousarray(np.asarray(inputs["L0_attn_bout"], f).reshape(NGV, 128).T),
        "bff_r": np.ascontiguousarray(np.asarray(inputs["L0_ffn_b"], f).reshape(NGV, 128).T),
        "gateb_r": np.ascontiguousarray(np.asarray(inputs["L0_ffn_gate_b"], f).reshape(1, 4)),
        "patm": _patm(),
    }
    for nm, key in (("ln1w_r", "L0_attn_ln1_w"), ("ln1b_r", "L0_attn_ln1_b"),
                    ("ln2w_r", "L0_attn_ln2_w"), ("ln2b_r", "L0_attn_ln2_b"),
                    ("scl_r", "L0_attn_sc_lin"), ("b2_r", "L0_attn_b2"),
                    ("phzre_r", "L0_attn_phz_re"), ("phzim_r", "L0_attn_phz_im"),
                    ("lcre_r", "L0_attn_lc_re"), ("lcim_r", "L0_attn_lc_im")):
        com[nm] = np.ascontiguousarray(
            np.asarray(inputs[key], f).reshape(NL, NGD, 128).transpose(0, 2, 1))
    in_maps = []
    for c in range(NCORE):
        mp = dict(com)
        mp["xT"] = np.ascontiguousarray(xT[:, c * TOK:(c + 1) * TOK])
        mp["mask8"] = np.ascontiguousarray(
            np.array([1.0] * c + [0.0] * (NCORE - c), f).reshape(1, 8))
        in_maps.append(mp)
    return in_maps


def kernel(**inputs):
    in_maps = _prep(inputs)
    nc = build_nc(debug=DEBUG, stages=STAGES)
    res = run_bass_kernel_spmd(nc, in_maps, core_ids=list(range(NCORE)))
    out = np.concatenate([np.asarray(res.results[c]["outT"]) for c in range(NCORE)], axis=1)
    if DEBUG:
        kernel._last = res
    return np.ascontiguousarray(out.T).reshape(1, 16384, 4096).astype(np.float32)


# version 04
# speedup vs baseline: 1.1377x; 1.0025x over previous
import contextlib
import numpy as np
import concourse.bass as bass
import concourse.mybir as mybir
from concourse.bass_utils import run_bass_kernel_spmd

F32 = mybir.dt.float32
BF16 = mybir.dt.bfloat16
I32 = mybir.dt.int32
AF = mybir.ActivationFunctionType
ALU = mybir.AluOpType

NCORE = 8
TOK = 2048
VD = 4096
DD = 512
FF = 2048
NL = 4
CK = 512
NCH = TOK // CK
NSLAB = 4
SLAB = TOK // NSLAB
NTT = 4
TT = 512
NGV = VD // 128
NGD = DD // 128
NGF = FF // 128
EPS = 1e-5
RSQRT_MAGIC = 1597463007.0

DEBUG = False
STAGES = 3


class VSafe:

    def __init__(self, e, state):
        self._e = e
        self._s = state

    @staticmethod
    def _names(aps):
        out = []
        for a in aps:
            if a is None or isinstance(a, (int, float)):
                continue
            t = getattr(a, "tensor", None)
            if t is None:
                continue
            try:
                off = a.offset
                ext = 1
                for step, cnt in list(a.ap)[1:]:
                    ext += (cnt - 1) * abs(step)
            except Exception:
                off, ext = 0, 1 << 40
            out.append((t.name, off, off + ext))
        return out

    @staticmethod
    def _hit(a, b):
        for n1, s1, e1 in a:
            for n2, s2, e2 in b:
                if n1 == n2 and s1 < e2 and s2 < e1:
                    return True
        return False

    def _do(self, ins, outs, call, scal=()):
        ii = self._names(ins) + self._names(scal)
        oo = self._names(outs)
        s = self._s
        sv = s["sv"]
        need = 0
        for pi, po, k in s["hist"]:
            if k <= s["cred"]:
                continue
            if self._hit(ii, po) or self._hit(oo, po) or self._hit(oo, pi):
                need = max(need, k)
        if len(s["hist"]) > 192:
            need = max(need, s["hist"][-160][2])
        if need > s["cred"]:
            self._e.wait_ge(sv.h, need)
            s["cred"] = need
            s["hist"] = [h for h in s["hist"] if h[2] > need]
        r = call()
        r.then_inc(sv.h, 1)
        k = sv.inc(1)
        s["hist"].append((ii, oo, k))
        return r

    def tensor_copy(self, out, in_):
        return self._do([in_], [out], lambda: self._e.tensor_copy(out=out, in_=in_))

    def memset(self, ap, c):
        return self._do([], [ap], lambda: self._e.memset(ap, c))

    def tensor_scalar(self, out, in0, scalar1, scalar2, op0, op1=None):
        kw = {} if op1 is None else {"op1": op1}
        return self._do([in0], [out],
                        lambda: self._e.tensor_scalar(out=out, in0=in0, scalar1=scalar1,
                                                      scalar2=scalar2, op0=op0, **kw),
                        scal=[scalar1, scalar2])

    def tensor_tensor(self, out, in0, in1, op):
        return self._do([in0, in1], [out],
                        lambda: self._e.tensor_tensor(out=out, in0=in0, in1=in1, op=op))

    def scalar_tensor_tensor(self, out, in0, scalar, in1, op0, op1):
        return self._do([in0, in1], [out],
                        lambda: self._e.scalar_tensor_tensor(out=out, in0=in0, scalar=scalar,
                                                             in1=in1, op0=op0, op1=op1),
                        scal=[scalar])

    def tensor_tensor_scan(self, out, data0, data1, initial, op0, op1):
        return self._do([data0, data1], [out],
                        lambda: self._e.tensor_tensor_scan(out=out, data0=data0, data1=data1,
                                                           initial=initial, op0=op0, op1=op1),
                        scal=[initial] if not isinstance(initial, float) else ())

    def reciprocal(self, out, in_):
        return self._do([in_], [out], lambda: self._e.reciprocal(out=out, in_=in_))

    def __getattr__(self, name):
        return getattr(self._e, name)


class Sem:
    def __init__(self, es, nc, name):
        self.h = es.enter_context(nc.semaphore(name))
        self.n = 0

    def inc(self, k=1):
        self.n += k
        return self.n


def _rsqrt(e, out, in_, tmp_i, tmp_a):
    n = in_.shape[-1]
    q = max(1, n // 4)
    sl = []
    lo = 0
    while lo < n:
        hi = min(n, lo + q)
        sl.append((lo, hi))
        lo = hi
    def S(ap, s):
        return ap[..., s[0]:s[1]]
    for s in sl:
        e.tensor_copy(out=S(tmp_a, s), in_=S(in_, s).bitcast(I32))
    for s in sl:
        e.tensor_scalar(out=S(tmp_a, s), in0=S(tmp_a, s), scalar1=-0.5 * 2.0 ** -8,
                        scalar2=RSQRT_MAGIC * 2.0 ** -8, op0=ALU.mult, op1=ALU.add)
    for s in sl:
        e.tensor_copy(out=S(tmp_i, s), in_=S(tmp_a, s))
    for s in sl:
        e.tensor_scalar(out=S(tmp_i, s), in0=S(tmp_i, s), scalar1=8, scalar2=None,
                        op0=ALU.arith_shift_left)
    for s in sl:
        e.tensor_copy(out=S(out, s), in_=S(tmp_i, s).bitcast(F32))
    for _ in range(3):
        for s in sl:
            e.tensor_tensor(out=S(tmp_a, s), in0=S(out, s), in1=S(out, s), op=ALU.mult)
        for s in sl:
            e.tensor_tensor(out=S(tmp_a, s), in0=S(tmp_a, s), in1=S(in_, s), op=ALU.mult)
        for s in sl:
            e.tensor_scalar(out=S(tmp_a, s), in0=S(tmp_a, s), scalar1=-0.5, scalar2=1.5,
                            op0=ALU.mult, op1=ALU.add)
        for s in sl:
            e.tensor_tensor(out=S(out, s), in0=S(out, s), in1=S(tmp_a, s), op=ALU.mult)


def build_nc(debug=False, stages=3):
    nc = bass.Bass("TRN2", target_bir_lowering=False)

    dxT = nc.dram_tensor("xT", [VD, TOK], F32, kind="ExternalInput").ap()
    dwin = nc.dram_tensor("winT", [VD, DD], F32, kind="ExternalInput").ap()
    dag = nc.dram_tensor("agT", [VD, 68], F32, kind="ExternalInput").ap()
    dw1 = nc.dram_tensor("w1T", [NL, DD, FF], F32, kind="ExternalInput").ap()
    dw2 = nc.dram_tensor("w2T", [NL, FF, DD], F32, kind="ExternalInput").ap()
    dwout = nc.dram_tensor("woutT", [DD, VD], F32, kind="ExternalInput").ap()
    dwff = nc.dram_tensor("wT", [VD, VD], F32, kind="ExternalInput").ap()
    dbT = nc.dram_tensor("bT", [64, VD], F32, kind="ExternalInput").ap()
    dbin = nc.dram_tensor("bin_r", [128, NGD], F32, kind="ExternalInput").ap()
    dln1w = nc.dram_tensor("ln1w_r", [NL, 128, NGD], F32, kind="ExternalInput").ap()
    dln1b = nc.dram_tensor("ln1b_r", [NL, 128, NGD], F32, kind="ExternalInput").ap()
    dln2w = nc.dram_tensor("ln2w_r", [NL, 128, NGD], F32, kind="ExternalInput").ap()
    dln2b = nc.dram_tensor("ln2b_r", [NL, 128, NGD], F32, kind="ExternalInput").ap()
    dscl = nc.dram_tensor("scl_r", [NL, 128, NGD], F32, kind="ExternalInput").ap()
    db2 = nc.dram_tensor("b2_r", [NL, 128, NGD], F32, kind="ExternalInput").ap()
    dphr = nc.dram_tensor("phzre_r", [NL, 128, NGD], F32, kind="ExternalInput").ap()
    dphi = nc.dram_tensor("phzim_r", [NL, 128, NGD], F32, kind="ExternalInput").ap()
    dlcr = nc.dram_tensor("lcre_r", [NL, 128, NGD], F32, kind="ExternalInput").ap()
    dlci = nc.dram_tensor("lcim_r", [NL, 128, NGD], F32, kind="ExternalInput").ap()
    db1 = nc.dram_tensor("b1_r", [NL, 128, NGF], F32, kind="ExternalInput").ap()
    dlnlw = nc.dram_tensor("lnlw_r", [128, NGD], F32, kind="ExternalInput").ap()
    dlnlb = nc.dram_tensor("lnlb_r", [128, NGD], F32, kind="ExternalInput").ap()
    dbout = nc.dram_tensor("bout_r", [128, NGV], F32, kind="ExternalInput").ap()
    dbff = nc.dram_tensor("bff_r", [128, NGV], F32, kind="ExternalInput").ap()
    dgateb = nc.dram_tensor("gateb_r", [1, 4], F32, kind="ExternalInput").ap()
    dmask8 = nc.dram_tensor("mask8", [1, 8], F32, kind="ExternalInput").ap()
    dpat = nc.dram_tensor("patm", [4, 64], BF16, kind="ExternalInput").ap()
    dout = nc.dram_tensor("outT", [VD, TOK], F32, kind="ExternalOutput").ap()

    dxn = nc.dram_tensor("xn_spill", [NGV, 128, TOK], BF16).ap()
    dlni = nc.dram_tensor("lni", [2 * TOK], F32).ap()
    dlno = nc.dram_tensor("lno", [2 * TOK], F32).ap()
    dgsc = nc.dram_tensor("gsc", [4, TOK], F32).ap()
    dgm = nc.dram_tensor("gm", [4, TOK], BF16).ap()
    hcc_i = [nc.dram_tensor(f"cc_i{l}", [128, 8], F32) for l in range(NL)]
    hcc_o = [nc.dram_tensor(f"cc_o{l}", [128 * 8, 8], F32) for l in range(NL)]

    dbg = {}
    if debug:
        dbg["h0"] = nc.dram_tensor("dbg_h0", [128, NGD, TOK], F32, kind="ExternalOutput").ap()
        dbg["xn"] = nc.dram_tensor("dbg_xn", [128, NGV, TOK], BF16, kind="ExternalOutput").ap()
        for l in range(NL):
            dbg[f"h{l+1}"] = nc.dram_tensor(f"dbg_h{l+1}", [128, NGD, TOK], F32, kind="ExternalOutput").ap()
        dbg["hn"] = nc.dram_tensor("dbg_hn", [128, NGD, TOK], BF16, kind="ExternalOutput").ap()
        dbg["um"] = nc.dram_tensor("dbg_um", [64, TOK], BF16, kind="ExternalOutput").ap()

    es = contextlib.ExitStack()
    with es:
        sb = lambda name, shape, dt: es.enter_context(nc.sbuf_tensor(name, shape, dt))
        es2 = contextlib.ExitStack()
        sbm = lambda name, shape, dt: es2.enter_context(nc.sbuf_tensor(name, shape, dt))
        scrbf = sb("scrbf", [128, NGD, TOK], BF16)
        umask = sb("umask", [64, TOK], BF16)
        onesb = sb("onesb", [128, 1], BF16)
        gbb = sb("gbb", [128, 4], F32)
        m8b = sb("m8b", [128, 8], F32)
        h = sbm("h", [128, NGD, TOK], F32)
        onesf = sbm("onesf", [128, CK], F32)
        binv = sbm("binv", [128, NGD], F32)
        lnlw = sbm("lnlw", [128, NGD], F32)
        lnlb = sbm("lnlb", [128, NGD], F32)
        ln1w = sbm("ln1w", [128, NGD], F32)
        ln1b = sbm("ln1b", [128, NGD], F32)
        ln2w = sbm("ln2w", [128, NGD], F32)
        ln2b = sbm("ln2b", [128, NGD], F32)
        sclv = sbm("sclv", [128, NGD], F32)
        b2v = sbm("b2v", [128, NGD], F32)
        b1v = sbm("b1v", [128, NGF], F32)
        phr = sbm("phr", [128, NGD], F32)
        phi = sbm("phi", [128, NGD], F32)
        rho4 = sbm("rho4", [128, NGD], F32)
        ure4 = sbm("ure4", [128, NGD], F32)
        uim4 = sbm("uim4", [128, NGD], F32)
        q4 = sbm("q4", [128, NGD, 2], F32)
        u5 = sbm("u5", [128, NGD, 2], F32)
        p2k = sbm("p2k", [128, NGD, 2], F32)
        pm1 = sbm("pm1", [128, NGD], F32)
        qpow = sbm("qpow", [128, NCH, NGD, 2], F32)
        iot = sbm("iot", [128, NGD, NCH, 2], F32)
        iosc = sbm("iosc", [128, NGD, NCH, 2], F32)
        Ev = sbm("Ev", [128, NGD, 2], F32)
        gat = sbm("gat", [128, 8, NGD, 2], F32)
        Hre = sbm("Hre", [128, NGD], F32)
        Him = sbm("Him", [128, NGD], F32)
        tA = sbm("tA", [128, NGD], F32)
        tB = sbm("tB", [128, NGD], F32)
        tC = sbm("tC", [128, NGD], F32)
        tD = sbm("tD", [128, NGD, 2], F32)
        tE = sbm("tE", [128, NGD, 2], F32)
        scs = sbm("scs", [128, 2], F32)
        sct = sbm("sct", [128, 4], F32)
        scs4 = sbm("scs4", [128, NGD, 2], F32)
        sct4 = sbm("sct4", [128, NGD, 4], F32)
        ends4 = sbm("ends4", [128, NGD, NCH, 2], F32)
        tF = sbm("tF", [128, NGD, 2], F32)
        statin = sbm("statin", [1, 2 * TOK], F32)
        statq = sbm("statq", [128, 2, TOK // 128], F32)
        stato = sbm("stato", [128, 2, TOK // 128], F32)
        stati = sbm("stati", [128, TOK // 128], I32)
        stata = sbm("stata", [128, TOK // 128], F32)
        rstdb = sbm("rstdb", [128, TOK], F32)
        nub = sbm("nub", [128, TOK], F32)
        hbt = sbm("hbt", [128, 2, TOK], BF16)
        h2t = sbm("h2t", [128, 2, TOK], BF16)
        ytmp = sbm("ytmp", [128, TOK], F32)
        ps = [es.enter_context(nc.psum_tensor(f"ps{i}", [128, TT], F32)) for i in range(8)]

        LD = Sem(es, nc, "LD")
        GD = Sem(es, nc, "GD")
        SV = Sem(es, nc, "SV")
        SSm = Sem(es, nc, "SS")
        ST = Sem(es, nc, "ST")
        SG = Sem(es, nc, "SG")
        CC = Sem(es, nc, "CC")
        GW = [Sem(es, nc, "GW0"), Sem(es, nc, "GW1")]

        blk = es.enter_context(nc.Block())
        _vstate = {"hist": [], "cred": 0, "sv": SV}

        class _BlkWrap:
            def __init__(self, b):
                self._b = b

            def vector(self, f):
                return self._b.vector(lambda e: f(VSafe(e, _vstate)))

            def __getattr__(self, name):
                return getattr(self._b, name)

        blk = _BlkWrap(blk)

        def dma_w(e, S, out, in_):
            e.dma_start(out=out, in_=in_).then_inc(S.h, 16)
            e.wait_ge(S.h, S.inc(16))

        def wait_all(e):
            for s in (LD, GD, SV, SSm, ST, SG, CC, GW[0], GW[1]):
                if s.n:
                    e.wait_ge(s.h, s.n)

        OWN = {"sync": LD, "scalar": SSm, "vector": SV, "gpsimd": SG, "tensor": ST}

        def barrier(engines=("sync", "scalar", "vector", "gpsimd", "tensor")):
            for name in engines:
                def _g(e):
                    wait_all(e)
                getattr(blk, name)(_g)

        def _p0_sync(e):
            for dst, srcap in ((binv, dbin), (lnlw, dlnlw), (lnlb, dlnlb)):
                dma_w(e, LD, dst[:], srcap)
            dma_w(e, LD, gbb[:], dgateb.broadcast_to([128, 4]))
            dma_w(e, LD, m8b[:], dmask8.broadcast_to([128, 8]))
        blk.sync(_p0_sync)

        def _p0_scal(e):
            e.preload_act_table(AF.Silu)
        blk.scalar(_p0_scal)

        def _p0_vec(e):
            e.memset(onesb[:], 1.0)
            e.memset(onesf[:], 1.0)
        blk.vector(_p0_vec)

        def ln_finish(width, inv_n, n_tiles, sig_banks, sq_banks):
            kq = width // 128

            def _evac(e):
                wait_all(e)
                for i in range(n_tiles):
                    e.activation(out=statin[0:1, i * TT:i * TT + TT], in_=sig_banks[i], func=AF.Identity)
                for i in range(n_tiles - 1):
                    e.activation(out=statin[0:1, width + i * TT:width + i * TT + TT],
                                 in_=sq_banks[i], func=AF.Identity)
                i = n_tiles - 1
                e.activation(out=statin[0:1, width + i * TT:width + i * TT + TT],
                             in_=sq_banks[i], func=AF.Identity).then_inc(SSm.h, 1)
            blk.scalar(_evac)
            SSm.inc(1)

            def _bounce1(e):
                e.wait_ge(SSm.h, SSm.n)
                dma_w(e, LD, dlni[0:2 * width].unsqueeze(0), statin[0:1, 0:2 * width])
                dma_w(e, LD, statq[:, :, 0:kq],
                      dlni[0:2 * width].rearrange("(a p k) -> p a k", p=128, a=2))
            blk.sync(_bounce1)

            def _finish(e):
                e.wait_ge(LD.h, LD.n)
                mu = stato[:, 0, 0:kq]
                var = stato[:, 1, 0:kq]
                e.tensor_scalar(out=mu, in0=statq[:, 0, 0:kq], scalar1=inv_n, scalar2=None, op0=ALU.mult)
                e.tensor_scalar(out=var, in0=statq[:, 1, 0:kq], scalar1=inv_n, scalar2=EPS,
                                op0=ALU.mult, op1=ALU.add)
                e.tensor_tensor(out=stata[:, 0:kq], in0=mu, in1=mu, op=ALU.mult)
                e.tensor_tensor(out=var, in0=var, in1=stata[:, 0:kq], op=ALU.subtract)
                _rsqrt(e, statq[:, 0, 0:kq], var, stati[:, 0:kq], stata[:, 0:kq])
                e.tensor_tensor(out=statq[:, 1, 0:kq], in0=mu, in1=statq[:, 0, 0:kq],
                                op=ALU.mult)
            blk.vector(_finish)

            def _bounce2(e):
                e.wait_ge(SV.h, SV.n)
                dma_w(e, LD, dlno[0:2 * width].rearrange("(a p k) -> p a k", p=128, a=2),
                      statq[:, :, 0:kq])
                dma_w(e, LD, rstdb[:, 0:width],
                      dlno[0:width].unsqueeze(0).broadcast_to([128, width]))
                dma_w(e, LD, nub[:, 0:width],
                      dlno[width:2 * width].unsqueeze(0).broadcast_to([128, width]))
            blk.sync(_bounce2)

        def attn_ln_stats():
            st0 = ST.n
            ss0 = SSm.n
            sv0 = SV.n

            def _cst(e, st0=st0):
                wait_all(e)
                for g in range(NGD):
                    if g >= 2:
                        e.wait_ge(ST.h, st0 + g - 1)
                    e.tensor_copy(out=hbt[:, g % 2, :], in_=h[:, g, :])
            blk.vector(_cst)

            def _sq(e, sv0=sv0, st0=st0):
                for g in range(NGD):
                    e.wait_ge(SV.h, sv0 + g + 1)
                    if g >= 2:
                        e.wait_ge(ST.h, st0 + g - 1)
                    e.activation(out=h2t[:, g % 2, :], in_=hbt[:, g % 2, :],
                                 func=AF.Square).then_inc(SSm.h, 1)
            blk.scalar(_sq)
            SSm.inc(NGD)

            def _mm(e, ss0=ss0):
                e.wait_ge(SV.h, 1)
                for g in range(NGD):
                    e.wait_ge(SSm.h, ss0 + g + 1)
                    ins = None
                    for tt in range(NTT):
                        e.matmul(out=ps[tt][0:1, :], lhsT=onesb[:],
                                 rhs=hbt[:, g % 2, tt * TT:tt * TT + TT],
                                 start=(g == 0), stop=(g == NGD - 1))
                        ins = e.matmul(out=ps[4 + tt][0:1, :], lhsT=onesb[:],
                                       rhs=h2t[:, g % 2, tt * TT:tt * TT + TT],
                                       start=(g == 0), stop=(g == NGD - 1))
                    ins.then_inc(ST.h, 1)
            blk.tensor(_mm)
            ST.inc(NGD)
            ln_finish(TOK, 1.0 / DD, NTT, [ps[tt][0:1, :] for tt in range(NTT)],
                      [ps[4 + tt][0:1, :] for tt in range(NTT)])
            barrier(("vector", "scalar", "sync"))

        with (
            nc.sbuf_tensor("xslab", [128, NGV, SLAB], BF16) as xslab,
            nc.sbuf_tensor("x2t", [128, 2, SLAB], BF16) as x2t,
            nc.sbuf_tensor("winb", [128, NGV, DD], BF16) as winb,
            nc.sbuf_tensor("agb", [128, NGV, 68], BF16) as agb,
            nc.sbuf_tensor("usc", [128, SLAB], F32) as usc,
            nc.sbuf_tensor("mrows", [4, TOK], BF16) as mrows,
            nc.sbuf_tensor("scm", [128, 4, TOK // 128], F32) as scm,
            nc.sbuf_tensor("sc2", [128, 2, TOK // 128], F32) as sc2,
            nc.sbuf_tensor("mpk", [128, 4, TOK // 128], BF16) as mpk,
            nc.sbuf_tensor("pat", [4, 64], BF16) as pat,
        ):
            def _w1(e):
                dma_w(e, GD, winb[:], dwin.rearrange("(g p) d -> p g d", p=128))
                dma_w(e, GD, agb[:], dag.rearrange("(g p) d -> p g d", p=128))
            blk.gpsimd(_w1)

            for s in range(NSLAB):
                t0 = s * SLAB

                def _lx(e, t0=t0):
                    dma_w(e, GD, xslab[:],
                          dxT[:, t0:t0 + SLAB].rearrange("(g p) t -> p g t", p=128))
                blk.gpsimd(_lx)
                gd_x = GD.n

                st0 = ST.n
                sq0 = SSm.n

                def _sq1(e, gd_x=gd_x, st0=st0):
                    e.wait_ge(GD.h, gd_x)
                    e.wait_ge(SV.h, 1)
                    for g in range(NGV):
                        if g >= 2:
                            e.wait_ge(ST.h, st0 + g - 1)
                        e.activation(out=x2t[:, g % 2, :], in_=xslab[:, g, :],
                                     func=AF.Square).then_inc(SSm.h, 1)
                blk.scalar(_sq1)
                SSm.inc(NGV)

                def _mms(e, gd_x=gd_x, sq0=sq0):
                    e.wait_ge(GD.h, gd_x)
                    e.wait_ge(SV.h, 1)
                    for g in range(NGV):
                        e.wait_ge(SSm.h, sq0 + g + 1)
                        e.matmul(out=ps[0][0:1, :], lhsT=onesb[:], rhs=xslab[:, g, :],
                                 start=(g == 0), stop=(g == NGV - 1))
                        e.matmul(out=ps[1][0:1, :], lhsT=onesb[:], rhs=x2t[:, g % 2, :],
                                 start=(g == 0), stop=(g == NGV - 1)).then_inc(ST.h, 1)
                blk.tensor(_mms)
                ST.inc(NGV)

                ln_finish(SLAB, 1.0 / VD, 1, [ps[0][0:1, :]], [ps[1][0:1, :]])

                def _xn(e):
                    wait_all(e)
                    ins = None
                    for g4 in range(0, NGV, 4):
                        for j in range(4):
                            e.tensor_tensor(out=ytmp[:, j * SLAB:(j + 1) * SLAB],
                                            in0=xslab[:, g4 + j, :],
                                            in1=rstdb[:, 0:SLAB], op=ALU.mult)
                        for j in range(4):
                            e.tensor_tensor(out=xslab[:, g4 + j, :],
                                            in0=ytmp[:, j * SLAB:(j + 1) * SLAB],
                                            in1=nub[:, 0:SLAB], op=ALU.subtract)
                blk.vector(_xn)

                def _spill(e, t0=t0):
                    e.wait_ge(SV.h, SV.n)
                    dma_w(e, LD, dxn[:, :, t0:t0 + SLAB].rearrange("g p t -> p g t"), xslab[:])
                blk.sync(_spill)

                def _mmw(e):
                    e.wait_ge(SV.h, SV.n)
                    e.wait_ge(GD.h, GD.n)
                    e.wait_ge(SSm.h, SSm.n)
                    for g in range(NGV):
                        for m in range(NGD):
                            e.matmul(out=ps[2 + m][:, :], lhsT=winb[:, g, m * 128:m * 128 + 128],
                                     rhs=xslab[:, g, :], start=(g == 0), stop=(g == NGV - 1))
                        ins = e.matmul(out=ps[6][0:68, :], lhsT=agb[:, g, :], rhs=xslab[:, g, :],
                                       start=(g == 0), stop=(g == NGV - 1))
                    ins.then_inc(ST.h, 1)
                blk.tensor(_mmw)
                ST.inc(1)

                def _evh(e, t0=t0):
                    e.wait_ge(ST.h, ST.n)
                    for m in range(NGD):
                        e.activation(out=h[:, m, t0:t0 + SLAB], in_=ps[2 + m][:, :],
                                     func=AF.Identity, bias=binv[:, m:m + 1])
                    e.activation(out=umask[0:64, t0:t0 + SLAB], in_=ps[6][0:64, :], func=AF.Identity)
                    e.activation(out=usc[64:68, :], in_=ps[6][64:68, :],
                                 func=AF.Identity).then_inc(SSm.h, 1)
                blk.scalar(_evh)
                SSm.inc(1)

                def _ssc(e, t0=t0):
                    e.wait_ge(SSm.h, SSm.n)
                    dma_w(e, LD, dgsc[:, t0:t0 + SLAB], usc[64:68, :])
                blk.sync(_ssc)
                barrier()

            def _mld(e):
                wait_all(e)
                dma_w(e, LD, pat[:], dpat)
                for r in range(4):
                    dma_w(e, LD, scm[:, r, :], dgsc[r].rearrange("(p k) -> p k", p=128))
            blk.sync(_mld)

            def _mcalc(e):
                e.wait_ge(LD.h, LD.n)
                KQ = TOK // 128
                e.tensor_tensor(out=scm[:], in0=scm[:],
                                in1=gbb[:].unsqueeze(2).broadcast_to([128, 4, KQ]), op=ALU.add)
                e.tensor_tensor(out=sc2[:, 0, :], in0=scm[:, 0, :], in1=scm[:, 1, :], op=ALU.max)
                e.tensor_tensor(out=sc2[:, 1, :], in0=scm[:, 2, :], in1=scm[:, 3, :], op=ALU.max)
                e.tensor_tensor(out=sc2[:, 0, :], in0=sc2[:, 0, :], in1=sc2[:, 1, :], op=ALU.max)
                mx = sc2[:, 0, :]
                e.memset(mpk[:, 0, :], 1.0)
                e.tensor_tensor(out=sc2[:, 1, :], in0=scm[:, 0, :], in1=mx, op=ALU.is_lt)
                e.tensor_copy(out=mpk[:, 1, :], in_=sc2[:, 1, :])
                e.tensor_tensor(out=scm[:, 1, :], in0=scm[:, 1, :], in1=mx, op=ALU.is_lt)
                e.tensor_tensor(out=sc2[:, 1, :], in0=sc2[:, 1, :], in1=scm[:, 1, :], op=ALU.mult)
                e.tensor_copy(out=mpk[:, 2, :], in_=sc2[:, 1, :])
                e.tensor_tensor(out=scm[:, 2, :], in0=scm[:, 2, :], in1=mx, op=ALU.is_lt)
                e.tensor_tensor(out=sc2[:, 1, :], in0=sc2[:, 1, :], in1=scm[:, 2, :], op=ALU.mult)
                e.tensor_copy(out=mpk[:, 3, :], in_=sc2[:, 1, :])
            blk.vector(_mcalc)

            def _mst(e):
                e.wait_ge(SV.h, SV.n)
                for r in range(4):
                    dma_w(e, LD, dgm[r].rearrange("(p k) -> p k", p=128), mpk[:, r, :])
                dma_w(e, LD, mrows[:], dgm)
            blk.sync(_mst)

            for tt in range(NTT):
                sv_mm = SV.n

                def _mmm(e, tt=tt, sv_mm=sv_mm):
                    e.wait_ge(LD.h, LD.n)
                    e.wait_ge(SV.h, sv_mm)
                    e.matmul(out=ps[7][0:64, :], lhsT=pat[:], rhs=mrows[:, tt * TT:tt * TT + TT],
                             start=True, stop=True).then_inc(ST.h, 1)
                blk.tensor(_mmm)
                ST.inc(1)

                def _umul(e, tt=tt):
                    e.wait_ge(ST.h, ST.n)
                    e.tensor_tensor(out=umask[0:64, tt * TT:tt * TT + TT],
                                    in0=umask[0:64, tt * TT:tt * TT + TT],
                                    in1=ps[7][0:64, :], op=ALU.mult)
                blk.vector(_umul)
            barrier()

            if debug:
                def _dbg1(e):
                    wait_all(e)
                    dma_w(e, LD, dbg["h0"], h[:])
                    dma_w(e, LD, dbg["um"], umask[:])
                blk.sync(_dbg1)
                barrier()

        if stages < 2:
            def _fin1(e):
                wait_all(e)
            for name in ("sync", "scalar", "vector", "gpsimd", "tensor"):
                getattr(blk, name)(_fin1)
            es2.close()
            return nc
        with (
            nc.sbuf_tensor("dw_", [128, NCH, CK, 2], F32) as dwv,
            nc.sbuf_tensor("midt", [128, NGF, TT], BF16) as midt,
            nc.sbuf_tensor("rint", [128, NGD, CK, 2], F32) as rint,
            nc.sbuf_tensor("dk4", [128, NGD, CK], F32) as dk4,
            nc.sbuf_tensor("w1b", [128, NGD, FF], BF16) as w1b,
            nc.sbuf_tensor("w2b", [128, NGF, DD], BF16) as w2b,
            nc.sbuf_tensor("rhot", [128, CK], F32) as rhot,
        ):
            for l in range(NL):
                def _lc(e, l=l):
                    for dst, srcap in ((ln1w, dln1w), (ln1b, dln1b), (ln2w, dln2w), (ln2b, dln2b),
                                       (sclv, dscl), (b2v, db2), (phr, dphr), (phi, dphi)):
                        dma_w(e, LD, dst[:], srcap[l])
                    dma_w(e, LD, b1v[:], db1[l])
                    dma_w(e, LD, Hre[:], dlcr[l])
                    dma_w(e, LD, Him[:], dlci[l])
                blk.sync(_lc)

                def _lw(e, l=l):
                    dma_w(e, GD, w1b[:], dw1[l].rearrange("(k p) f -> p k f", p=128))
                    dma_w(e, GD, w2b[:], dw2[l].rearrange("(k p) f -> p k f", p=128))
                blk.gpsimd(_lw)

                def _pc(e, l=l):
                    wait_all(e)
                    e.tensor_tensor(out=tA[:], in0=phr[:], in1=phr[:], op=ALU.mult)
                    e.tensor_tensor(out=tB[:], in0=phi[:], in1=phi[:], op=ALU.mult)
                    e.tensor_tensor(out=tA[:], in0=tA[:], in1=tB[:], op=ALU.add)
                    _rsqrt(e, tB[:], tA[:], stati[:, 0:NGD], sct[:])
                    e.tensor_tensor(out=tC[:], in0=tA[:], in1=tB[:], op=ALU.mult)
                    e.tensor_tensor(out=ure4[:], in0=phr[:], in1=tB[:], op=ALU.mult)
                    e.tensor_tensor(out=uim4[:], in0=phi[:], in1=tB[:], op=ALU.mult)
                blk.vector(_pc)

                def _tanh(e):
                    e.wait_ge(SV.h, SV.n)
                    e.activation(out=rho4[:], in_=tC[:], func=AF.Tanh).then_inc(SSm.h, 1)
                blk.scalar(_tanh)
                SSm.inc(1)

                def _tbl(e):
                    e.wait_ge(SSm.h, SSm.n)
                    for g in range(NGD):
                        ri = rint[:, g]
                        e.memset(ri[:, 0:1, 0:1], 1.0)
                        e.memset(ri[:, 0:1, 1:2], 0.0)
                        e.tensor_copy(out=dk4[:, g, 0:1], in_=rho4[:, g:g + 1])
                        e.tensor_copy(out=scs4[:, g, 0:1], in_=ure4[:, g:g + 1])
                        e.tensor_copy(out=scs4[:, g, 1:2], in_=uim4[:, g:g + 1])
                    n = 1
                    while n < CK:
                        m = min(n, CK - n)
                        for g in range(NGD):
                            e.tensor_scalar(out=ytmp[:, g * 512:g * 512 + m], in0=rint[:, g, 0:m, 1],
                                            scalar1=scs4[:, g, 1:2], scalar2=None, op0=ALU.mult)
                        for g in range(NGD):
                            e.scalar_tensor_tensor(out=rint[:, g, n:n + m, 0], in0=rint[:, g, 0:m, 0],
                                                   scalar=scs4[:, g, 0:1], in1=ytmp[:, g * 512:g * 512 + m],
                                                   op0=ALU.mult, op1=ALU.add)
                        for g in range(NGD):
                            e.tensor_scalar(out=ytmp[:, g * 512:g * 512 + m], in0=rint[:, g, 0:m, 0],
                                            scalar1=scs4[:, g, 1:2], scalar2=None, op0=ALU.mult)
                        for g in range(NGD):
                            e.scalar_tensor_tensor(out=rint[:, g, n:n + m, 1], in0=rint[:, g, 0:m, 1],
                                                   scalar=scs4[:, g, 0:1], in1=ytmp[:, g * 512:g * 512 + m],
                                                   op0=ALU.mult, op1=ALU.subtract)
                        for g in range(NGD):
                            e.tensor_scalar(out=dk4[:, g, n:n + m], in0=dk4[:, g, 0:m],
                                            scalar1=dk4[:, g, n - 1:n], scalar2=None, op0=ALU.mult)
                        for g in range(NGD):
                            e.tensor_tensor(out=sct4[:, g, 0:2], in0=scs4[:, g, 0:2],
                                            in1=scs4[:, g, 0:2], op=ALU.mult)
                        for g in range(NGD):
                            e.tensor_tensor(out=sct4[:, g, 2:3], in0=scs4[:, g, 0:1],
                                            in1=scs4[:, g, 1:2], op=ALU.mult)
                        for g in range(NGD):
                            e.tensor_tensor(out=scs4[:, g, 0:1], in0=sct4[:, g, 0:1],
                                            in1=sct4[:, g, 1:2], op=ALU.subtract)
                        for g in range(NGD):
                            e.tensor_scalar(out=scs4[:, g, 1:2], in0=sct4[:, g, 2:3], scalar1=2.0,
                                            scalar2=None, op0=ALU.mult)
                        n *= 2
                    e.tensor_copy(out=u5[:], in_=scs4[:])
                    e.tensor_tensor(out=q4[:, :, 0], in0=scs4[:, :, 0],
                                    in1=dk4[:, :, CK - 1], op=ALU.mult)
                    e.tensor_tensor(out=q4[:, :, 1], in0=scs4[:, :, 1],
                                    in1=dk4[:, :, CK - 1], op=ALU.mult)

                    def csq(dst, src_):
                        e.tensor_tensor(out=tD[:, :, 0], in0=src_[:, :, 0], in1=src_[:, :, 0], op=ALU.mult)
                        e.tensor_tensor(out=tA[:], in0=src_[:, :, 1], in1=src_[:, :, 1], op=ALU.mult)
                        e.tensor_tensor(out=tD[:, :, 1], in0=src_[:, :, 0], in1=src_[:, :, 1], op=ALU.mult)
                        e.tensor_tensor(out=dst[:, :, 0], in0=tD[:, :, 0], in1=tA[:], op=ALU.subtract)
                        e.tensor_scalar(out=dst[:, :, 1], in0=tD[:, :, 1], scalar1=2.0, scalar2=None,
                                        op0=ALU.mult)
                    csq(tE, q4)
                    csq(p2k, tE)
                    e.tensor_scalar(out=pm1[:], in0=p2k[:, :, 0], scalar1=1.0, scalar2=-1.0,
                                    op0=ALU.mult, op1=ALU.add)
                    e.memset(qpow[:, 0, :, 0], 1.0)
                    e.memset(qpow[:, 0, :, 1], 0.0)
                    for k in range(1, NCH):
                        a = qpow[:, k - 1]
                        e.tensor_tensor(out=tD[:, :, 0], in0=a[:, :, 0], in1=q4[:, :, 0], op=ALU.mult)
                        e.tensor_tensor(out=tA[:], in0=a[:, :, 1], in1=q4[:, :, 1], op=ALU.mult)
                        e.tensor_tensor(out=tD[:, :, 1], in0=a[:, :, 0], in1=q4[:, :, 1], op=ALU.mult)
                        e.tensor_tensor(out=tB[:], in0=a[:, :, 1], in1=q4[:, :, 0], op=ALU.mult)
                        e.tensor_tensor(out=qpow[:, k, :, 0], in0=tD[:, :, 0], in1=tA[:], op=ALU.subtract)
                        e.tensor_tensor(out=qpow[:, k, :, 1], in0=tD[:, :, 1], in1=tB[:], op=ALU.add)
                blk.vector(_tbl)
                barrier(("vector", "scalar", "tensor"))

                attn_ln_stats()

                for g in range(NGD):
                    def _nrm(e, g=g):
                        if g == 0:
                            wait_all(e)
                        else:
                            e.wait_ge(SSm.h, SSm.n)
                        e.tensor_tensor(out=ytmp[:], in0=h[:, g, :], in1=rstdb[:], op=ALU.mult)
                        e.tensor_tensor(out=ytmp[:], in0=ytmp[:], in1=nub[:],
                                        op=ALU.subtract)
                    blk.vector(_nrm)

                    def _silu(e, g=g):
                        e.wait_ge(SV.h, SV.n)
                        e.activation(out=ytmp[:], in_=ytmp[:], func=AF.Silu,
                                     scale=ln1w[:, g:g + 1], bias=ln1b[:, g:g + 1]).then_inc(SSm.h, 1)
                    blk.scalar(_silu)
                    SSm.inc(1)

                    def _scan(e, g=g):
                        e.wait_ge(SSm.h, SSm.n)
                        e.tensor_tensor(
                            out=dwv[:],
                            in0=ytmp[:].rearrange("p (c t) -> p c t", c=NCH).unsqueeze(3).broadcast_to([128, NCH, CK, 2]),
                            in1=rint[:, g].unsqueeze(1).broadcast_to([128, NCH, CK, 2]),
                            op=ALU.mult)
                        e.tensor_scalar(out=rhot[:], in0=onesf[:], scalar1=rho4[:, g:g + 1],
                                        scalar2=None, op0=ALU.mult)
                        for comp in range(2):
                            for c in range(NCH):
                                e.tensor_tensor_scan(out=dwv[:, c, :, comp], data0=rhot[:],
                                                     data1=dwv[:, c, :, comp], initial=0.0,
                                                     op0=ALU.mult, op1=ALU.add)
                        e.tensor_copy(out=ends4[:, g], in_=dwv[:, :, CK - 1, :])
                        e.tensor_tensor(out=dwv[:], in0=dwv[:],
                                        in1=rint[:, g].unsqueeze(1).broadcast_to([128, NCH, CK, 2]),
                                        op=ALU.mult)
                        hv = h[:, g, :].rearrange("p (c t) -> p c t", c=NCH)
                        e.scalar_tensor_tensor(out=hv, in0=dwv[:, :, :, 0], scalar=sclv[:, g:g + 1],
                                               in1=hv, op0=ALU.mult, op1=ALU.add)
                        e.scalar_tensor_tensor(out=hv, in0=dwv[:, :, :, 1], scalar=sclv[:, g:g + 1],
                                               in1=hv, op0=ALU.mult, op1=ALU.add)
                    blk.vector(_scan)

                def _chains(e):
                    e.wait_ge(SV.h, SV.n)
                    e.memset(iot[:, :, 0, :], 0.0)
                    for c in range(1, NCH):
                        pi = iot[:, :, c - 1, :]
                        pe = ends4[:, :, c - 1, :]
                        e.tensor_tensor(out=tD[:, :, 0], in0=q4[:, :, 0], in1=pi[:, :, 0], op=ALU.mult)
                        e.tensor_tensor(out=tE[:, :, 0], in0=u5[:, :, 0], in1=pe[:, :, 0], op=ALU.mult)
                        e.tensor_tensor(out=tD[:, :, 1], in0=q4[:, :, 0], in1=pi[:, :, 1], op=ALU.mult)
                        e.tensor_tensor(out=tE[:, :, 1], in0=u5[:, :, 0], in1=pe[:, :, 1], op=ALU.mult)
                        e.tensor_tensor(out=tA[:], in0=q4[:, :, 1], in1=pi[:, :, 1], op=ALU.mult)
                        e.tensor_tensor(out=tB[:], in0=u5[:, :, 1], in1=pe[:, :, 1], op=ALU.mult)
                        e.tensor_tensor(out=tD[:, :, 0], in0=tD[:, :, 0], in1=tA[:], op=ALU.subtract)
                        e.tensor_tensor(out=tE[:, :, 0], in0=tE[:, :, 0], in1=tB[:], op=ALU.subtract)
                        e.tensor_tensor(out=tA[:], in0=q4[:, :, 1], in1=pi[:, :, 0], op=ALU.mult)
                        e.tensor_tensor(out=tB[:], in0=u5[:, :, 1], in1=pe[:, :, 0], op=ALU.mult)
                        e.tensor_tensor(out=tD[:, :, 1], in0=tD[:, :, 1], in1=tA[:], op=ALU.add)
                        e.tensor_tensor(out=tE[:, :, 1], in0=tE[:, :, 1], in1=tB[:], op=ALU.add)
                        e.tensor_tensor(out=iot[:, :, c, :], in0=tD[:, :, :], in1=tE[:, :, :], op=ALU.add)
                    e.tensor_tensor(out=tD[:, :, 0], in0=iot[:, :, NCH - 1, 0],
                                    in1=dk4[:, :, CK - 1], op=ALU.mult)
                    e.tensor_tensor(out=tD[:, :, 1], in0=iot[:, :, NCH - 1, 1],
                                    in1=dk4[:, :, CK - 1], op=ALU.mult)
                    e.tensor_tensor(out=tD[:, :, :], in0=tD[:, :, :],
                                    in1=ends4[:, :, NCH - 1, :], op=ALU.add)
                    e.tensor_copy(out=tF[:, :, 0], in_=rint[:, :, CK - 1, 0])
                    e.tensor_scalar(out=tF[:, :, 1], in0=rint[:, :, CK - 1, 1], scalar1=-1.0,
                                    scalar2=None, op0=ALU.mult)
                    e.tensor_tensor(out=tA[:], in0=tF[:, :, 0], in1=tD[:, :, 0], op=ALU.mult)
                    e.tensor_tensor(out=tB[:], in0=tF[:, :, 1], in1=tD[:, :, 1], op=ALU.mult)
                    e.tensor_tensor(out=tC[:], in0=tF[:, :, 1], in1=tD[:, :, 0], op=ALU.mult)
                    e.tensor_tensor(out=tE[:, :, 1], in0=tF[:, :, 0], in1=tD[:, :, 1], op=ALU.mult)
                    e.tensor_tensor(out=Ev[:, :, 0], in0=tA[:], in1=tB[:], op=ALU.subtract)
                    e.tensor_tensor(out=Ev[:, :, 1], in0=tE[:, :, 1], in1=tC[:], op=ALU.add)
                blk.vector(_chains)

                def _pub(e, l=l):
                    e.wait_ge(SV.h, SV.n)
                    dma_w(e, GD, hcc_i[l].ap(), Ev[:].rearrange("p g c -> p (g c)"))
                    e.collective_compute("AllGather", ALU.bypass, replica_groups=[list(range(NCORE))],
                                         ins=[hcc_i[l].ap().opt()], outs=[hcc_o[l].ap().opt()]
                                         ).then_inc(CC.h, 1)
                    CC.inc(1)
                    e.wait_ge(CC.h, CC.n)
                    dma_w(e, GD, gat[:].rearrange("p s g c -> p s (g c)"),
                          hcc_o[l].ap().rearrange("(s p) c -> p s c", p=128))
                blk.gpsimd(_pub)

                def _comb(e):
                    e.wait_ge(GD.h, GD.n)
                    hbufs = [(Hre[:], Him[:]), (tE[:, :, 0], tE[:, :, 1])]
                    for j in range(NCORE):
                        mj = m8b[:, j:j + 1]
                        hr, hi = hbufs[j % 2]
                        nr, ni = hbufs[(j + 1) % 2]
                        e.tensor_scalar(out=tA[:], in0=pm1[:], scalar1=mj, scalar2=1.0,
                                        op0=ALU.mult, op1=ALU.add)
                        e.tensor_scalar(out=tB[:], in0=p2k[:, :, 1], scalar1=mj, scalar2=None, op0=ALU.mult)
                        e.tensor_scalar(out=tD[:, :, 0], in0=gat[:, j, :, 0], scalar1=mj, scalar2=None,
                                        op0=ALU.mult)
                        e.tensor_scalar(out=tD[:, :, 1], in0=gat[:, j, :, 1], scalar1=mj, scalar2=None,
                                        op0=ALU.mult)
                        e.tensor_tensor(out=tC[:], in0=tA[:], in1=hr, op=ALU.mult)
                        e.tensor_tensor(out=tF[:, :, 0], in0=tB[:], in1=hi, op=ALU.mult)
                        e.tensor_tensor(out=tF[:, :, 1], in0=tA[:], in1=hi, op=ALU.mult)
                        e.tensor_tensor(out=tB[:], in0=tB[:], in1=hr, op=ALU.mult)
                        e.tensor_tensor(out=tD[:, :, 0], in0=tD[:, :, 0], in1=tC[:], op=ALU.add)
                        e.tensor_tensor(out=tD[:, :, 1], in0=tD[:, :, 1], in1=tF[:, :, 1], op=ALU.add)
                        e.tensor_tensor(out=nr, in0=tD[:, :, 0], in1=tF[:, :, 0], op=ALU.subtract)
                        e.tensor_tensor(out=ni, in0=tD[:, :, 1], in1=tB[:], op=ALU.add)
                    hr, hi = hbufs[NCORE % 2]
                    e.tensor_tensor(out=tA[:], in0=ure4[:], in1=hr, op=ALU.mult)
                    e.tensor_tensor(out=tB[:], in0=uim4[:], in1=hi, op=ALU.mult)
                    e.tensor_tensor(out=tC[:], in0=ure4[:], in1=hi, op=ALU.mult)
                    e.tensor_tensor(out=tD[:, :, 0], in0=uim4[:], in1=hr, op=ALU.mult)
                    e.tensor_tensor(out=tA[:], in0=tA[:], in1=tB[:], op=ALU.subtract)
                    e.tensor_tensor(out=tB[:], in0=tC[:], in1=tD[:, :, 0], op=ALU.add)
                    for c in range(NCH):
                        qp = qpow[:, c]
                        e.tensor_tensor(out=tD[:, :, 0], in0=qp[:, :, 0], in1=tA[:], op=ALU.mult)
                        e.tensor_tensor(out=tC[:], in0=qp[:, :, 1], in1=tB[:], op=ALU.mult)
                        e.tensor_tensor(out=tD[:, :, 1], in0=qp[:, :, 0], in1=tB[:], op=ALU.mult)
                        e.tensor_tensor(out=tF[:, :, 0], in0=qp[:, :, 1], in1=tA[:], op=ALU.mult)
                        e.tensor_tensor(out=tD[:, :, 0], in0=tD[:, :, 0], in1=tC[:], op=ALU.subtract)
                        e.tensor_tensor(out=tD[:, :, 1], in0=tD[:, :, 1], in1=tF[:, :, 0], op=ALU.add)
                        for comp in range(2):
                            e.tensor_tensor(out=iot[:, :, c, comp], in0=iot[:, :, c, comp],
                                            in1=tD[:, :, comp], op=ALU.add)
                    e.tensor_tensor(out=iosc[:], in0=iot[:],
                                    in1=sclv[:].unsqueeze(2).unsqueeze(3).broadcast_to([128, NGD, NCH, 2]),
                                    op=ALU.mult)
                blk.vector(_comb)

                def _ctr(e):
                    e.wait_ge(SV.h, SV.n)
                    t12s = [ytmp[:, 0:CK * 2].rearrange("p (t x) -> p t x", x=2),
                            ytmp[:, CK * 2:CK * 4].rearrange("p (t x) -> p t x", x=2)]

                    def build(g):
                        e.tensor_tensor(out=t12s[g % 2], in0=rint[:, g],
                                        in1=dk4[:, g].unsqueeze(2).broadcast_to([128, CK, 2]),
                                        op=ALU.mult)

                    def stts(g):
                        for comp in range(2):
                            for c in range(NCH):
                                hs = h[:, g, c * CK:c * CK + CK]
                                e.scalar_tensor_tensor(out=hs, in0=t12s[g % 2][:, :, comp],
                                                       scalar=iosc[:, g, c, comp:comp + 1],
                                                       in1=hs, op0=ALU.mult, op1=ALU.add)
                    build(0)
                    build(1)
                    stts(0)
                    build(2)
                    stts(1)
                    build(3)
                    stts(2)
                    stts(3)
                blk.vector(_ctr)
                barrier(("vector", "scalar", "tensor"))

                attn_ln_stats()
                for g in range(NGD):
                    def _nr2(e, g=g):
                        if g == 0:
                            wait_all(e)
                        else:
                            e.wait_ge(SSm.h, SSm.n)
                        e.tensor_tensor(out=ytmp[:], in0=h[:, g, :], in1=rstdb[:], op=ALU.mult)
                        e.tensor_tensor(out=ytmp[:], in0=ytmp[:], in1=nub[:],
                                        op=ALU.subtract)
                    blk.vector(_nr2)

                    def _aff(e, g=g):
                        e.wait_ge(SV.h, SV.n)
                        e.activation(out=scrbf[:, g, :], in_=ytmp[:], func=AF.Identity,
                                     scale=ln2w[:, g:g + 1], bias=ln2b[:, g:g + 1]).then_inc(SSm.h, 1)
                    blk.scalar(_aff)
                    SSm.inc(1)
                barrier(("vector", "scalar", "tensor", "gpsimd"))

                for tt in range(NTT):
                    ss_f1 = SSm.n

                    def _mmf1(e, tt=tt, ss_f1=ss_f1):
                        if tt == 0:
                            wait_all(e)
                        else:
                            e.wait_ge(SSm.h, ss_f1)
                        for m in range(NGF):
                            bank = 2 * (m % 2)
                            if m >= 2:
                                e.wait_ge(SSm.h, ss_f1 + m - 1)
                            ins = None
                            for k in range(NGD):
                                ins = e.matmul(out=ps[bank][:, :], lhsT=w1b[:, k, m * 128:m * 128 + 128],
                                               rhs=scrbf[:, k, tt * TT:tt * TT + TT],
                                               start=(k == 0), stop=(k == NGD - 1))
                            ins.then_inc(ST.h, 1)
                    blk.tensor(_mmf1)
                    st_f1 = ST.n
                    ST.inc(NGF)

                    def _evf1(e, st_f1=st_f1):
                        for m in range(NGF):
                            bank = 2 * (m % 2)
                            e.wait_ge(ST.h, st_f1 + m + 1)
                            e.activation(out=midt[:, m, :], in_=ps[bank][:, :], func=AF.Silu,
                                         bias=b1v[:, m:m + 1]).then_inc(SSm.h, 1)
                    blk.scalar(_evf1)
                    SSm.inc(NGF)

                    sv_f2 = SV.n

                    def _mmf2(e, sv_f2=sv_f2):
                        e.wait_ge(SSm.h, SSm.n)
                        e.wait_ge(SV.h, sv_f2)
                        for m in range(NGD):
                            bank = 4 + 2 * (m % 2)
                            if m >= 2:
                                e.wait_ge(SV.h, sv_f2 + m - 1)
                            ins = None
                            for k in range(NGF):
                                ins = e.matmul(out=ps[bank][:, :], lhsT=w2b[:, k, m * 128:m * 128 + 128],
                                               rhs=midt[:, k, :], start=(k == 0), stop=(k == NGF - 1))
                            ins.then_inc(ST.h, 1)
                    blk.tensor(_mmf2)
                    st_f2 = ST.n
                    ST.inc(NGD)

                    def _evf2(e, tt=tt, st_f2=st_f2):
                        for m in range(NGD):
                            bank = 4 + 2 * (m % 2)
                            e.wait_ge(ST.h, st_f2 + m + 1)
                            hs = h[:, m, tt * TT:tt * TT + TT]
                            e.scalar_tensor_tensor(out=hs, in0=ps[bank][:, :], scalar=b2v[:, m:m + 1],
                                                   in1=hs, op0=ALU.add, op1=ALU.add)
                    blk.vector(_evf2)
                barrier()

                if debug:
                    def _dbg2(e, l=l):
                        wait_all(e)
                        dma_w(e, LD, dbg[f"h{l+1}"], h[:])
                    blk.sync(_dbg2)
                    barrier()

            attn_ln_stats()
            for g in range(NGD):
                def _nr4(e, g=g):
                    if g == 0:
                        wait_all(e)
                    else:
                        e.wait_ge(SSm.h, SSm.n)
                    e.tensor_tensor(out=ytmp[:], in0=h[:, g, :], in1=rstdb[:], op=ALU.mult)
                    e.tensor_tensor(out=ytmp[:], in0=ytmp[:], in1=nub[:],
                                    op=ALU.subtract)
                blk.vector(_nr4)

                def _aff4(e, g=g):
                    e.wait_ge(SV.h, SV.n)
                    e.activation(out=scrbf[:, g, :], in_=ytmp[:], func=AF.Identity,
                                 scale=lnlw[:, g:g + 1], bias=lnlb[:, g:g + 1]).then_inc(SSm.h, 1)
                blk.scalar(_aff4)
                SSm.inc(1)
            barrier()

            if debug:
                def _dbg3(e):
                    wait_all(e)
                    dma_w(e, LD, dbg["hn"], scrbf[:])
                blk.sync(_dbg3)
                barrier()

        if stages < 3:
            def _fin2(e):
                wait_all(e)
            for name in ("sync", "scalar", "vector", "gpsimd", "tensor"):
                getattr(blk, name)(_fin2)
            es2.close()
            return nc
        es2.close()
        with (
            nc.sbuf_tensor("xnf", [128, NGV, TOK], BF16) as xnf,
            nc.sbuf_tensor("wtb", [128, 2, NGV, 128], BF16) as wtb,
            nc.sbuf_tensor("woutm", [128, 2, NGD, 128], BF16) as woutm,
            nc.sbuf_tensor("btb", [64, VD], BF16) as btb,
            nc.sbuf_tensor("xm", [128, TOK], F32) as xm,
            nc.sbuf_tensor("outst", [128, 2, TOK], F32) as outst,
            nc.sbuf_tensor("biast", [128, NGV], F32) as biast,
            nc.sbuf_tensor("bias2", [128, NGV], F32) as bias2,
        ):
            def _l3(e):
                for i in range(NGV // 8):
                    dma_w(e, LD, xnf[:, i * 8:(i + 1) * 8, :],
                          dxn[i * 8:(i + 1) * 8].rearrange("g p t -> p g t"))
                dma_w(e, LD, biast[:], dbout)
                dma_w(e, LD, bias2[:], dbff)
            blk.sync(_l3)

            def _l3g(e):
                dma_w(e, GD, btb[:], dbT)
            blk.gpsimd(_l3g)

            def _bsum(e):
                wait_all(e)
                e.tensor_tensor(out=biast[:], in0=biast[:], in1=bias2[:], op=ALU.add)
            blk.vector(_bsum)
            barrier()

            st_mm = []
            for m in range(NGV):
                buf = m % 2
                ms = m * 128

                GWp = GW[buf]

                def _lw3(e, m=m, buf=buf, ms=ms, w=(st_mm[m - 2] if m >= 2 else 0), GWp=GWp):
                    if m >= 2:
                        e.wait_ge(ST.h, w)
                    e.dma_start(out=wtb[:, buf],
                                in_=dwff[:, ms:ms + 128].rearrange("(k p) f -> p k f", p=128)
                                ).then_inc(GWp.h, 16)
                    GWp.inc(16)
                    e.dma_start(out=woutm[:, buf],
                                in_=dwout[:, ms:ms + 128].rearrange("(k p) f -> p k f", p=128)
                                ).then_inc(GWp.h, 16)
                    GWp.inc(16)
                blk.gpsimd(_lw3)

                def _lx3(e, m=m, ms=ms):
                    if m >= 1:
                        e.wait_ge(SV.h, SV.n)
                    dma_w(e, LD, xm[:], dxT[ms:ms + 128, :])
                blk.sync(_lx3)

                gw_m = GW[buf].n
                sv_banks = SV.n

                def _mm5(e, m=m, buf=buf, gw_m=gw_m, sv_banks=sv_banks):
                    e.wait_ge(GW[buf].h, gw_m)
                    e.wait_ge(GD.h, GD.n)
                    if m >= 2:
                        e.wait_ge(SV.h, sv_banks - 4)
                    ins = None
                    for tt in range(NTT):
                        e.matmul(out=ps[4 * buf + tt][:, :], lhsT=btb[0:64, m * 128:m * 128 + 128],
                                 rhs=umask[0:64, tt * TT:tt * TT + TT], start=True, stop=False)
                    for k in range(NGD):
                        for tt in range(NTT):
                            e.matmul(out=ps[4 * buf + tt][:, :], lhsT=woutm[:, buf, k, :],
                                     rhs=scrbf[:, k, tt * TT:tt * TT + TT], start=False, stop=False)
                    for k in range(NGV):
                        for tt in range(NTT):
                            ins = e.matmul(out=ps[4 * buf + tt][:, :], lhsT=wtb[:, buf, k, :],
                                           rhs=xnf[:, k, tt * TT:tt * TT + TT], start=False,
                                           stop=(k == NGV - 1))
                    ins.then_inc(ST.h, 1)
                blk.tensor(_mm5)
                ST.inc(1)
                st_mm.append(ST.n)

                st_m = ST.n
                ld_m = LD.n

                def _ev5(e, m=m, buf=buf, st_m=st_m, ld_m=ld_m):
                    e.wait_ge(ST.h, st_m)
                    e.wait_ge(LD.h, ld_m)
                    for tt in range(NTT):
                        bank = 4 * buf + tt
                        e.scalar_tensor_tensor(out=outst[:, buf, tt * TT:tt * TT + TT],
                                               in0=ps[bank][:, :], scalar=biast[:, m:m + 1],
                                               in1=xm[:, tt * TT:tt * TT + TT],
                                               op0=ALU.add, op1=ALU.add)
                blk.vector(_ev5)

                sv_m = SV.n

                def _st5(e, buf=buf, ms=ms, sv_m=sv_m):
                    e.wait_ge(SV.h, sv_m)
                    dma_w(e, LD, dout[ms:ms + 128, :], outst[:, buf, :])
                blk.sync(_st5)

            def _fin(e):
                e.wait_ge(LD.h, LD.n)
            blk.sync(_fin)

            if debug:
                def _dbg4(e):
                    wait_all(e)
                    dma_w(e, LD, dbg["xn"], xnf[:])
                blk.sync(_dbg4)

    return nc


def _patm():
    import ml_dtypes
    p = np.zeros((4, 64), ml_dtypes.bfloat16)
    p[0, 0:8] = 1
    p[1, 8:16] = 1
    p[2, 16:32] = 1
    p[3, 32:64] = 1
    return p


def _prep(inputs):
    f = np.float32
    x = np.asarray(inputs["L0_attn_x"], f)[0]
    xT = np.ascontiguousarray(x.T)
    Win = np.asarray(inputs["L0_attn_Win"], f)
    A = np.asarray(inputs["L0_ffn_A"], f)
    gw = np.asarray(inputs["L0_ffn_gate_w"], f)
    com = {
        "winT": np.ascontiguousarray(Win.T),
        "agT": np.ascontiguousarray(np.concatenate([A.T, gw.T], axis=1)),
        "w1T": np.ascontiguousarray(np.asarray(inputs["L0_attn_W1"], f).transpose(0, 2, 1)),
        "w2T": np.ascontiguousarray(np.asarray(inputs["L0_attn_W2"], f).transpose(0, 2, 1)),
        "woutT": np.ascontiguousarray(np.asarray(inputs["L0_attn_Wout"], f).T),
        "wT": np.ascontiguousarray(np.asarray(inputs["L0_ffn_W"], f).T),
        "bT": np.ascontiguousarray(np.asarray(inputs["L0_ffn_B"], f).T),
        "bin_r": np.ascontiguousarray(np.asarray(inputs["L0_attn_bin"], f).reshape(NGD, 128).T),
        "b1_r": np.ascontiguousarray(np.asarray(inputs["L0_attn_b1"], f).reshape(NL, NGF, 128).transpose(0, 2, 1)),
        "lnlw_r": np.ascontiguousarray(np.asarray(inputs["L0_attn_lnl_w"], f).reshape(NGD, 128).T),
        "lnlb_r": np.ascontiguousarray(np.asarray(inputs["L0_attn_lnl_b"], f).reshape(NGD, 128).T),
        "bout_r": np.ascontiguousarray(np.asarray(inputs["L0_attn_bout"], f).reshape(NGV, 128).T),
        "bff_r": np.ascontiguousarray(np.asarray(inputs["L0_ffn_b"], f).reshape(NGV, 128).T),
        "gateb_r": np.ascontiguousarray(np.asarray(inputs["L0_ffn_gate_b"], f).reshape(1, 4)),
        "patm": _patm(),
    }
    for nm, key in (("ln1w_r", "L0_attn_ln1_w"), ("ln1b_r", "L0_attn_ln1_b"),
                    ("ln2w_r", "L0_attn_ln2_w"), ("ln2b_r", "L0_attn_ln2_b"),
                    ("scl_r", "L0_attn_sc_lin"), ("b2_r", "L0_attn_b2"),
                    ("phzre_r", "L0_attn_phz_re"), ("phzim_r", "L0_attn_phz_im"),
                    ("lcre_r", "L0_attn_lc_re"), ("lcim_r", "L0_attn_lc_im")):
        com[nm] = np.ascontiguousarray(
            np.asarray(inputs[key], f).reshape(NL, NGD, 128).transpose(0, 2, 1))
    in_maps = []
    for c in range(NCORE):
        mp = dict(com)
        mp["xT"] = np.ascontiguousarray(xT[:, c * TOK:(c + 1) * TOK])
        mp["mask8"] = np.ascontiguousarray(
            np.array([1.0] * c + [0.0] * (NCORE - c), f).reshape(1, 8))
        in_maps.append(mp)
    return in_maps


def kernel(**inputs):
    in_maps = _prep(inputs)
    nc = build_nc(debug=DEBUG, stages=STAGES)
    res = run_bass_kernel_spmd(nc, in_maps, core_ids=list(range(NCORE)))
    out = np.concatenate([np.asarray(res.results[c]["outT"]) for c in range(NCORE)], axis=1)
    if DEBUG:
        kernel._last = res
    return np.ascontiguousarray(out.T).reshape(1, 16384, 4096).astype(np.float32)


# version 06
# speedup vs baseline: 1.1588x; 1.0185x over previous
import contextlib
import numpy as np
import concourse.bass as bass
import concourse.mybir as mybir
from concourse.bass_utils import run_bass_kernel_spmd

F32 = mybir.dt.float32
BF16 = mybir.dt.bfloat16
I32 = mybir.dt.int32
AF = mybir.ActivationFunctionType
ALU = mybir.AluOpType

NCORE = 8
TOK = 2048
VD = 4096
DD = 512
FF = 2048
NL = 4
CK = 512
NCH = TOK // CK
NSLAB = 4
SLAB = TOK // NSLAB
NTT = 4
TT = 512
NGV = VD // 128
NGD = DD // 128
NGF = FF // 128
EPS = 1e-5
RSQRT_MAGIC = 1597463007.0

DEBUG = False
STAGES = 3


class VSafe:

    def __init__(self, e, state):
        self._e = e
        self._s = state

    @staticmethod
    def _names(aps):
        out = []
        for a in aps:
            if a is None or isinstance(a, (int, float)):
                continue
            t = getattr(a, "tensor", None)
            if t is None:
                continue
            try:
                off = a.offset
                ext = 1
                for step, cnt in list(a.ap)[1:]:
                    ext += (cnt - 1) * abs(step)
            except Exception:
                off, ext = 0, 1 << 40
            out.append((t.name, off, off + ext))
        return out

    @staticmethod
    def _hit(a, b):
        for n1, s1, e1 in a:
            for n2, s2, e2 in b:
                if n1 == n2 and s1 < e2 and s2 < e1:
                    return True
        return False

    def _do(self, ins, outs, call, scal=()):
        ii = self._names(ins) + self._names(scal)
        oo = self._names(outs)
        s = self._s
        sv = s["sv"]
        need = 0
        for pi, po, k in s["hist"]:
            if k <= s["cred"]:
                continue
            if self._hit(ii, po) or self._hit(oo, po) or self._hit(oo, pi):
                need = max(need, k)
        if len(s["hist"]) > 192:
            need = max(need, s["hist"][-160][2])
        if need > s["cred"]:
            self._e.wait_ge(sv.h, need)
            s["cred"] = need
            s["hist"] = [h for h in s["hist"] if h[2] > need]
        r = call()
        r.then_inc(sv.h, 1)
        k = sv.inc(1)
        s["hist"].append((ii, oo, k))
        return r

    def tensor_copy(self, out, in_):
        return self._do([in_], [out], lambda: self._e.tensor_copy(out=out, in_=in_))

    def memset(self, ap, c):
        return self._do([], [ap], lambda: self._e.memset(ap, c))

    def tensor_scalar(self, out, in0, scalar1, scalar2, op0, op1=None):
        kw = {} if op1 is None else {"op1": op1}
        return self._do([in0], [out],
                        lambda: self._e.tensor_scalar(out=out, in0=in0, scalar1=scalar1,
                                                      scalar2=scalar2, op0=op0, **kw),
                        scal=[scalar1, scalar2])

    def tensor_tensor(self, out, in0, in1, op):
        return self._do([in0, in1], [out],
                        lambda: self._e.tensor_tensor(out=out, in0=in0, in1=in1, op=op))

    def scalar_tensor_tensor(self, out, in0, scalar, in1, op0, op1):
        return self._do([in0, in1], [out],
                        lambda: self._e.scalar_tensor_tensor(out=out, in0=in0, scalar=scalar,
                                                             in1=in1, op0=op0, op1=op1),
                        scal=[scalar])

    def tensor_tensor_scan(self, out, data0, data1, initial, op0, op1):
        return self._do([data0, data1], [out],
                        lambda: self._e.tensor_tensor_scan(out=out, data0=data0, data1=data1,
                                                           initial=initial, op0=op0, op1=op1),
                        scal=[initial] if not isinstance(initial, float) else ())

    def reciprocal(self, out, in_):
        return self._do([in_], [out], lambda: self._e.reciprocal(out=out, in_=in_))

    def __getattr__(self, name):
        return getattr(self._e, name)


class Sem:
    def __init__(self, es, nc, name):
        self.h = es.enter_context(nc.semaphore(name))
        self.n = 0

    def inc(self, k=1):
        self.n += k
        return self.n


def _rsqrt(e, out, in_, tmp_i, tmp_a):
    n = in_.shape[-1]
    q = max(1, n // 4)
    sl = []
    lo = 0
    while lo < n:
        hi = min(n, lo + q)
        sl.append((lo, hi))
        lo = hi
    def S(ap, s):
        return ap[..., s[0]:s[1]]
    for s in sl:
        e.tensor_copy(out=S(tmp_a, s), in_=S(in_, s).bitcast(I32))
    for s in sl:
        e.tensor_scalar(out=S(tmp_a, s), in0=S(tmp_a, s), scalar1=-0.5 * 2.0 ** -8,
                        scalar2=RSQRT_MAGIC * 2.0 ** -8, op0=ALU.mult, op1=ALU.add)
    for s in sl:
        e.tensor_copy(out=S(tmp_i, s), in_=S(tmp_a, s))
    for s in sl:
        e.tensor_scalar(out=S(tmp_i, s), in0=S(tmp_i, s), scalar1=8, scalar2=None,
                        op0=ALU.arith_shift_left)
    for s in sl:
        e.tensor_copy(out=S(out, s), in_=S(tmp_i, s).bitcast(F32))
    for _ in range(3):
        for s in sl:
            e.tensor_tensor(out=S(tmp_a, s), in0=S(out, s), in1=S(out, s), op=ALU.mult)
        for s in sl:
            e.tensor_tensor(out=S(tmp_a, s), in0=S(tmp_a, s), in1=S(in_, s), op=ALU.mult)
        for s in sl:
            e.tensor_scalar(out=S(tmp_a, s), in0=S(tmp_a, s), scalar1=-0.5, scalar2=1.5,
                            op0=ALU.mult, op1=ALU.add)
        for s in sl:
            e.tensor_tensor(out=S(out, s), in0=S(out, s), in1=S(tmp_a, s), op=ALU.mult)


def build_nc(debug=False, stages=3):
    nc = bass.Bass("TRN2", target_bir_lowering=False)

    dxT = nc.dram_tensor("xT", [VD, TOK], F32, kind="ExternalInput").ap()
    dwin = nc.dram_tensor("winT", [VD, DD], F32, kind="ExternalInput").ap()
    dag = nc.dram_tensor("agT", [VD, 68], F32, kind="ExternalInput").ap()
    dw1 = nc.dram_tensor("w1T", [NL, DD, FF], F32, kind="ExternalInput").ap()
    dw2 = nc.dram_tensor("w2T", [NL, FF, DD], F32, kind="ExternalInput").ap()
    dwout = nc.dram_tensor("woutT", [DD, VD], F32, kind="ExternalInput").ap()
    dwff = nc.dram_tensor("wT", [VD, VD], F32, kind="ExternalInput").ap()
    dbT = nc.dram_tensor("bT", [64, VD], F32, kind="ExternalInput").ap()
    dbin = nc.dram_tensor("bin_r", [128, NGD], F32, kind="ExternalInput").ap()
    dlcon = nc.dram_tensor("lcon_r", [NL, 128, 56], F32, kind="ExternalInput").ap()
    dlnlw = nc.dram_tensor("lnlw_r", [128, NGD], F32, kind="ExternalInput").ap()
    dlnlb = nc.dram_tensor("lnlb_r", [128, NGD], F32, kind="ExternalInput").ap()
    dbout = nc.dram_tensor("bout_r", [128, NGV], F32, kind="ExternalInput").ap()
    dbff = nc.dram_tensor("bff_r", [128, NGV], F32, kind="ExternalInput").ap()
    dgateb = nc.dram_tensor("gateb_r", [1, 4], F32, kind="ExternalInput").ap()
    dmask8 = nc.dram_tensor("mask8", [1, 8], F32, kind="ExternalInput").ap()
    dpat = nc.dram_tensor("patm", [4, 64], BF16, kind="ExternalInput").ap()
    dout = nc.dram_tensor("outT", [VD, TOK], F32, kind="ExternalOutput").ap()

    dxn = nc.dram_tensor("xn_spill", [NGV, 128, TOK], BF16).ap()
    dlni = nc.dram_tensor("lni", [2 * TOK], F32).ap()
    dlno = nc.dram_tensor("lno", [2 * TOK], F32).ap()
    dgsc = nc.dram_tensor("gsc", [4, TOK], F32).ap()
    dgm = nc.dram_tensor("gm", [4, TOK], BF16).ap()
    hcc_i = [nc.dram_tensor(f"cc_i{l}", [128, 8], F32) for l in range(NL)]
    hcc_o = [nc.dram_tensor(f"cc_o{l}", [128 * 8, 8], F32) for l in range(NL)]

    dbg = {}
    if debug:
        dbg["h0"] = nc.dram_tensor("dbg_h0", [128, NGD, TOK], F32, kind="ExternalOutput").ap()
        dbg["xn"] = nc.dram_tensor("dbg_xn", [128, NGV, TOK], BF16, kind="ExternalOutput").ap()
        for l in range(NL):
            dbg[f"h{l+1}"] = nc.dram_tensor(f"dbg_h{l+1}", [128, NGD, TOK], F32, kind="ExternalOutput").ap()
        dbg["hn"] = nc.dram_tensor("dbg_hn", [128, NGD, TOK], BF16, kind="ExternalOutput").ap()
        dbg["um"] = nc.dram_tensor("dbg_um", [64, TOK], BF16, kind="ExternalOutput").ap()

    es = contextlib.ExitStack()
    with es:
        sb = lambda name, shape, dt: es.enter_context(nc.sbuf_tensor(name, shape, dt))
        es2 = contextlib.ExitStack()
        sbm = lambda name, shape, dt: es2.enter_context(nc.sbuf_tensor(name, shape, dt))
        scrbf = sb("scrbf", [128, NGD, TOK], BF16)
        umask = sb("umask", [64, TOK], BF16)
        onesb = sb("onesb", [128, 1], BF16)
        gbb = sb("gbb", [128, 4], F32)
        m8b = sb("m8b", [128, 8], F32)
        h = sbm("h", [128, NGD, TOK], F32)
        onesf = sbm("onesf", [128, CK], F32)
        binv = sbm("binv", [128, NGD], F32)
        lnlw = sbm("lnlw", [128, NGD], F32)
        lnlb = sbm("lnlb", [128, NGD], F32)
        lcon = sbm("lcon", [128, 56], F32)
        rho4 = sbm("rho4", [128, NGD], F32)
        ure4 = sbm("ure4", [128, NGD], F32)
        uim4 = sbm("uim4", [128, NGD], F32)
        q4 = sbm("q4", [128, NGD, 2], F32)
        u5 = sbm("u5", [128, NGD, 2], F32)
        p2k = sbm("p2k", [128, NGD, 2], F32)
        pm1 = sbm("pm1", [128, NGD], F32)
        qpow = sbm("qpow", [128, NCH, NGD, 2], F32)
        iot = sbm("iot", [128, NGD, NCH, 2], F32)
        iosc = sbm("iosc", [128, NGD, NCH, 2], F32)
        Ev = sbm("Ev", [128, NGD, 2], F32)
        gat = sbm("gat", [128, 8, NGD, 2], F32)
        Hre = sbm("Hre", [128, NGD], F32)
        Him = sbm("Him", [128, NGD], F32)
        tA = sbm("tA", [128, NGD], F32)
        tB = sbm("tB", [128, NGD], F32)
        tC = sbm("tC", [128, NGD], F32)
        tD = sbm("tD", [128, NGD, 2], F32)
        tE = sbm("tE", [128, NGD, 2], F32)
        scs = sbm("scs", [128, 2], F32)
        sct = sbm("sct", [128, 4], F32)
        scs4 = sbm("scs4", [128, NGD, 2], F32)
        sct4 = sbm("sct4", [128, NGD, 4], F32)
        ends4 = sbm("ends4", [128, NGD, NCH, 2], F32)
        tF = sbm("tF", [128, NGD, 2], F32)
        statin = sbm("statin", [1, 2 * TOK], F32)
        statq = sbm("statq", [128, 2, TOK // 128], F32)
        stato = sbm("stato", [128, 2, TOK // 128], F32)
        stati = sbm("stati", [128, TOK // 128], I32)
        stata = sbm("stata", [128, TOK // 128], F32)
        rsnu = sbm("rsnu", [128, 2, TOK], F32)
        rstdb = rsnu[:, 0]
        nub = rsnu[:, 1]
        hbt = sbm("hbt", [128, 2, TOK], BF16)
        h2t = sbm("h2t", [128, 2, TOK], BF16)
        ytmp = sbm("ytmp", [128, TOK], F32)
        ps = [es.enter_context(nc.psum_tensor(f"ps{i}", [128, TT], F32)) for i in range(8)]

        LD = Sem(es, nc, "LD")
        GD = Sem(es, nc, "GD")
        SV = Sem(es, nc, "SV")
        SSm = Sem(es, nc, "SS")
        ST = Sem(es, nc, "ST")
        SG = Sem(es, nc, "SG")
        CC = Sem(es, nc, "CC")
        GW = [Sem(es, nc, "GW0"), Sem(es, nc, "GW1")]

        blk = es.enter_context(nc.Block())
        _vstate = {"hist": [], "cred": 0, "sv": SV}

        class _BlkWrap:
            def __init__(self, b):
                self._b = b

            def vector(self, f):
                return self._b.vector(lambda e: f(VSafe(e, _vstate)))

            def __getattr__(self, name):
                return getattr(self._b, name)

        blk = _BlkWrap(blk)

        def dma_w(e, S, out, in_):
            e.dma_start(out=out, in_=in_).then_inc(S.h, 16)
            e.wait_ge(S.h, S.inc(16))

        def wait_all(e):
            for s in (LD, GD, SV, SSm, ST, SG, CC, GW[0], GW[1]):
                if s.n:
                    e.wait_ge(s.h, s.n)

        OWN = {"sync": LD, "scalar": SSm, "vector": SV, "gpsimd": SG, "tensor": ST}

        def barrier(engines=("sync", "scalar", "vector", "gpsimd", "tensor")):
            for name in engines:
                def _g(e):
                    wait_all(e)
                getattr(blk, name)(_g)

        def _p0_sync(e):
            for dst, srcap in ((binv, dbin), (lnlw, dlnlw), (lnlb, dlnlb)):
                dma_w(e, LD, dst[:], srcap)
            dma_w(e, LD, gbb[:], dgateb.broadcast_to([128, 4]))
            dma_w(e, LD, m8b[:], dmask8.broadcast_to([128, 8]))
        blk.sync(_p0_sync)

        def _p0_scal(e):
            e.preload_act_table(AF.Silu)
        blk.scalar(_p0_scal)

        def _p0_vec(e):
            e.memset(onesb[:], 1.0)
            e.memset(onesf[:], 1.0)
        blk.vector(_p0_vec)

        def ln_finish(width, inv_n, n_tiles, sig_banks, sq_banks):
            kq = width // 128

            def _evac(e):
                wait_all(e)
                for i in range(n_tiles):
                    e.activation(out=statin[0:1, i * TT:i * TT + TT], in_=sig_banks[i], func=AF.Identity)
                for i in range(n_tiles - 1):
                    e.activation(out=statin[0:1, width + i * TT:width + i * TT + TT],
                                 in_=sq_banks[i], func=AF.Identity)
                i = n_tiles - 1
                e.activation(out=statin[0:1, width + i * TT:width + i * TT + TT],
                             in_=sq_banks[i], func=AF.Identity).then_inc(SSm.h, 1)
            blk.scalar(_evac)
            SSm.inc(1)

            def _bounce1(e):
                e.wait_ge(SSm.h, SSm.n)
                dma_w(e, LD, dlni[0:2 * width].unsqueeze(0), statin[0:1, 0:2 * width])
                dma_w(e, LD, statq[:, :, 0:kq],
                      dlni[0:2 * width].rearrange("(a p k) -> p a k", p=128, a=2))
            blk.sync(_bounce1)

            def _finish(e):
                e.wait_ge(LD.h, LD.n)
                mu = stato[:, 0, 0:kq]
                var = stato[:, 1, 0:kq]
                e.tensor_scalar(out=mu, in0=statq[:, 0, 0:kq], scalar1=inv_n, scalar2=None, op0=ALU.mult)
                e.tensor_scalar(out=var, in0=statq[:, 1, 0:kq], scalar1=inv_n, scalar2=EPS,
                                op0=ALU.mult, op1=ALU.add)
                e.tensor_tensor(out=stata[:, 0:kq], in0=mu, in1=mu, op=ALU.mult)
                e.tensor_tensor(out=var, in0=var, in1=stata[:, 0:kq], op=ALU.subtract)
                _rsqrt(e, statq[:, 0, 0:kq], var, stati[:, 0:kq], stata[:, 0:kq])
                e.tensor_tensor(out=statq[:, 1, 0:kq], in0=mu, in1=statq[:, 0, 0:kq],
                                op=ALU.mult)
            blk.vector(_finish)

            def _bounce2(e):
                e.wait_ge(SV.h, SV.n)
                dma_w(e, LD, dlno[0:2 * width].rearrange("(a p k) -> p a k", p=128, a=2),
                      statq[:, :, 0:kq])
                dma_w(e, LD, rsnu[:, :, 0:width],
                      dlno[0:2 * width].rearrange("(a t) -> a t", a=2).unsqueeze(0)
                      .broadcast_to([128, 2, width]))
            blk.sync(_bounce2)

        def attn_ln_stats():
            st0 = ST.n
            ss0 = SSm.n
            sv0 = SV.n

            def _cst(e, st0=st0):
                wait_all(e)
                for g in range(NGD):
                    if g >= 2:
                        e.wait_ge(ST.h, st0 + g - 1)
                    e.tensor_copy(out=hbt[:, g % 2, :], in_=h[:, g, :])
            blk.vector(_cst)

            def _sq(e, sv0=sv0, st0=st0):
                for g in range(NGD):
                    e.wait_ge(SV.h, sv0 + g + 1)
                    if g >= 2:
                        e.wait_ge(ST.h, st0 + g - 1)
                    e.activation(out=h2t[:, g % 2, :], in_=hbt[:, g % 2, :],
                                 func=AF.Square).then_inc(SSm.h, 1)
            blk.scalar(_sq)
            SSm.inc(NGD)

            def _mm(e, ss0=ss0):
                e.wait_ge(SV.h, 1)
                for g in range(NGD):
                    e.wait_ge(SSm.h, ss0 + g + 1)
                    ins = None
                    for tt in range(NTT):
                        e.matmul(out=ps[tt][0:1, :], lhsT=onesb[:],
                                 rhs=hbt[:, g % 2, tt * TT:tt * TT + TT],
                                 start=(g == 0), stop=(g == NGD - 1))
                        ins = e.matmul(out=ps[4 + tt][0:1, :], lhsT=onesb[:],
                                       rhs=h2t[:, g % 2, tt * TT:tt * TT + TT],
                                       start=(g == 0), stop=(g == NGD - 1))
                    ins.then_inc(ST.h, 1)
            blk.tensor(_mm)
            ST.inc(NGD)
            ln_finish(TOK, 1.0 / DD, NTT, [ps[tt][0:1, :] for tt in range(NTT)],
                      [ps[4 + tt][0:1, :] for tt in range(NTT)])
            barrier(("vector", "scalar", "sync"))

        with (
            nc.sbuf_tensor("xslab", [128, NGV, SLAB], BF16) as xslab,
            nc.sbuf_tensor("x2t", [128, 2, SLAB], BF16) as x2t,
            nc.sbuf_tensor("winb", [128, NGV, DD], BF16) as winb,
            nc.sbuf_tensor("agb", [128, NGV, 68], BF16) as agb,
            nc.sbuf_tensor("usc", [128, SLAB], F32) as usc,
            nc.sbuf_tensor("mrows", [4, TOK], BF16) as mrows,
            nc.sbuf_tensor("scm", [128, 4, TOK // 128], F32) as scm,
            nc.sbuf_tensor("sc2", [128, 2, TOK // 128], F32) as sc2,
            nc.sbuf_tensor("mpk", [128, 4, TOK // 128], BF16) as mpk,
            nc.sbuf_tensor("pat", [4, 64], BF16) as pat,
        ):
            def _w1(e):
                dma_w(e, GD, winb[:], dwin.rearrange("(g p) d -> p g d", p=128))
                dma_w(e, GD, agb[:], dag.rearrange("(g p) d -> p g d", p=128))
            blk.gpsimd(_w1)

            for s in range(NSLAB):
                t0 = s * SLAB

                def _lx(e, t0=t0):
                    dma_w(e, GD, xslab[:],
                          dxT[:, t0:t0 + SLAB].rearrange("(g p) t -> p g t", p=128))
                blk.gpsimd(_lx)
                gd_x = GD.n

                st0 = ST.n
                sq0 = SSm.n

                def _sq1(e, gd_x=gd_x, st0=st0):
                    e.wait_ge(GD.h, gd_x)
                    e.wait_ge(SV.h, 1)
                    for g in range(NGV):
                        if g >= 2:
                            e.wait_ge(ST.h, st0 + g - 1)
                        e.activation(out=x2t[:, g % 2, :], in_=xslab[:, g, :],
                                     func=AF.Square).then_inc(SSm.h, 1)
                blk.scalar(_sq1)
                SSm.inc(NGV)

                def _mms(e, gd_x=gd_x, sq0=sq0):
                    e.wait_ge(GD.h, gd_x)
                    e.wait_ge(SV.h, 1)
                    for g in range(NGV):
                        e.wait_ge(SSm.h, sq0 + g + 1)
                        e.matmul(out=ps[0][0:1, :], lhsT=onesb[:], rhs=xslab[:, g, :],
                                 start=(g == 0), stop=(g == NGV - 1))
                        e.matmul(out=ps[1][0:1, :], lhsT=onesb[:], rhs=x2t[:, g % 2, :],
                                 start=(g == 0), stop=(g == NGV - 1)).then_inc(ST.h, 1)
                blk.tensor(_mms)
                ST.inc(NGV)

                ln_finish(SLAB, 1.0 / VD, 1, [ps[0][0:1, :]], [ps[1][0:1, :]])

                def _xn(e):
                    wait_all(e)
                    ins = None
                    for g4 in range(0, NGV, 4):
                        for j in range(4):
                            e.tensor_tensor(out=ytmp[:, j * SLAB:(j + 1) * SLAB],
                                            in0=xslab[:, g4 + j, :],
                                            in1=rstdb[:, 0:SLAB], op=ALU.mult)
                        for j in range(4):
                            e.tensor_tensor(out=xslab[:, g4 + j, :],
                                            in0=ytmp[:, j * SLAB:(j + 1) * SLAB],
                                            in1=nub[:, 0:SLAB], op=ALU.subtract)
                blk.vector(_xn)

                def _spill(e, t0=t0):
                    e.wait_ge(SV.h, SV.n)
                    dma_w(e, LD, dxn[:, :, t0:t0 + SLAB].rearrange("g p t -> p g t"), xslab[:])
                blk.sync(_spill)

                def _mmw(e):
                    e.wait_ge(SV.h, SV.n)
                    e.wait_ge(GD.h, GD.n)
                    e.wait_ge(SSm.h, SSm.n)
                    for g in range(NGV):
                        for m in range(NGD):
                            e.matmul(out=ps[2 + m][:, :], lhsT=winb[:, g, m * 128:m * 128 + 128],
                                     rhs=xslab[:, g, :], start=(g == 0), stop=(g == NGV - 1))
                        ins = e.matmul(out=ps[6][0:68, :], lhsT=agb[:, g, :], rhs=xslab[:, g, :],
                                       start=(g == 0), stop=(g == NGV - 1))
                    ins.then_inc(ST.h, 1)
                blk.tensor(_mmw)
                ST.inc(1)

                def _evh(e, t0=t0):
                    e.wait_ge(ST.h, ST.n)
                    for m in range(NGD):
                        e.activation(out=h[:, m, t0:t0 + SLAB], in_=ps[2 + m][:, :],
                                     func=AF.Identity, bias=binv[:, m:m + 1])
                    e.activation(out=umask[0:64, t0:t0 + SLAB], in_=ps[6][0:64, :], func=AF.Identity)
                    e.activation(out=usc[64:68, :], in_=ps[6][64:68, :],
                                 func=AF.Identity).then_inc(SSm.h, 1)
                blk.scalar(_evh)
                SSm.inc(1)

                def _ssc(e, t0=t0):
                    e.wait_ge(SSm.h, SSm.n)
                    dma_w(e, LD, dgsc[:, t0:t0 + SLAB], usc[64:68, :])
                blk.sync(_ssc)
                barrier()

            def _mld(e):
                wait_all(e)
                dma_w(e, LD, pat[:], dpat)
                for r in range(4):
                    dma_w(e, LD, scm[:, r, :], dgsc[r].rearrange("(p k) -> p k", p=128))
            blk.sync(_mld)

            def _mcalc(e):
                e.wait_ge(LD.h, LD.n)
                KQ = TOK // 128
                e.tensor_tensor(out=scm[:], in0=scm[:],
                                in1=gbb[:].unsqueeze(2).broadcast_to([128, 4, KQ]), op=ALU.add)
                e.tensor_tensor(out=sc2[:, 0, :], in0=scm[:, 0, :], in1=scm[:, 1, :], op=ALU.max)
                e.tensor_tensor(out=sc2[:, 1, :], in0=scm[:, 2, :], in1=scm[:, 3, :], op=ALU.max)
                e.tensor_tensor(out=sc2[:, 0, :], in0=sc2[:, 0, :], in1=sc2[:, 1, :], op=ALU.max)
                mx = sc2[:, 0, :]
                e.memset(mpk[:, 0, :], 1.0)
                e.tensor_tensor(out=sc2[:, 1, :], in0=scm[:, 0, :], in1=mx, op=ALU.is_lt)
                e.tensor_copy(out=mpk[:, 1, :], in_=sc2[:, 1, :])
                e.tensor_tensor(out=scm[:, 1, :], in0=scm[:, 1, :], in1=mx, op=ALU.is_lt)
                e.tensor_tensor(out=sc2[:, 1, :], in0=sc2[:, 1, :], in1=scm[:, 1, :], op=ALU.mult)
                e.tensor_copy(out=mpk[:, 2, :], in_=sc2[:, 1, :])
                e.tensor_tensor(out=scm[:, 2, :], in0=scm[:, 2, :], in1=mx, op=ALU.is_lt)
                e.tensor_tensor(out=sc2[:, 1, :], in0=sc2[:, 1, :], in1=scm[:, 2, :], op=ALU.mult)
                e.tensor_copy(out=mpk[:, 3, :], in_=sc2[:, 1, :])
            blk.vector(_mcalc)

            def _mst(e):
                e.wait_ge(SV.h, SV.n)
                for r in range(4):
                    dma_w(e, LD, dgm[r].rearrange("(p k) -> p k", p=128), mpk[:, r, :])
                dma_w(e, LD, mrows[:], dgm)
            blk.sync(_mst)

            for tt in range(NTT):
                sv_mm = SV.n

                def _mmm(e, tt=tt, sv_mm=sv_mm):
                    e.wait_ge(LD.h, LD.n)
                    e.wait_ge(SV.h, sv_mm)
                    e.matmul(out=ps[7][0:64, :], lhsT=pat[:], rhs=mrows[:, tt * TT:tt * TT + TT],
                             start=True, stop=True).then_inc(ST.h, 1)
                blk.tensor(_mmm)
                ST.inc(1)

                def _umul(e, tt=tt):
                    e.wait_ge(ST.h, ST.n)
                    e.tensor_tensor(out=umask[0:64, tt * TT:tt * TT + TT],
                                    in0=umask[0:64, tt * TT:tt * TT + TT],
                                    in1=ps[7][0:64, :], op=ALU.mult)
                blk.vector(_umul)
            barrier()

            if debug:
                def _dbg1(e):
                    wait_all(e)
                    dma_w(e, LD, dbg["h0"], h[:])
                    dma_w(e, LD, dbg["um"], umask[:])
                blk.sync(_dbg1)
                barrier()

        if stages < 2:
            def _fin1(e):
                wait_all(e)
            for name in ("sync", "scalar", "vector", "gpsimd", "tensor"):
                getattr(blk, name)(_fin1)
            es2.close()
            return nc
        with (
            nc.sbuf_tensor("dw_", [128, NCH, CK, 2], F32) as dwv,
            nc.sbuf_tensor("midt", [128, NGF, TT], BF16) as midt,
            nc.sbuf_tensor("rint", [128, NGD, CK, 2], F32) as rint,
            nc.sbuf_tensor("dk4", [128, NGD, CK], F32) as dk4,
            nc.sbuf_tensor("w1b", [128, NGD, FF], BF16) as w1b,
            nc.sbuf_tensor("w2b", [128, NGF, DD], BF16) as w2b,
            nc.sbuf_tensor("rhot", [128, CK], F32) as rhot,
        ):
            for l in range(NL):
                ln1w = lcon[:, 0:4]
                ln1b = lcon[:, 4:8]
                ln2w = lcon[:, 8:12]
                ln2b = lcon[:, 12:16]
                sclv = lcon[:, 16:20]
                b2v = lcon[:, 20:24]
                phr = lcon[:, 24:28]
                phi = lcon[:, 28:32]
                lcr = lcon[:, 32:36]
                lci = lcon[:, 36:40]
                b1v = lcon[:, 40:56]

                def _lc(e, l=l):
                    dma_w(e, LD, lcon[:], dlcon[l])
                blk.sync(_lc)

                def _lw(e, l=l):
                    dma_w(e, GD, w1b[:], dw1[l].rearrange("(k p) f -> p k f", p=128))
                    dma_w(e, GD, w2b[:], dw2[l].rearrange("(k p) f -> p k f", p=128))
                blk.gpsimd(_lw)

                def _pc(e, l=l):
                    wait_all(e)
                    e.tensor_copy(out=Hre[:], in_=lcr)
                    e.tensor_copy(out=Him[:], in_=lci)
                    e.tensor_tensor(out=tA[:], in0=phr[:], in1=phr[:], op=ALU.mult)
                    e.tensor_tensor(out=tB[:], in0=phi[:], in1=phi[:], op=ALU.mult)
                    e.tensor_tensor(out=tA[:], in0=tA[:], in1=tB[:], op=ALU.add)
                    _rsqrt(e, tB[:], tA[:], stati[:, 0:NGD], sct[:])
                    e.tensor_tensor(out=tC[:], in0=tA[:], in1=tB[:], op=ALU.mult)
                    e.tensor_tensor(out=ure4[:], in0=phr[:], in1=tB[:], op=ALU.mult)
                    e.tensor_tensor(out=uim4[:], in0=phi[:], in1=tB[:], op=ALU.mult)
                blk.vector(_pc)

                def _tanh(e):
                    e.wait_ge(SV.h, SV.n)
                    e.activation(out=rho4[:], in_=tC[:], func=AF.Tanh).then_inc(SSm.h, 1)
                blk.scalar(_tanh)
                SSm.inc(1)

                def _tbl(e):
                    e.wait_ge(SSm.h, SSm.n)
                    for g in range(NGD):
                        ri = rint[:, g]
                        e.memset(ri[:, 0:1, 0:1], 1.0)
                        e.memset(ri[:, 0:1, 1:2], 0.0)
                        e.tensor_copy(out=dk4[:, g, 0:1], in_=rho4[:, g:g + 1])
                        e.tensor_copy(out=scs4[:, g, 0:1], in_=ure4[:, g:g + 1])
                        e.tensor_copy(out=scs4[:, g, 1:2], in_=uim4[:, g:g + 1])
                    n = 1
                    while n < CK:
                        m = min(n, CK - n)
                        for g in range(NGD):
                            e.tensor_scalar(out=ytmp[:, g * 512:g * 512 + m], in0=rint[:, g, 0:m, 1],
                                            scalar1=scs4[:, g, 1:2], scalar2=None, op0=ALU.mult)
                        for g in range(NGD):
                            e.scalar_tensor_tensor(out=rint[:, g, n:n + m, 0], in0=rint[:, g, 0:m, 0],
                                                   scalar=scs4[:, g, 0:1], in1=ytmp[:, g * 512:g * 512 + m],
                                                   op0=ALU.mult, op1=ALU.add)
                        for g in range(NGD):
                            e.tensor_scalar(out=ytmp[:, g * 512:g * 512 + m], in0=rint[:, g, 0:m, 0],
                                            scalar1=scs4[:, g, 1:2], scalar2=None, op0=ALU.mult)
                        for g in range(NGD):
                            e.scalar_tensor_tensor(out=rint[:, g, n:n + m, 1], in0=rint[:, g, 0:m, 1],
                                                   scalar=scs4[:, g, 0:1], in1=ytmp[:, g * 512:g * 512 + m],
                                                   op0=ALU.mult, op1=ALU.subtract)
                        for g in range(NGD):
                            e.tensor_scalar(out=dk4[:, g, n:n + m], in0=dk4[:, g, 0:m],
                                            scalar1=dk4[:, g, n - 1:n], scalar2=None, op0=ALU.mult)
                        for g in range(NGD):
                            e.tensor_tensor(out=sct4[:, g, 0:2], in0=scs4[:, g, 0:2],
                                            in1=scs4[:, g, 0:2], op=ALU.mult)
                        for g in range(NGD):
                            e.tensor_tensor(out=sct4[:, g, 2:3], in0=scs4[:, g, 0:1],
                                            in1=scs4[:, g, 1:2], op=ALU.mult)
                        for g in range(NGD):
                            e.tensor_tensor(out=scs4[:, g, 0:1], in0=sct4[:, g, 0:1],
                                            in1=sct4[:, g, 1:2], op=ALU.subtract)
                        for g in range(NGD):
                            e.tensor_scalar(out=scs4[:, g, 1:2], in0=sct4[:, g, 2:3], scalar1=2.0,
                                            scalar2=None, op0=ALU.mult)
                        n *= 2
                    e.tensor_copy(out=u5[:], in_=scs4[:])
                    e.tensor_tensor(out=q4[:, :, 0], in0=scs4[:, :, 0],
                                    in1=dk4[:, :, CK - 1], op=ALU.mult)
                    e.tensor_tensor(out=q4[:, :, 1], in0=scs4[:, :, 1],
                                    in1=dk4[:, :, CK - 1], op=ALU.mult)

                    def csq(dst, src_):
                        e.tensor_tensor(out=tD[:, :, 0], in0=src_[:, :, 0], in1=src_[:, :, 0], op=ALU.mult)
                        e.tensor_tensor(out=tA[:], in0=src_[:, :, 1], in1=src_[:, :, 1], op=ALU.mult)
                        e.tensor_tensor(out=tD[:, :, 1], in0=src_[:, :, 0], in1=src_[:, :, 1], op=ALU.mult)
                        e.tensor_tensor(out=dst[:, :, 0], in0=tD[:, :, 0], in1=tA[:], op=ALU.subtract)
                        e.tensor_scalar(out=dst[:, :, 1], in0=tD[:, :, 1], scalar1=2.0, scalar2=None,
                                        op0=ALU.mult)
                    csq(tE, q4)
                    csq(p2k, tE)
                    e.tensor_scalar(out=pm1[:], in0=p2k[:, :, 0], scalar1=1.0, scalar2=-1.0,
                                    op0=ALU.mult, op1=ALU.add)
                    e.memset(qpow[:, 0, :, 0], 1.0)
                    e.memset(qpow[:, 0, :, 1], 0.0)
                    for k in range(1, NCH):
                        a = qpow[:, k - 1]
                        e.tensor_tensor(out=tD[:, :, 0], in0=a[:, :, 0], in1=q4[:, :, 0], op=ALU.mult)
                        e.tensor_tensor(out=tA[:], in0=a[:, :, 1], in1=q4[:, :, 1], op=ALU.mult)
                        e.tensor_tensor(out=tD[:, :, 1], in0=a[:, :, 0], in1=q4[:, :, 1], op=ALU.mult)
                        e.tensor_tensor(out=tB[:], in0=a[:, :, 1], in1=q4[:, :, 0], op=ALU.mult)
                        e.tensor_tensor(out=qpow[:, k, :, 0], in0=tD[:, :, 0], in1=tA[:], op=ALU.subtract)
                        e.tensor_tensor(out=qpow[:, k, :, 1], in0=tD[:, :, 1], in1=tB[:], op=ALU.add)
                blk.vector(_tbl)
                barrier(("vector", "scalar", "tensor"))

                attn_ln_stats()

                for g in range(NGD):
                    def _nrm(e, g=g):
                        if g == 0:
                            wait_all(e)
                        else:
                            e.wait_ge(SSm.h, SSm.n)
                        e.tensor_tensor(out=ytmp[:], in0=h[:, g, :], in1=rstdb[:], op=ALU.mult)
                        e.tensor_tensor(out=ytmp[:], in0=ytmp[:], in1=nub[:],
                                        op=ALU.subtract)
                    blk.vector(_nrm)

                    def _silu(e, g=g):
                        e.wait_ge(SV.h, SV.n)
                        e.activation(out=ytmp[:], in_=ytmp[:], func=AF.Silu,
                                     scale=ln1w[:, g:g + 1], bias=ln1b[:, g:g + 1]).then_inc(SSm.h, 1)
                    blk.scalar(_silu)
                    SSm.inc(1)

                    def _scan(e, g=g):
                        e.wait_ge(SSm.h, SSm.n)
                        e.tensor_tensor(
                            out=dwv[:],
                            in0=ytmp[:].rearrange("p (c t) -> p c t", c=NCH).unsqueeze(3).broadcast_to([128, NCH, CK, 2]),
                            in1=rint[:, g].unsqueeze(1).broadcast_to([128, NCH, CK, 2]),
                            op=ALU.mult)
                        e.tensor_scalar(out=rhot[:], in0=onesf[:], scalar1=rho4[:, g:g + 1],
                                        scalar2=None, op0=ALU.mult)
                        for comp in range(2):
                            for c in range(NCH):
                                e.tensor_tensor_scan(out=dwv[:, c, :, comp], data0=rhot[:],
                                                     data1=dwv[:, c, :, comp], initial=0.0,
                                                     op0=ALU.mult, op1=ALU.add)
                        e.tensor_copy(out=ends4[:, g], in_=dwv[:, :, CK - 1, :])
                        e.tensor_tensor(out=dwv[:], in0=dwv[:],
                                        in1=rint[:, g].unsqueeze(1).broadcast_to([128, NCH, CK, 2]),
                                        op=ALU.mult)
                        hv = h[:, g, :].rearrange("p (c t) -> p c t", c=NCH)
                        e.scalar_tensor_tensor(out=hv, in0=dwv[:, :, :, 0], scalar=sclv[:, g:g + 1],
                                               in1=hv, op0=ALU.mult, op1=ALU.add)
                        e.scalar_tensor_tensor(out=hv, in0=dwv[:, :, :, 1], scalar=sclv[:, g:g + 1],
                                               in1=hv, op0=ALU.mult, op1=ALU.add)
                    blk.vector(_scan)

                def _chains(e):
                    e.wait_ge(SV.h, SV.n)
                    e.memset(iot[:, :, 0, :], 0.0)
                    for c in range(1, NCH):
                        pi = iot[:, :, c - 1, :]
                        pe = ends4[:, :, c - 1, :]
                        e.tensor_tensor(out=tD[:, :, 0], in0=q4[:, :, 0], in1=pi[:, :, 0], op=ALU.mult)
                        e.tensor_tensor(out=tE[:, :, 0], in0=u5[:, :, 0], in1=pe[:, :, 0], op=ALU.mult)
                        e.tensor_tensor(out=tD[:, :, 1], in0=q4[:, :, 0], in1=pi[:, :, 1], op=ALU.mult)
                        e.tensor_tensor(out=tE[:, :, 1], in0=u5[:, :, 0], in1=pe[:, :, 1], op=ALU.mult)
                        e.tensor_tensor(out=tA[:], in0=q4[:, :, 1], in1=pi[:, :, 1], op=ALU.mult)
                        e.tensor_tensor(out=tB[:], in0=u5[:, :, 1], in1=pe[:, :, 1], op=ALU.mult)
                        e.tensor_tensor(out=tD[:, :, 0], in0=tD[:, :, 0], in1=tA[:], op=ALU.subtract)
                        e.tensor_tensor(out=tE[:, :, 0], in0=tE[:, :, 0], in1=tB[:], op=ALU.subtract)
                        e.tensor_tensor(out=tA[:], in0=q4[:, :, 1], in1=pi[:, :, 0], op=ALU.mult)
                        e.tensor_tensor(out=tB[:], in0=u5[:, :, 1], in1=pe[:, :, 0], op=ALU.mult)
                        e.tensor_tensor(out=tD[:, :, 1], in0=tD[:, :, 1], in1=tA[:], op=ALU.add)
                        e.tensor_tensor(out=tE[:, :, 1], in0=tE[:, :, 1], in1=tB[:], op=ALU.add)
                        e.tensor_tensor(out=iot[:, :, c, :], in0=tD[:, :, :], in1=tE[:, :, :], op=ALU.add)
                    e.tensor_tensor(out=tD[:, :, 0], in0=iot[:, :, NCH - 1, 0],
                                    in1=dk4[:, :, CK - 1], op=ALU.mult)
                    e.tensor_tensor(out=tD[:, :, 1], in0=iot[:, :, NCH - 1, 1],
                                    in1=dk4[:, :, CK - 1], op=ALU.mult)
                    e.tensor_tensor(out=tD[:, :, :], in0=tD[:, :, :],
                                    in1=ends4[:, :, NCH - 1, :], op=ALU.add)
                    e.tensor_copy(out=tF[:, :, 0], in_=rint[:, :, CK - 1, 0])
                    e.tensor_scalar(out=tF[:, :, 1], in0=rint[:, :, CK - 1, 1], scalar1=-1.0,
                                    scalar2=None, op0=ALU.mult)
                    e.tensor_tensor(out=tA[:], in0=tF[:, :, 0], in1=tD[:, :, 0], op=ALU.mult)
                    e.tensor_tensor(out=tB[:], in0=tF[:, :, 1], in1=tD[:, :, 1], op=ALU.mult)
                    e.tensor_tensor(out=tC[:], in0=tF[:, :, 1], in1=tD[:, :, 0], op=ALU.mult)
                    e.tensor_tensor(out=tE[:, :, 1], in0=tF[:, :, 0], in1=tD[:, :, 1], op=ALU.mult)
                    e.tensor_tensor(out=Ev[:, :, 0], in0=tA[:], in1=tB[:], op=ALU.subtract)
                    e.tensor_tensor(out=Ev[:, :, 1], in0=tE[:, :, 1], in1=tC[:], op=ALU.add)
                blk.vector(_chains)

                def _pub(e, l=l):
                    e.wait_ge(SV.h, SV.n)
                    dma_w(e, GD, hcc_i[l].ap(), Ev[:].rearrange("p g c -> p (g c)"))
                    e.collective_compute("AllGather", ALU.bypass, replica_groups=[list(range(NCORE))],
                                         ins=[hcc_i[l].ap().opt()], outs=[hcc_o[l].ap().opt()]
                                         ).then_inc(CC.h, 1)
                    CC.inc(1)
                    e.wait_ge(CC.h, CC.n)
                    dma_w(e, GD, gat[:].rearrange("p s g c -> p s (g c)"),
                          hcc_o[l].ap().rearrange("(s p) c -> p s c", p=128))
                blk.gpsimd(_pub)

                def _comb(e):
                    e.wait_ge(GD.h, GD.n)
                    hbufs = [(Hre[:], Him[:]), (tE[:, :, 0], tE[:, :, 1])]
                    for j in range(NCORE):
                        mj = m8b[:, j:j + 1]
                        hr, hi = hbufs[j % 2]
                        nr, ni = hbufs[(j + 1) % 2]
                        e.tensor_scalar(out=tA[:], in0=pm1[:], scalar1=mj, scalar2=1.0,
                                        op0=ALU.mult, op1=ALU.add)
                        e.tensor_scalar(out=tB[:], in0=p2k[:, :, 1], scalar1=mj, scalar2=None, op0=ALU.mult)
                        e.tensor_scalar(out=tD[:, :, 0], in0=gat[:, j, :, 0], scalar1=mj, scalar2=None,
                                        op0=ALU.mult)
                        e.tensor_scalar(out=tD[:, :, 1], in0=gat[:, j, :, 1], scalar1=mj, scalar2=None,
                                        op0=ALU.mult)
                        e.tensor_tensor(out=tC[:], in0=tA[:], in1=hr, op=ALU.mult)
                        e.tensor_tensor(out=tF[:, :, 0], in0=tB[:], in1=hi, op=ALU.mult)
                        e.tensor_tensor(out=tF[:, :, 1], in0=tA[:], in1=hi, op=ALU.mult)
                        e.tensor_tensor(out=tB[:], in0=tB[:], in1=hr, op=ALU.mult)
                        e.tensor_tensor(out=tD[:, :, 0], in0=tD[:, :, 0], in1=tC[:], op=ALU.add)
                        e.tensor_tensor(out=tD[:, :, 1], in0=tD[:, :, 1], in1=tF[:, :, 1], op=ALU.add)
                        e.tensor_tensor(out=nr, in0=tD[:, :, 0], in1=tF[:, :, 0], op=ALU.subtract)
                        e.tensor_tensor(out=ni, in0=tD[:, :, 1], in1=tB[:], op=ALU.add)
                    hr, hi = hbufs[NCORE % 2]
                    e.tensor_tensor(out=tA[:], in0=ure4[:], in1=hr, op=ALU.mult)
                    e.tensor_tensor(out=tB[:], in0=uim4[:], in1=hi, op=ALU.mult)
                    e.tensor_tensor(out=tC[:], in0=ure4[:], in1=hi, op=ALU.mult)
                    e.tensor_tensor(out=tD[:, :, 0], in0=uim4[:], in1=hr, op=ALU.mult)
                    e.tensor_tensor(out=tA[:], in0=tA[:], in1=tB[:], op=ALU.subtract)
                    e.tensor_tensor(out=tB[:], in0=tC[:], in1=tD[:, :, 0], op=ALU.add)
                    for c in range(NCH):
                        qp = qpow[:, c]
                        e.tensor_tensor(out=tD[:, :, 0], in0=qp[:, :, 0], in1=tA[:], op=ALU.mult)
                        e.tensor_tensor(out=tC[:], in0=qp[:, :, 1], in1=tB[:], op=ALU.mult)
                        e.tensor_tensor(out=tD[:, :, 1], in0=qp[:, :, 0], in1=tB[:], op=ALU.mult)
                        e.tensor_tensor(out=tF[:, :, 0], in0=qp[:, :, 1], in1=tA[:], op=ALU.mult)
                        e.tensor_tensor(out=tD[:, :, 0], in0=tD[:, :, 0], in1=tC[:], op=ALU.subtract)
                        e.tensor_tensor(out=tD[:, :, 1], in0=tD[:, :, 1], in1=tF[:, :, 0], op=ALU.add)
                        for comp in range(2):
                            e.tensor_tensor(out=iot[:, :, c, comp], in0=iot[:, :, c, comp],
                                            in1=tD[:, :, comp], op=ALU.add)
                    e.tensor_tensor(out=iosc[:], in0=iot[:],
                                    in1=sclv[:].unsqueeze(2).unsqueeze(3).broadcast_to([128, NGD, NCH, 2]),
                                    op=ALU.mult)
                blk.vector(_comb)

                def _ctr(e):
                    e.wait_ge(SV.h, SV.n)
                    t12s = [ytmp[:, 0:CK * 2].rearrange("p (t x) -> p t x", x=2),
                            ytmp[:, CK * 2:CK * 4].rearrange("p (t x) -> p t x", x=2)]

                    def build(g):
                        e.tensor_tensor(out=t12s[g % 2], in0=rint[:, g],
                                        in1=dk4[:, g].unsqueeze(2).broadcast_to([128, CK, 2]),
                                        op=ALU.mult)

                    def stts(g):
                        for comp in range(2):
                            for c in range(NCH):
                                hs = h[:, g, c * CK:c * CK + CK]
                                e.scalar_tensor_tensor(out=hs, in0=t12s[g % 2][:, :, comp],
                                                       scalar=iosc[:, g, c, comp:comp + 1],
                                                       in1=hs, op0=ALU.mult, op1=ALU.add)
                    build(0)
                    build(1)
                    stts(0)
                    build(2)
                    stts(1)
                    build(3)
                    stts(2)
                    stts(3)
                blk.vector(_ctr)
                barrier(("vector", "scalar", "tensor"))

                attn_ln_stats()
                for g in range(NGD):
                    def _nr2(e, g=g):
                        if g == 0:
                            wait_all(e)
                        else:
                            e.wait_ge(SSm.h, SSm.n)
                        e.tensor_tensor(out=ytmp[:], in0=h[:, g, :], in1=rstdb[:], op=ALU.mult)
                        e.tensor_tensor(out=ytmp[:], in0=ytmp[:], in1=nub[:],
                                        op=ALU.subtract)
                    blk.vector(_nr2)

                    def _aff(e, g=g):
                        e.wait_ge(SV.h, SV.n)
                        e.activation(out=scrbf[:, g, :], in_=ytmp[:], func=AF.Identity,
                                     scale=ln2w[:, g:g + 1], bias=ln2b[:, g:g + 1]).then_inc(SSm.h, 1)
                    blk.scalar(_aff)
                    SSm.inc(1)
                barrier(("vector", "scalar", "tensor", "gpsimd"))

                for tt in range(NTT):
                    ss_f1 = SSm.n

                    def _mmf1(e, tt=tt, ss_f1=ss_f1):
                        if tt == 0:
                            wait_all(e)
                        else:
                            e.wait_ge(SSm.h, ss_f1)
                        for m in range(NGF):
                            bank = 2 * (m % 2)
                            if m >= 2:
                                e.wait_ge(SSm.h, ss_f1 + m - 1)
                            ins = None
                            for k in range(NGD):
                                ins = e.matmul(out=ps[bank][:, :], lhsT=w1b[:, k, m * 128:m * 128 + 128],
                                               rhs=scrbf[:, k, tt * TT:tt * TT + TT],
                                               start=(k == 0), stop=(k == NGD - 1))
                            ins.then_inc(ST.h, 1)
                    blk.tensor(_mmf1)
                    st_f1 = ST.n
                    ST.inc(NGF)

                    def _evf1(e, st_f1=st_f1):
                        for m in range(NGF):
                            bank = 2 * (m % 2)
                            e.wait_ge(ST.h, st_f1 + m + 1)
                            e.activation(out=midt[:, m, :], in_=ps[bank][:, :], func=AF.Silu,
                                         bias=b1v[:, m:m + 1]).then_inc(SSm.h, 1)
                    blk.scalar(_evf1)
                    SSm.inc(NGF)

                    sv_f2 = SV.n

                    def _mmf2(e, sv_f2=sv_f2):
                        e.wait_ge(SSm.h, SSm.n)
                        e.wait_ge(SV.h, sv_f2)
                        for m in range(NGD):
                            bank = 4 + 2 * (m % 2)
                            if m >= 2:
                                e.wait_ge(SV.h, sv_f2 + m - 1)
                            ins = None
                            for k in range(NGF):
                                ins = e.matmul(out=ps[bank][:, :], lhsT=w2b[:, k, m * 128:m * 128 + 128],
                                               rhs=midt[:, k, :], start=(k == 0), stop=(k == NGF - 1))
                            ins.then_inc(ST.h, 1)
                    blk.tensor(_mmf2)
                    st_f2 = ST.n
                    ST.inc(NGD)

                    def _evf2(e, tt=tt, st_f2=st_f2):
                        for m in range(NGD):
                            bank = 4 + 2 * (m % 2)
                            e.wait_ge(ST.h, st_f2 + m + 1)
                            hs = h[:, m, tt * TT:tt * TT + TT]
                            e.scalar_tensor_tensor(out=hs, in0=ps[bank][:, :], scalar=b2v[:, m:m + 1],
                                                   in1=hs, op0=ALU.add, op1=ALU.add)
                    blk.vector(_evf2)
                barrier()

                if debug:
                    def _dbg2(e, l=l):
                        wait_all(e)
                        dma_w(e, LD, dbg[f"h{l+1}"], h[:])
                    blk.sync(_dbg2)
                    barrier()

            attn_ln_stats()
            for g in range(NGD):
                def _nr4(e, g=g):
                    if g == 0:
                        wait_all(e)
                    else:
                        e.wait_ge(SSm.h, SSm.n)
                    e.tensor_tensor(out=ytmp[:], in0=h[:, g, :], in1=rstdb[:], op=ALU.mult)
                    e.tensor_tensor(out=ytmp[:], in0=ytmp[:], in1=nub[:],
                                    op=ALU.subtract)
                blk.vector(_nr4)

                def _aff4(e, g=g):
                    e.wait_ge(SV.h, SV.n)
                    e.activation(out=scrbf[:, g, :], in_=ytmp[:], func=AF.Identity,
                                 scale=lnlw[:, g:g + 1], bias=lnlb[:, g:g + 1]).then_inc(SSm.h, 1)
                blk.scalar(_aff4)
                SSm.inc(1)
            barrier()

            if debug:
                def _dbg3(e):
                    wait_all(e)
                    dma_w(e, LD, dbg["hn"], scrbf[:])
                blk.sync(_dbg3)
                barrier()

        if stages < 3:
            def _fin2(e):
                wait_all(e)
            for name in ("sync", "scalar", "vector", "gpsimd", "tensor"):
                getattr(blk, name)(_fin2)
            es2.close()
            return nc
        es2.close()
        with (
            nc.sbuf_tensor("xnf", [128, NGV, TOK], BF16) as xnf,
            nc.sbuf_tensor("wtb", [128, 2, NGV, 128], BF16) as wtb,
            nc.sbuf_tensor("woutm", [128, 2, NGD, 128], BF16) as woutm,
            nc.sbuf_tensor("btb", [64, VD], BF16) as btb,
            nc.sbuf_tensor("xm", [128, TOK], F32) as xm,
            nc.sbuf_tensor("outst", [128, 2, TOK], F32) as outst,
            nc.sbuf_tensor("biast", [128, NGV], F32) as biast,
            nc.sbuf_tensor("bias2", [128, NGV], F32) as bias2,
        ):
            def _l3(e):
                for i in range(NGV // 8):
                    dma_w(e, LD, xnf[:, i * 8:(i + 1) * 8, :],
                          dxn[i * 8:(i + 1) * 8].rearrange("g p t -> p g t"))
                dma_w(e, LD, biast[:], dbout)
                dma_w(e, LD, bias2[:], dbff)
            blk.sync(_l3)

            def _l3g(e):
                dma_w(e, GD, btb[:], dbT)
            blk.gpsimd(_l3g)

            def _bsum(e):
                wait_all(e)
                e.tensor_tensor(out=biast[:], in0=biast[:], in1=bias2[:], op=ALU.add)
            blk.vector(_bsum)
            barrier()

            st_mm = []
            for m in range(NGV):
                buf = m % 2
                ms = m * 128

                GWp = GW[buf]

                def _lw3(e, m=m, buf=buf, ms=ms, w=(st_mm[m - 2] if m >= 2 else 0), GWp=GWp):
                    if m >= 2:
                        e.wait_ge(ST.h, w)
                    e.dma_start(out=wtb[:, buf],
                                in_=dwff[:, ms:ms + 128].rearrange("(k p) f -> p k f", p=128)
                                ).then_inc(GWp.h, 16)
                    GWp.inc(16)
                    e.dma_start(out=woutm[:, buf],
                                in_=dwout[:, ms:ms + 128].rearrange("(k p) f -> p k f", p=128)
                                ).then_inc(GWp.h, 16)
                    GWp.inc(16)
                blk.gpsimd(_lw3)

                def _lx3(e, m=m, ms=ms):
                    if m >= 1:
                        e.wait_ge(SV.h, SV.n)
                    dma_w(e, LD, xm[:], dxT[ms:ms + 128, :])
                blk.sync(_lx3)

                gw_m = GW[buf].n
                sv_banks = SV.n

                def _mm5(e, m=m, buf=buf, gw_m=gw_m, sv_banks=sv_banks):
                    e.wait_ge(GW[buf].h, gw_m)
                    e.wait_ge(GD.h, GD.n)
                    if m >= 2:
                        e.wait_ge(SV.h, sv_banks - 4)
                    ins = None
                    for tt in range(NTT):
                        e.matmul(out=ps[4 * buf + tt][:, :], lhsT=btb[0:64, m * 128:m * 128 + 128],
                                 rhs=umask[0:64, tt * TT:tt * TT + TT], start=True, stop=False)
                    for k in range(NGD):
                        for tt in range(NTT):
                            e.matmul(out=ps[4 * buf + tt][:, :], lhsT=woutm[:, buf, k, :],
                                     rhs=scrbf[:, k, tt * TT:tt * TT + TT], start=False, stop=False)
                    for k in range(NGV):
                        for tt in range(NTT):
                            ins = e.matmul(out=ps[4 * buf + tt][:, :], lhsT=wtb[:, buf, k, :],
                                           rhs=xnf[:, k, tt * TT:tt * TT + TT], start=False,
                                           stop=(k == NGV - 1))
                    ins.then_inc(ST.h, 1)
                blk.tensor(_mm5)
                ST.inc(1)
                st_mm.append(ST.n)

                st_m = ST.n
                ld_m = LD.n

                def _ev5(e, m=m, buf=buf, st_m=st_m, ld_m=ld_m):
                    e.wait_ge(ST.h, st_m)
                    e.wait_ge(LD.h, ld_m)
                    for tt in range(NTT):
                        bank = 4 * buf + tt
                        e.scalar_tensor_tensor(out=outst[:, buf, tt * TT:tt * TT + TT],
                                               in0=ps[bank][:, :], scalar=biast[:, m:m + 1],
                                               in1=xm[:, tt * TT:tt * TT + TT],
                                               op0=ALU.add, op1=ALU.add)
                blk.vector(_ev5)

                sv_m = SV.n

                def _st5(e, buf=buf, ms=ms, sv_m=sv_m):
                    e.wait_ge(SV.h, sv_m)
                    dma_w(e, LD, dout[ms:ms + 128, :], outst[:, buf, :])
                blk.sync(_st5)

            def _fin(e):
                e.wait_ge(LD.h, LD.n)
            blk.sync(_fin)

            if debug:
                def _dbg4(e):
                    wait_all(e)
                    dma_w(e, LD, dbg["xn"], xnf[:])
                blk.sync(_dbg4)

    return nc


def _patm():
    import ml_dtypes
    p = np.zeros((4, 64), ml_dtypes.bfloat16)
    p[0, 0:8] = 1
    p[1, 8:16] = 1
    p[2, 16:32] = 1
    p[3, 32:64] = 1
    return p


def _prep(inputs):
    f = np.float32
    x = np.asarray(inputs["L0_attn_x"], f)[0]
    xT = np.ascontiguousarray(x.T)
    Win = np.asarray(inputs["L0_attn_Win"], f)
    A = np.asarray(inputs["L0_ffn_A"], f)
    gw = np.asarray(inputs["L0_ffn_gate_w"], f)
    com = {
        "winT": np.ascontiguousarray(Win.T),
        "agT": np.ascontiguousarray(np.concatenate([A.T, gw.T], axis=1)),
        "w1T": np.ascontiguousarray(np.asarray(inputs["L0_attn_W1"], f).transpose(0, 2, 1)),
        "w2T": np.ascontiguousarray(np.asarray(inputs["L0_attn_W2"], f).transpose(0, 2, 1)),
        "woutT": np.ascontiguousarray(np.asarray(inputs["L0_attn_Wout"], f).T),
        "wT": np.ascontiguousarray(np.asarray(inputs["L0_ffn_W"], f).T),
        "bT": np.ascontiguousarray(np.asarray(inputs["L0_ffn_B"], f).T),
        "bin_r": np.ascontiguousarray(np.asarray(inputs["L0_attn_bin"], f).reshape(NGD, 128).T),
        "lnlw_r": np.ascontiguousarray(np.asarray(inputs["L0_attn_lnl_w"], f).reshape(NGD, 128).T),
        "lnlb_r": np.ascontiguousarray(np.asarray(inputs["L0_attn_lnl_b"], f).reshape(NGD, 128).T),
        "bout_r": np.ascontiguousarray(np.asarray(inputs["L0_attn_bout"], f).reshape(NGV, 128).T),
        "bff_r": np.ascontiguousarray(np.asarray(inputs["L0_ffn_b"], f).reshape(NGV, 128).T),
        "gateb_r": np.ascontiguousarray(np.asarray(inputs["L0_ffn_gate_b"], f).reshape(1, 4)),
        "patm": _patm(),
    }
    parts = [np.asarray(inputs[key], f).reshape(NL, NGD, 128).transpose(0, 2, 1)
             for key in ("L0_attn_ln1_w", "L0_attn_ln1_b", "L0_attn_ln2_w",
                         "L0_attn_ln2_b", "L0_attn_sc_lin", "L0_attn_b2",
                         "L0_attn_phz_re", "L0_attn_phz_im", "L0_attn_lc_re",
                         "L0_attn_lc_im")]
    parts.append(np.asarray(inputs["L0_attn_b1"], f).reshape(NL, NGF, 128).transpose(0, 2, 1))
    com["lcon_r"] = np.ascontiguousarray(np.concatenate(parts, axis=2))
    in_maps = []
    for c in range(NCORE):
        mp = dict(com)
        mp["xT"] = np.ascontiguousarray(xT[:, c * TOK:(c + 1) * TOK])
        mp["mask8"] = np.ascontiguousarray(
            np.array([1.0] * c + [0.0] * (NCORE - c), f).reshape(1, 8))
        in_maps.append(mp)
    return in_maps


def kernel(**inputs):
    in_maps = _prep(inputs)
    nc = build_nc(debug=DEBUG, stages=STAGES)
    res = run_bass_kernel_spmd(nc, in_maps, core_ids=list(range(NCORE)))
    out = np.concatenate([np.asarray(res.results[c]["outT"]) for c in range(NCORE)], axis=1)
    if DEBUG:
        kernel._last = res
    return np.ascontiguousarray(out.T).reshape(1, 16384, 4096).astype(np.float32)


# version 08
# speedup vs baseline: 1.2209x; 1.0536x over previous
import contextlib
import numpy as np
import concourse.bass as bass
import concourse.mybir as mybir
from concourse.bass_utils import run_bass_kernel_spmd

F32 = mybir.dt.float32
BF16 = mybir.dt.bfloat16
I32 = mybir.dt.int32
AF = mybir.ActivationFunctionType
ALU = mybir.AluOpType

NCORE = 8
TOK = 2048
VD = 4096
DD = 512
FF = 2048
NL = 4
CK = 512
NCH = TOK // CK
NSLAB = 4
SLAB = TOK // NSLAB
NTT = 4
TT = 512
NGV = VD // 128
NGD = DD // 128
NGF = FF // 128
EPS = 1e-5
RSQRT_MAGIC = 1597463007.0

DEBUG = False
STAGES = 3


class VSafe:

    def __init__(self, e, state):
        self._e = e
        self._s = state

    @staticmethod
    def _names(aps):
        out = []
        for a in aps:
            if a is None or isinstance(a, (int, float)):
                continue
            t = getattr(a, "tensor", None)
            if t is None:
                continue
            try:
                off = a.offset
                ext = 1
                for step, cnt in list(a.ap)[1:]:
                    ext += (cnt - 1) * abs(step)
            except Exception:
                off, ext = 0, 1 << 40
            out.append((t.name, off, off + ext))
        return out

    @staticmethod
    def _hit(a, b):
        for n1, s1, e1 in a:
            for n2, s2, e2 in b:
                if n1 == n2 and s1 < e2 and s2 < e1:
                    return True
        return False

    def _do(self, ins, outs, call, scal=()):
        ii = self._names(ins) + self._names(scal)
        oo = self._names(outs)
        s = self._s
        sv = s["sv"]
        need = 0
        for pi, po, k in s["hist"]:
            if k <= s["cred"]:
                continue
            if self._hit(ii, po) or self._hit(oo, po) or self._hit(oo, pi):
                need = max(need, k)
        if len(s["hist"]) > 192:
            need = max(need, s["hist"][-160][2])
        if need > s["cred"]:
            self._e.wait_ge(sv.h, need)
            s["cred"] = need
            s["hist"] = [h for h in s["hist"] if h[2] > need]
        r = call()
        r.then_inc(sv.h, 1)
        k = sv.inc(1)
        s["hist"].append((ii, oo, k))
        return r

    def tensor_copy(self, out, in_):
        return self._do([in_], [out], lambda: self._e.tensor_copy(out=out, in_=in_))

    def memset(self, ap, c):
        return self._do([], [ap], lambda: self._e.memset(ap, c))

    def tensor_scalar(self, out, in0, scalar1, scalar2, op0, op1=None):
        kw = {} if op1 is None else {"op1": op1}
        return self._do([in0], [out],
                        lambda: self._e.tensor_scalar(out=out, in0=in0, scalar1=scalar1,
                                                      scalar2=scalar2, op0=op0, **kw),
                        scal=[scalar1, scalar2])

    def tensor_tensor(self, out, in0, in1, op):
        return self._do([in0, in1], [out],
                        lambda: self._e.tensor_tensor(out=out, in0=in0, in1=in1, op=op))

    def scalar_tensor_tensor(self, out, in0, scalar, in1, op0, op1):
        return self._do([in0, in1], [out],
                        lambda: self._e.scalar_tensor_tensor(out=out, in0=in0, scalar=scalar,
                                                             in1=in1, op0=op0, op1=op1),
                        scal=[scalar])

    def tensor_tensor_scan(self, out, data0, data1, initial, op0, op1):
        return self._do([data0, data1], [out],
                        lambda: self._e.tensor_tensor_scan(out=out, data0=data0, data1=data1,
                                                           initial=initial, op0=op0, op1=op1),
                        scal=[initial] if not isinstance(initial, float) else ())

    def reciprocal(self, out, in_):
        return self._do([in_], [out], lambda: self._e.reciprocal(out=out, in_=in_))

    def __getattr__(self, name):
        return getattr(self._e, name)


class Sem:
    def __init__(self, es, nc, name):
        self.h = es.enter_context(nc.semaphore(name))
        self.n = 0

    def inc(self, k=1):
        self.n += k
        return self.n


def _rsqrt(e, out, in_, tmp_i, tmp_a):
    n = in_.shape[-1]
    q = max(1, n // 4)
    sl = []
    lo = 0
    while lo < n:
        hi = min(n, lo + q)
        sl.append((lo, hi))
        lo = hi
    def S(ap, s):
        return ap[..., s[0]:s[1]]
    for s in sl:
        e.tensor_copy(out=S(tmp_a, s), in_=S(in_, s).bitcast(I32))
    for s in sl:
        e.tensor_scalar(out=S(tmp_a, s), in0=S(tmp_a, s), scalar1=-0.5 * 2.0 ** -8,
                        scalar2=RSQRT_MAGIC * 2.0 ** -8, op0=ALU.mult, op1=ALU.add)
    for s in sl:
        e.tensor_copy(out=S(tmp_i, s), in_=S(tmp_a, s))
    for s in sl:
        e.tensor_scalar(out=S(tmp_i, s), in0=S(tmp_i, s), scalar1=8, scalar2=None,
                        op0=ALU.arith_shift_left)
    for s in sl:
        e.tensor_copy(out=S(out, s), in_=S(tmp_i, s).bitcast(F32))
    for _ in range(3):
        for s in sl:
            e.tensor_tensor(out=S(tmp_a, s), in0=S(out, s), in1=S(out, s), op=ALU.mult)
        for s in sl:
            e.tensor_tensor(out=S(tmp_a, s), in0=S(tmp_a, s), in1=S(in_, s), op=ALU.mult)
        for s in sl:
            e.tensor_scalar(out=S(tmp_a, s), in0=S(tmp_a, s), scalar1=-0.5, scalar2=1.5,
                            op0=ALU.mult, op1=ALU.add)
        for s in sl:
            e.tensor_tensor(out=S(out, s), in0=S(out, s), in1=S(tmp_a, s), op=ALU.mult)


def build_nc(debug=False, stages=3):
    nc = bass.Bass("TRN2", target_bir_lowering=False)

    dxT = nc.dram_tensor("xT", [VD, TOK], F32, kind="ExternalInput").ap()
    dwin = nc.dram_tensor("winT", [VD, DD], F32, kind="ExternalInput").ap()
    dag = nc.dram_tensor("agT", [VD, 68], F32, kind="ExternalInput").ap()
    dw1 = nc.dram_tensor("w1T", [NL, DD, FF], F32, kind="ExternalInput").ap()
    dw2 = nc.dram_tensor("w2T", [NL, FF, DD], F32, kind="ExternalInput").ap()
    dwout = nc.dram_tensor("woutT", [DD, VD], F32, kind="ExternalInput").ap()
    dwff = nc.dram_tensor("wT", [VD, VD], F32, kind="ExternalInput").ap()
    dbT = nc.dram_tensor("bT", [64, VD], F32, kind="ExternalInput").ap()
    dbin = nc.dram_tensor("bin_r", [128, NGD], F32, kind="ExternalInput").ap()
    dlcon = nc.dram_tensor("lcon_r", [NL, 128, 56], F32, kind="ExternalInput").ap()
    dlnlw = nc.dram_tensor("lnlw_r", [128, NGD], F32, kind="ExternalInput").ap()
    dlnlb = nc.dram_tensor("lnlb_r", [128, NGD], F32, kind="ExternalInput").ap()
    dbout = nc.dram_tensor("bout_r", [128, NGV], F32, kind="ExternalInput").ap()
    dbff = nc.dram_tensor("bff_r", [128, NGV], F32, kind="ExternalInput").ap()
    dgateb = nc.dram_tensor("gateb_r", [1, 4], F32, kind="ExternalInput").ap()
    dmask8 = nc.dram_tensor("mask8", [1, 8], F32, kind="ExternalInput").ap()
    dpat = nc.dram_tensor("patm", [4, 64], BF16, kind="ExternalInput").ap()
    dout = nc.dram_tensor("outT", [VD, TOK], F32, kind="ExternalOutput").ap()

    dxn = nc.dram_tensor("xn_spill", [NGV, 128, TOK], BF16).ap()
    dlni = nc.dram_tensor("lni", [2 * TOK], F32).ap()
    dlno = nc.dram_tensor("lno", [2 * TOK], F32).ap()
    dgsc = nc.dram_tensor("gsc", [4, TOK], F32).ap()
    dgm = nc.dram_tensor("gm", [4, TOK], BF16).ap()
    hcc_i = [nc.dram_tensor(f"cc_i{l}", [128, 8], F32) for l in range(NL)]
    hcc_o = [nc.dram_tensor(f"cc_o{l}", [128 * 8, 8], F32) for l in range(NL)]

    dbg = {}
    if debug:
        dbg["h0"] = nc.dram_tensor("dbg_h0", [128, NGD, TOK], F32, kind="ExternalOutput").ap()
        dbg["xn"] = nc.dram_tensor("dbg_xn", [128, NGV, TOK], BF16, kind="ExternalOutput").ap()
        for l in range(NL):
            dbg[f"h{l+1}"] = nc.dram_tensor(f"dbg_h{l+1}", [128, NGD, TOK], F32, kind="ExternalOutput").ap()
        dbg["hn"] = nc.dram_tensor("dbg_hn", [128, NGD, TOK], BF16, kind="ExternalOutput").ap()
        dbg["um"] = nc.dram_tensor("dbg_um", [64, TOK], BF16, kind="ExternalOutput").ap()

    es = contextlib.ExitStack()
    with es:
        sb = lambda name, shape, dt: es.enter_context(nc.sbuf_tensor(name, shape, dt))
        es2 = contextlib.ExitStack()
        sbm = lambda name, shape, dt: es2.enter_context(nc.sbuf_tensor(name, shape, dt))
        scrbf = sb("scrbf", [128, NGD, TOK], BF16)
        umask = sb("umask", [64, TOK], BF16)
        onesb = sb("onesb", [128, 1], BF16)
        gbb = sb("gbb", [128, 4], F32)
        m8b = sb("m8b", [128, 8], F32)
        h = sbm("h", [128, NGD, TOK], F32)
        onesf = sbm("onesf", [128, CK], F32)
        binv = sbm("binv", [128, NGD], F32)
        lnlw = sbm("lnlw", [128, NGD], F32)
        lnlb = sbm("lnlb", [128, NGD], F32)
        lcon4 = sbm("lcon4", [128, NL, 56], F32)
        rho44 = sbm("rho44", [128, NL, NGD], F32)
        ure44 = sbm("ure44", [128, NL, NGD], F32)
        uim44 = sbm("uim44", [128, NL, NGD], F32)
        rr44 = sbm("rr44", [128, NL, NGD], F32)
        gy = sbm("gy", [128, NGD * 256], F32)
        gA = sbm("gA", [128, NGD], F32)
        gB = sbm("gB", [128, NGD], F32)
        gD = sbm("gD", [128, NGD, 2], F32)
        gE = sbm("gE", [128, NGD, 2], F32)
        gti = sbm("gti", [128, NGD], I32)
        q4 = sbm("q4", [128, NGD, 2], F32)
        u5 = sbm("u5", [128, NGD, 2], F32)
        p2k = sbm("p2k", [128, NGD, 2], F32)
        pm1 = sbm("pm1", [128, NGD], F32)
        qpow = sbm("qpow", [128, NCH, NGD, 2], F32)
        iot = sbm("iot", [128, NGD, NCH, 2], F32)
        iosc = sbm("iosc", [128, NGD, NCH, 2], F32)
        Ev = sbm("Ev", [128, NGD, 2], F32)
        gat = sbm("gat", [128, 8, NGD, 2], F32)
        Hre = sbm("Hre", [128, NGD], F32)
        Him = sbm("Him", [128, NGD], F32)
        tA = sbm("tA", [128, NGD], F32)
        tB = sbm("tB", [128, NGD], F32)
        tC = sbm("tC", [128, NGD], F32)
        tD = sbm("tD", [128, NGD, 2], F32)
        tE = sbm("tE", [128, NGD, 2], F32)
        scs = sbm("scs", [128, 2], F32)
        sct = sbm("sct", [128, 4], F32)
        scs4 = sbm("scs4", [128, NGD, 2], F32)
        sct4 = sbm("sct4", [128, NGD, 4], F32)
        ends4 = sbm("ends4", [128, NGD, NCH, 2], F32)
        tF = sbm("tF", [128, NGD, 2], F32)
        statin = sbm("statin", [1, 2 * TOK], F32)
        statq = sbm("statq", [128, 2, TOK // 128], F32)
        stato = sbm("stato", [128, 2, TOK // 128], F32)
        stati = sbm("stati", [128, TOK // 128], I32)
        stata = sbm("stata", [128, TOK // 128], F32)
        rsnu = sbm("rsnu", [128, 2, TOK], F32)
        rstdb = rsnu[:, 0]
        nub = rsnu[:, 1]
        rint = sbm("rint", [128, NGD, CK, 2], F32)
        dk4 = sbm("dk4", [128, NGD, CK], F32)
        ytmp = sbm("ytmp", [128, TOK], F32)
        ps = [es.enter_context(nc.psum_tensor(f"ps{i}", [128, TT], F32)) for i in range(8)]

        LD = Sem(es, nc, "LD")
        GD = Sem(es, nc, "GD")
        SV = Sem(es, nc, "SV")
        SSm = Sem(es, nc, "SS")
        ST = Sem(es, nc, "ST")
        SG = Sem(es, nc, "SG")
        CC = Sem(es, nc, "CC")
        GW = [Sem(es, nc, "GW0"), Sem(es, nc, "GW1")]
        GV = Sem(es, nc, "GV")

        blk = es.enter_context(nc.Block())
        _vstate = {"hist": [], "cred": 0, "sv": SV}
        _gstate = {"hist": [], "cred": 0, "sv": GV}

        class _BlkWrap:
            def __init__(self, b):
                self._b = b

            def vector(self, f):
                return self._b.vector(lambda e: f(VSafe(e, _vstate)))

            def gpsimd(self, f):
                return self._b.gpsimd(lambda e: f(VSafe(e, _gstate)))

            def __getattr__(self, name):
                return getattr(self._b, name)

        blk = _BlkWrap(blk)

        def dma_w(e, S, out, in_):
            e.dma_start(out=out, in_=in_).then_inc(S.h, 16)
            e.wait_ge(S.h, S.inc(16))

        def wait_all(e):
            for s in (LD, GD, SV, SSm, ST, SG, CC, GW[0], GW[1]):
                if s.n:
                    e.wait_ge(s.h, s.n)

        OWN = {"sync": LD, "scalar": SSm, "vector": SV, "gpsimd": SG, "tensor": ST}

        def barrier(engines=("sync", "scalar", "vector", "gpsimd", "tensor")):
            for name in engines:
                def _g(e):
                    wait_all(e)
                getattr(blk, name)(_g)

        def _p0_sync(e):
            for dst, srcap in ((binv, dbin), (lnlw, dlnlw), (lnlb, dlnlb)):
                dma_w(e, LD, dst[:], srcap)
            dma_w(e, LD, gbb[:], dgateb.broadcast_to([128, 4]))
            dma_w(e, LD, m8b[:], dmask8.broadcast_to([128, 8]))
            dma_w(e, LD, lcon4[:], dlcon.rearrange("l p c -> p l c"))
        blk.sync(_p0_sync)
        ld_lcon = LD.n

        def _p0_scal(e):
            e.preload_act_table(AF.Silu)
        blk.scalar(_p0_scal)

        def _p0_vec(e):
            e.memset(onesb[:], 1.0)
            e.memset(onesf[:], 1.0)
        blk.vector(_p0_vec)

        def ln_finish(width, inv_n, n_tiles, sig_banks, sq_banks):
            kq = width // 128

            def _evac(e):
                wait_all(e)
                for i in range(n_tiles):
                    e.activation(out=statin[0:1, i * TT:i * TT + TT], in_=sig_banks[i], func=AF.Identity)
                for i in range(n_tiles - 1):
                    e.activation(out=statin[0:1, width + i * TT:width + i * TT + TT],
                                 in_=sq_banks[i], func=AF.Identity)
                i = n_tiles - 1
                e.activation(out=statin[0:1, width + i * TT:width + i * TT + TT],
                             in_=sq_banks[i], func=AF.Identity).then_inc(SSm.h, 1)
            blk.scalar(_evac)
            SSm.inc(1)

            def _bounce1(e):
                e.wait_ge(SSm.h, SSm.n)
                dma_w(e, LD, dlni[0:2 * width].unsqueeze(0), statin[0:1, 0:2 * width])
                dma_w(e, LD, statq[:, :, 0:kq],
                      dlni[0:2 * width].rearrange("(a p k) -> p a k", p=128, a=2))
            blk.sync(_bounce1)

            def _finish(e):
                e.wait_ge(LD.h, LD.n)
                mu = stato[:, 0, 0:kq]
                var = stato[:, 1, 0:kq]
                e.tensor_scalar(out=mu, in0=statq[:, 0, 0:kq], scalar1=inv_n, scalar2=None, op0=ALU.mult)
                e.tensor_scalar(out=var, in0=statq[:, 1, 0:kq], scalar1=inv_n, scalar2=EPS,
                                op0=ALU.mult, op1=ALU.add)
                e.tensor_tensor(out=stata[:, 0:kq], in0=mu, in1=mu, op=ALU.mult)
                e.tensor_tensor(out=var, in0=var, in1=stata[:, 0:kq], op=ALU.subtract)
                _rsqrt(e, statq[:, 0, 0:kq], var, stati[:, 0:kq], stata[:, 0:kq])
                e.tensor_tensor(out=statq[:, 1, 0:kq], in0=mu, in1=statq[:, 0, 0:kq],
                                op=ALU.mult)
            blk.vector(_finish)

            def _bounce2(e):
                e.wait_ge(SV.h, SV.n)
                dma_w(e, LD, dlno[0:2 * width].rearrange("(a p k) -> p a k", p=128, a=2),
                      statq[:, :, 0:kq])
                dma_w(e, LD, rsnu[:, :, 0:width],
                      dlno[0:2 * width].rearrange("(a t) -> a t", a=2).unsqueeze(0)
                      .broadcast_to([128, 2, width]))
            blk.sync(_bounce2)

        def attn_ln_stats():
            st0 = ST.n
            ss0 = SSm.n
            sv0 = SV.n

            def _cst(e, st0=st0):
                wait_all(e)
                for g in range(NGD):
                    if g >= 2:
                        e.wait_ge(ST.h, st0 + g - 1)
                    e.tensor_copy(out=hbt[:, g % 2, :], in_=h[:, g, :])
            blk.vector(_cst)

            def _sq(e, sv0=sv0, st0=st0):
                for g in range(NGD):
                    e.wait_ge(SV.h, sv0 + g + 1)
                    if g >= 2:
                        e.wait_ge(ST.h, st0 + g - 1)
                    e.activation(out=h2t[:, g % 2, :], in_=hbt[:, g % 2, :],
                                 func=AF.Square).then_inc(SSm.h, 1)
            blk.scalar(_sq)
            SSm.inc(NGD)

            def _mm(e, ss0=ss0):
                e.wait_ge(SV.h, 1)
                for g in range(NGD):
                    e.wait_ge(SSm.h, ss0 + g + 1)
                    ins = None
                    for tt in range(NTT):
                        e.matmul(out=ps[tt][0:1, :], lhsT=onesb[:],
                                 rhs=hbt[:, g % 2, tt * TT:tt * TT + TT],
                                 start=(g == 0), stop=(g == NGD - 1))
                        ins = e.matmul(out=ps[4 + tt][0:1, :], lhsT=onesb[:],
                                       rhs=h2t[:, g % 2, tt * TT:tt * TT + TT],
                                       start=(g == 0), stop=(g == NGD - 1))
                    ins.then_inc(ST.h, 1)
            blk.tensor(_mm)
            ST.inc(NGD)
            ln_finish(TOK, 1.0 / DD, NTT, [ps[tt][0:1, :] for tt in range(NTT)],
                      [ps[4 + tt][0:1, :] for tt in range(NTT)])
            barrier(("vector", "scalar", "sync"))

        tanh_mark = [0]

        def emit_gp_prep():
            def _prep(e):
                e.wait_ge(LD.h, ld_lcon)
                for l in range(NL):
                    phr = lcon4[:, l, 24:28]
                    phi = lcon4[:, l, 28:32]
                    e.tensor_tensor(out=gA[:], in0=phr, in1=phr, op=ALU.mult)
                    e.tensor_tensor(out=gB[:], in0=phi, in1=phi, op=ALU.mult)
                    e.tensor_tensor(out=gA[:], in0=gA[:], in1=gB[:], op=ALU.add)
                    _rsqrt(e, gB[:], gA[:], gti[:], gD[:, :, 0])
                    e.tensor_tensor(out=rr44[:, l], in0=gA[:], in1=gB[:], op=ALU.mult)
                    e.tensor_tensor(out=ure44[:, l], in0=phr, in1=gB[:], op=ALU.mult)
                    e.tensor_tensor(out=uim44[:, l], in0=phi, in1=gB[:], op=ALU.mult)
            blk.vector(_prep)
            prep_mark = SV.n

            def _tanh4(e):
                e.wait_ge(SV.h, prep_mark)
                e.activation(out=rho44[:], in_=rr44[:], func=AF.Tanh).then_inc(SSm.h, 1)
            blk.scalar(_tanh4)
            SSm.inc(1)
            tanh_mark[0] = SSm.n

        def emit_gp_tables(l, sv_gate):
            def _gtb(e, l=l, sv_gate=sv_gate):
                if sv_gate:
                    e.wait_ge(SV.h, sv_gate)
                if l == 0:
                    e.wait_ge(SSm.h, tanh_mark[0])
                e.tensor_copy(out=Hre[:], in_=lcon4[:, l, 32:36])
                e.tensor_copy(out=Him[:], in_=lcon4[:, l, 36:40])
                for g in range(NGD):
                    ri = rint[:, g]
                    e.memset(ri[:, 0:1, 0:1], 1.0)
                    e.memset(ri[:, 0:1, 1:2], 0.0)
                    e.tensor_copy(out=dk4[:, g, 0:1], in_=rho44[:, l, g:g + 1])
                    e.tensor_copy(out=scs4[:, g, 0:1], in_=ure44[:, l, g:g + 1])
                    e.tensor_copy(out=scs4[:, g, 1:2], in_=uim44[:, l, g:g + 1])
                n = 1
                while n < CK:
                    m = min(n, CK - n)
                    def bc(ap, m=m):
                        return ap.broadcast_to([128, m])
                    for g in range(NGD):
                        e.tensor_tensor(out=gy[:, g * 256:g * 256 + m], in0=rint[:, g, 0:m, 1],
                                        in1=bc(scs4[:, g, 1:2]), op=ALU.mult)
                    for g in range(NGD):
                        e.tensor_tensor(out=rint[:, g, n:n + m, 0], in0=rint[:, g, 0:m, 0],
                                        in1=bc(scs4[:, g, 0:1]), op=ALU.mult)
                    for g in range(NGD):
                        e.tensor_tensor(out=rint[:, g, n:n + m, 0], in0=rint[:, g, n:n + m, 0],
                                        in1=gy[:, g * 256:g * 256 + m], op=ALU.add)
                    for g in range(NGD):
                        e.tensor_tensor(out=gy[:, g * 256:g * 256 + m], in0=rint[:, g, 0:m, 0],
                                        in1=bc(scs4[:, g, 1:2]), op=ALU.mult)
                    for g in range(NGD):
                        e.tensor_tensor(out=rint[:, g, n:n + m, 1], in0=rint[:, g, 0:m, 1],
                                        in1=bc(scs4[:, g, 0:1]), op=ALU.mult)
                    for g in range(NGD):
                        e.tensor_tensor(out=rint[:, g, n:n + m, 1], in0=rint[:, g, n:n + m, 1],
                                        in1=gy[:, g * 256:g * 256 + m], op=ALU.subtract)
                    for g in range(NGD):
                        e.tensor_tensor(out=dk4[:, g, n:n + m], in0=dk4[:, g, 0:m],
                                        in1=bc(dk4[:, g, n - 1:n]), op=ALU.mult)
                    for g in range(NGD):
                        e.tensor_tensor(out=sct4[:, g, 0:2], in0=scs4[:, g, 0:2],
                                        in1=scs4[:, g, 0:2], op=ALU.mult)
                    for g in range(NGD):
                        e.tensor_tensor(out=sct4[:, g, 2:3], in0=scs4[:, g, 0:1],
                                        in1=scs4[:, g, 1:2], op=ALU.mult)
                    for g in range(NGD):
                        e.tensor_tensor(out=scs4[:, g, 0:1], in0=sct4[:, g, 0:1],
                                        in1=sct4[:, g, 1:2], op=ALU.subtract)
                    for g in range(NGD):
                        e.tensor_scalar(out=scs4[:, g, 1:2], in0=sct4[:, g, 2:3], scalar1=2.0,
                                        scalar2=None, op0=ALU.mult)
                    n *= 2
                e.tensor_copy(out=u5[:], in_=scs4[:])
                e.tensor_tensor(out=q4[:, :, 0], in0=scs4[:, :, 0],
                                in1=dk4[:, :, CK - 1], op=ALU.mult)
                e.tensor_tensor(out=q4[:, :, 1], in0=scs4[:, :, 1],
                                in1=dk4[:, :, CK - 1], op=ALU.mult)

                def csq(dst, src_):
                    e.tensor_tensor(out=gD[:, :, 0], in0=src_[:, :, 0], in1=src_[:, :, 0], op=ALU.mult)
                    e.tensor_tensor(out=gA[:], in0=src_[:, :, 1], in1=src_[:, :, 1], op=ALU.mult)
                    e.tensor_tensor(out=gD[:, :, 1], in0=src_[:, :, 0], in1=src_[:, :, 1], op=ALU.mult)
                    e.tensor_tensor(out=dst[:, :, 0], in0=gD[:, :, 0], in1=gA[:], op=ALU.subtract)
                    e.tensor_scalar(out=dst[:, :, 1], in0=gD[:, :, 1], scalar1=2.0, scalar2=None,
                                    op0=ALU.mult)
                csq(gE, q4)
                csq(p2k, gE)
                e.tensor_scalar(out=pm1[:], in0=p2k[:, :, 0], scalar1=1.0, scalar2=-1.0,
                                op0=ALU.mult, op1=ALU.add)
                e.memset(qpow[:, 0, :, 0], 1.0)
                e.memset(qpow[:, 0, :, 1], 0.0)
                for k in range(1, NCH):
                    a = qpow[:, k - 1]
                    e.tensor_tensor(out=gD[:, :, 0], in0=a[:, :, 0], in1=q4[:, :, 0], op=ALU.mult)
                    e.tensor_tensor(out=gA[:], in0=a[:, :, 1], in1=q4[:, :, 1], op=ALU.mult)
                    e.tensor_tensor(out=gD[:, :, 1], in0=a[:, :, 0], in1=q4[:, :, 1], op=ALU.mult)
                    e.tensor_tensor(out=gB[:], in0=a[:, :, 1], in1=q4[:, :, 0], op=ALU.mult)
                    e.tensor_tensor(out=qpow[:, k, :, 0], in0=gD[:, :, 0], in1=gA[:], op=ALU.subtract)
                    e.tensor_tensor(out=qpow[:, k, :, 1], in0=gD[:, :, 1], in1=gB[:], op=ALU.add)
            blk.gpsimd(_gtb)
            return GV.n

        with (
            nc.sbuf_tensor("xslab", [128, NGV, SLAB], BF16) as xslab,
            nc.sbuf_tensor("x2t", [128, 2, SLAB], BF16) as x2t,
            nc.sbuf_tensor("winb", [128, NGV, DD], BF16) as winb,
            nc.sbuf_tensor("agb", [128, NGV, 68], BF16) as agb,
            nc.sbuf_tensor("usc", [128, SLAB], F32) as usc,
            nc.sbuf_tensor("mrows", [4, TOK], BF16) as mrows,
            nc.sbuf_tensor("scm", [128, 4, TOK // 128], F32) as scm,
            nc.sbuf_tensor("sc2", [128, 2, TOK // 128], F32) as sc2,
            nc.sbuf_tensor("mpk", [128, 4, TOK // 128], BF16) as mpk,
            nc.sbuf_tensor("pat", [4, 64], BF16) as pat,
        ):
            def _w1(e):
                dma_w(e, GD, winb[:], dwin.rearrange("(g p) d -> p g d", p=128))
                dma_w(e, GD, agb[:], dag.rearrange("(g p) d -> p g d", p=128))
            blk.gpsimd(_w1)
            emit_gp_prep()
            tbl_done = {}

            for s in range(NSLAB):
                t0 = s * SLAB

                def _lx(e, t0=t0):
                    dma_w(e, GD, xslab[:],
                          dxT[:, t0:t0 + SLAB].rearrange("(g p) t -> p g t", p=128))
                blk.gpsimd(_lx)
                if s == 0:
                    tbl_done[0] = emit_gp_tables(0, 0)
                gd_x = GD.n

                st0 = ST.n
                sq0 = SSm.n

                def _sq1(e, gd_x=gd_x, st0=st0):
                    e.wait_ge(GD.h, gd_x)
                    e.wait_ge(SV.h, 1)
                    for g in range(NGV):
                        if g >= 2:
                            e.wait_ge(ST.h, st0 + g - 1)
                        e.activation(out=x2t[:, g % 2, :], in_=xslab[:, g, :],
                                     func=AF.Square).then_inc(SSm.h, 1)
                blk.scalar(_sq1)
                SSm.inc(NGV)

                def _mms(e, gd_x=gd_x, sq0=sq0):
                    e.wait_ge(GD.h, gd_x)
                    e.wait_ge(SV.h, 1)
                    for g in range(NGV):
                        e.wait_ge(SSm.h, sq0 + g + 1)
                        e.matmul(out=ps[0][0:1, :], lhsT=onesb[:], rhs=xslab[:, g, :],
                                 start=(g == 0), stop=(g == NGV - 1))
                        e.matmul(out=ps[1][0:1, :], lhsT=onesb[:], rhs=x2t[:, g % 2, :],
                                 start=(g == 0), stop=(g == NGV - 1)).then_inc(ST.h, 1)
                blk.tensor(_mms)
                ST.inc(NGV)

                ln_finish(SLAB, 1.0 / VD, 1, [ps[0][0:1, :]], [ps[1][0:1, :]])

                def _xn(e):
                    wait_all(e)
                    ins = None
                    for g4 in range(0, NGV, 4):
                        for j in range(4):
                            e.tensor_tensor(out=ytmp[:, j * SLAB:(j + 1) * SLAB],
                                            in0=xslab[:, g4 + j, :],
                                            in1=rstdb[:, 0:SLAB], op=ALU.mult)
                        for j in range(4):
                            e.tensor_tensor(out=xslab[:, g4 + j, :],
                                            in0=ytmp[:, j * SLAB:(j + 1) * SLAB],
                                            in1=nub[:, 0:SLAB], op=ALU.subtract)
                blk.vector(_xn)

                def _spill(e, t0=t0):
                    e.wait_ge(SV.h, SV.n)
                    dma_w(e, LD, dxn[:, :, t0:t0 + SLAB].rearrange("g p t -> p g t"), xslab[:])
                blk.sync(_spill)

                def _mmw(e):
                    e.wait_ge(SV.h, SV.n)
                    e.wait_ge(GD.h, GD.n)
                    e.wait_ge(SSm.h, SSm.n)
                    for g in range(NGV):
                        for m in range(NGD):
                            e.matmul(out=ps[2 + m][:, :], lhsT=winb[:, g, m * 128:m * 128 + 128],
                                     rhs=xslab[:, g, :], start=(g == 0), stop=(g == NGV - 1))
                        ins = e.matmul(out=ps[6][0:68, :], lhsT=agb[:, g, :], rhs=xslab[:, g, :],
                                       start=(g == 0), stop=(g == NGV - 1))
                    ins.then_inc(ST.h, 1)
                blk.tensor(_mmw)
                ST.inc(1)

                def _evh(e, t0=t0):
                    e.wait_ge(ST.h, ST.n)
                    for m in range(NGD):
                        e.activation(out=h[:, m, t0:t0 + SLAB], in_=ps[2 + m][:, :],
                                     func=AF.Identity, bias=binv[:, m:m + 1])
                    e.activation(out=umask[0:64, t0:t0 + SLAB], in_=ps[6][0:64, :], func=AF.Identity)
                    e.activation(out=usc[64:68, :], in_=ps[6][64:68, :],
                                 func=AF.Identity).then_inc(SSm.h, 1)
                blk.scalar(_evh)
                SSm.inc(1)

                def _ssc(e, t0=t0):
                    e.wait_ge(SSm.h, SSm.n)
                    dma_w(e, LD, dgsc[:, t0:t0 + SLAB], usc[64:68, :])
                blk.sync(_ssc)
                barrier()

            def _mld(e):
                wait_all(e)
                dma_w(e, LD, pat[:], dpat)
                for r in range(4):
                    dma_w(e, LD, scm[:, r, :], dgsc[r].rearrange("(p k) -> p k", p=128))
            blk.sync(_mld)

            def _mcalc(e):
                e.wait_ge(LD.h, LD.n)
                KQ = TOK // 128
                e.tensor_tensor(out=scm[:], in0=scm[:],
                                in1=gbb[:].unsqueeze(2).broadcast_to([128, 4, KQ]), op=ALU.add)
                e.tensor_tensor(out=sc2[:, 0, :], in0=scm[:, 0, :], in1=scm[:, 1, :], op=ALU.max)
                e.tensor_tensor(out=sc2[:, 1, :], in0=scm[:, 2, :], in1=scm[:, 3, :], op=ALU.max)
                e.tensor_tensor(out=sc2[:, 0, :], in0=sc2[:, 0, :], in1=sc2[:, 1, :], op=ALU.max)
                mx = sc2[:, 0, :]
                e.memset(mpk[:, 0, :], 1.0)
                e.tensor_tensor(out=sc2[:, 1, :], in0=scm[:, 0, :], in1=mx, op=ALU.is_lt)
                e.tensor_copy(out=mpk[:, 1, :], in_=sc2[:, 1, :])
                e.tensor_tensor(out=scm[:, 1, :], in0=scm[:, 1, :], in1=mx, op=ALU.is_lt)
                e.tensor_tensor(out=sc2[:, 1, :], in0=sc2[:, 1, :], in1=scm[:, 1, :], op=ALU.mult)
                e.tensor_copy(out=mpk[:, 2, :], in_=sc2[:, 1, :])
                e.tensor_tensor(out=scm[:, 2, :], in0=scm[:, 2, :], in1=mx, op=ALU.is_lt)
                e.tensor_tensor(out=sc2[:, 1, :], in0=sc2[:, 1, :], in1=scm[:, 2, :], op=ALU.mult)
                e.tensor_copy(out=mpk[:, 3, :], in_=sc2[:, 1, :])
            blk.vector(_mcalc)

            def _mst(e):
                e.wait_ge(SV.h, SV.n)
                for r in range(4):
                    dma_w(e, LD, dgm[r].rearrange("(p k) -> p k", p=128), mpk[:, r, :])
                dma_w(e, LD, mrows[:], dgm)
            blk.sync(_mst)

            for tt in range(NTT):
                sv_mm = SV.n

                def _mmm(e, tt=tt, sv_mm=sv_mm):
                    e.wait_ge(LD.h, LD.n)
                    e.wait_ge(SV.h, sv_mm)
                    e.matmul(out=ps[7][0:64, :], lhsT=pat[:], rhs=mrows[:, tt * TT:tt * TT + TT],
                             start=True, stop=True).then_inc(ST.h, 1)
                blk.tensor(_mmm)
                ST.inc(1)

                def _umul(e, tt=tt):
                    e.wait_ge(ST.h, ST.n)
                    e.tensor_tensor(out=umask[0:64, tt * TT:tt * TT + TT],
                                    in0=umask[0:64, tt * TT:tt * TT + TT],
                                    in1=ps[7][0:64, :], op=ALU.mult)
                blk.vector(_umul)
            barrier()

            if debug:
                def _dbg1(e):
                    wait_all(e)
                    dma_w(e, LD, dbg["h0"], h[:])
                    dma_w(e, LD, dbg["um"], umask[:])
                blk.sync(_dbg1)
                barrier()

        if stages < 2:
            def _fin1(e):
                wait_all(e)
            for name in ("sync", "scalar", "vector", "gpsimd", "tensor"):
                getattr(blk, name)(_fin1)
            es2.close()
            return nc
        with (
            nc.sbuf_tensor("dw_", [128, NCH, CK, 2], F32) as dwv,
            nc.sbuf_tensor("midt", [128, NGF, TT], BF16) as midt,
            nc.sbuf_tensor("hbt", [128, 2, TOK], BF16) as hbt,
            nc.sbuf_tensor("h2t", [128, 2, TOK], BF16) as h2t,
            nc.sbuf_tensor("w1b", [128, NGD, FF], BF16) as w1b,
            nc.sbuf_tensor("w2b", [128, NGF, DD], BF16) as w2b,
            nc.sbuf_tensor("rhot", [128, CK], F32) as rhot,
        ):
            for l in range(NL):
                ln1w = lcon4[:, l, 0:4]
                ln1b = lcon4[:, l, 4:8]
                ln2w = lcon4[:, l, 8:12]
                ln2b = lcon4[:, l, 12:16]
                sclv = lcon4[:, l, 16:20]
                b2v = lcon4[:, l, 20:24]
                b1v = lcon4[:, l, 40:56]

                def _lw(e, l=l):
                    dma_w(e, GD, w1b[:], dw1[l].rearrange("(k p) f -> p k f", p=128))
                    dma_w(e, GD, w2b[:], dw2[l].rearrange("(k p) f -> p k f", p=128))
                blk.gpsimd(_lw)

                attn_ln_stats()

                for g in range(NGD):
                    def _nrm(e, g=g):
                        if g == 0:
                            wait_all(e)
                        else:
                            e.wait_ge(SSm.h, SSm.n)
                        e.tensor_tensor(out=ytmp[:], in0=h[:, g, :], in1=rstdb[:], op=ALU.mult)
                        e.tensor_tensor(out=ytmp[:], in0=ytmp[:], in1=nub[:],
                                        op=ALU.subtract)
                    blk.vector(_nrm)

                    def _silu(e, g=g):
                        e.wait_ge(SV.h, SV.n)
                        e.activation(out=ytmp[:], in_=ytmp[:], func=AF.Silu,
                                     scale=ln1w[:, g:g + 1], bias=ln1b[:, g:g + 1]).then_inc(SSm.h, 1)
                    blk.scalar(_silu)
                    SSm.inc(1)

                    def _scan(e, g=g, l=l):
                        e.wait_ge(SSm.h, SSm.n)
                        if g == 0:
                            e.wait_ge(GV.h, tbl_done[l])
                        e.tensor_tensor(
                            out=dwv[:],
                            in0=ytmp[:].rearrange("p (c t) -> p c t", c=NCH).unsqueeze(3).broadcast_to([128, NCH, CK, 2]),
                            in1=rint[:, g].unsqueeze(1).broadcast_to([128, NCH, CK, 2]),
                            op=ALU.mult)
                        e.tensor_scalar(out=rhot[:], in0=onesf[:], scalar1=rho44[:, l, g:g + 1],
                                        scalar2=None, op0=ALU.mult)
                        for comp in range(2):
                            for c in range(NCH):
                                e.tensor_tensor_scan(out=dwv[:, c, :, comp], data0=rhot[:],
                                                     data1=dwv[:, c, :, comp], initial=0.0,
                                                     op0=ALU.mult, op1=ALU.add)
                        e.tensor_copy(out=ends4[:, g], in_=dwv[:, :, CK - 1, :])
                        e.tensor_tensor(out=dwv[:], in0=dwv[:],
                                        in1=rint[:, g].unsqueeze(1).broadcast_to([128, NCH, CK, 2]),
                                        op=ALU.mult)
                        hv = h[:, g, :].rearrange("p (c t) -> p c t", c=NCH)
                        e.scalar_tensor_tensor(out=hv, in0=dwv[:, :, :, 0], scalar=sclv[:, g:g + 1],
                                               in1=hv, op0=ALU.mult, op1=ALU.add)
                        e.scalar_tensor_tensor(out=hv, in0=dwv[:, :, :, 1], scalar=sclv[:, g:g + 1],
                                               in1=hv, op0=ALU.mult, op1=ALU.add)
                    blk.vector(_scan)

                def _chains(e):
                    e.wait_ge(SV.h, SV.n)
                    e.memset(iot[:, :, 0, :], 0.0)
                    for c in range(1, NCH):
                        pi = iot[:, :, c - 1, :]
                        pe = ends4[:, :, c - 1, :]
                        e.tensor_tensor(out=tD[:, :, 0], in0=q4[:, :, 0], in1=pi[:, :, 0], op=ALU.mult)
                        e.tensor_tensor(out=tE[:, :, 0], in0=u5[:, :, 0], in1=pe[:, :, 0], op=ALU.mult)
                        e.tensor_tensor(out=tD[:, :, 1], in0=q4[:, :, 0], in1=pi[:, :, 1], op=ALU.mult)
                        e.tensor_tensor(out=tE[:, :, 1], in0=u5[:, :, 0], in1=pe[:, :, 1], op=ALU.mult)
                        e.tensor_tensor(out=tA[:], in0=q4[:, :, 1], in1=pi[:, :, 1], op=ALU.mult)
                        e.tensor_tensor(out=tB[:], in0=u5[:, :, 1], in1=pe[:, :, 1], op=ALU.mult)
                        e.tensor_tensor(out=tD[:, :, 0], in0=tD[:, :, 0], in1=tA[:], op=ALU.subtract)
                        e.tensor_tensor(out=tE[:, :, 0], in0=tE[:, :, 0], in1=tB[:], op=ALU.subtract)
                        e.tensor_tensor(out=tA[:], in0=q4[:, :, 1], in1=pi[:, :, 0], op=ALU.mult)
                        e.tensor_tensor(out=tB[:], in0=u5[:, :, 1], in1=pe[:, :, 0], op=ALU.mult)
                        e.tensor_tensor(out=tD[:, :, 1], in0=tD[:, :, 1], in1=tA[:], op=ALU.add)
                        e.tensor_tensor(out=tE[:, :, 1], in0=tE[:, :, 1], in1=tB[:], op=ALU.add)
                        e.tensor_tensor(out=iot[:, :, c, :], in0=tD[:, :, :], in1=tE[:, :, :], op=ALU.add)
                    e.tensor_tensor(out=tD[:, :, 0], in0=iot[:, :, NCH - 1, 0],
                                    in1=dk4[:, :, CK - 1], op=ALU.mult)
                    e.tensor_tensor(out=tD[:, :, 1], in0=iot[:, :, NCH - 1, 1],
                                    in1=dk4[:, :, CK - 1], op=ALU.mult)
                    e.tensor_tensor(out=tD[:, :, :], in0=tD[:, :, :],
                                    in1=ends4[:, :, NCH - 1, :], op=ALU.add)
                    e.tensor_copy(out=tF[:, :, 0], in_=rint[:, :, CK - 1, 0])
                    e.tensor_scalar(out=tF[:, :, 1], in0=rint[:, :, CK - 1, 1], scalar1=-1.0,
                                    scalar2=None, op0=ALU.mult)
                    e.tensor_tensor(out=tA[:], in0=tF[:, :, 0], in1=tD[:, :, 0], op=ALU.mult)
                    e.tensor_tensor(out=tB[:], in0=tF[:, :, 1], in1=tD[:, :, 1], op=ALU.mult)
                    e.tensor_tensor(out=tC[:], in0=tF[:, :, 1], in1=tD[:, :, 0], op=ALU.mult)
                    e.tensor_tensor(out=tE[:, :, 1], in0=tF[:, :, 0], in1=tD[:, :, 1], op=ALU.mult)
                    e.tensor_tensor(out=Ev[:, :, 0], in0=tA[:], in1=tB[:], op=ALU.subtract)
                    e.tensor_tensor(out=Ev[:, :, 1], in0=tE[:, :, 1], in1=tC[:], op=ALU.add)
                blk.vector(_chains)

                def _pub(e, l=l):
                    e.wait_ge(SV.h, SV.n)
                    dma_w(e, GD, hcc_i[l].ap(), Ev[:].rearrange("p g c -> p (g c)"))
                    e.collective_compute("AllGather", ALU.bypass, replica_groups=[list(range(NCORE))],
                                         ins=[hcc_i[l].ap().opt()], outs=[hcc_o[l].ap().opt()]
                                         ).then_inc(CC.h, 1)
                    CC.inc(1)
                    e.wait_ge(CC.h, CC.n)
                    dma_w(e, GD, gat[:].rearrange("p s g c -> p s (g c)"),
                          hcc_o[l].ap().rearrange("(s p) c -> p s c", p=128))
                blk.gpsimd(_pub)

                def _comb(e, l=l):
                    e.wait_ge(GD.h, GD.n)
                    hbufs = [(Hre[:], Him[:]), (tE[:, :, 0], tE[:, :, 1])]
                    for j in range(NCORE):
                        mj = m8b[:, j:j + 1]
                        hr, hi = hbufs[j % 2]
                        nr, ni = hbufs[(j + 1) % 2]
                        e.tensor_scalar(out=tA[:], in0=pm1[:], scalar1=mj, scalar2=1.0,
                                        op0=ALU.mult, op1=ALU.add)
                        e.tensor_scalar(out=tB[:], in0=p2k[:, :, 1], scalar1=mj, scalar2=None, op0=ALU.mult)
                        e.tensor_scalar(out=tD[:, :, 0], in0=gat[:, j, :, 0], scalar1=mj, scalar2=None,
                                        op0=ALU.mult)
                        e.tensor_scalar(out=tD[:, :, 1], in0=gat[:, j, :, 1], scalar1=mj, scalar2=None,
                                        op0=ALU.mult)
                        e.tensor_tensor(out=tC[:], in0=tA[:], in1=hr, op=ALU.mult)
                        e.tensor_tensor(out=tF[:, :, 0], in0=tB[:], in1=hi, op=ALU.mult)
                        e.tensor_tensor(out=tF[:, :, 1], in0=tA[:], in1=hi, op=ALU.mult)
                        e.tensor_tensor(out=tB[:], in0=tB[:], in1=hr, op=ALU.mult)
                        e.tensor_tensor(out=tD[:, :, 0], in0=tD[:, :, 0], in1=tC[:], op=ALU.add)
                        e.tensor_tensor(out=tD[:, :, 1], in0=tD[:, :, 1], in1=tF[:, :, 1], op=ALU.add)
                        e.tensor_tensor(out=nr, in0=tD[:, :, 0], in1=tF[:, :, 0], op=ALU.subtract)
                        e.tensor_tensor(out=ni, in0=tD[:, :, 1], in1=tB[:], op=ALU.add)
                    hr, hi = hbufs[NCORE % 2]
                    e.tensor_tensor(out=tA[:], in0=ure44[:, l], in1=hr, op=ALU.mult)
                    e.tensor_tensor(out=tB[:], in0=uim44[:, l], in1=hi, op=ALU.mult)
                    e.tensor_tensor(out=tC[:], in0=ure44[:, l], in1=hi, op=ALU.mult)
                    e.tensor_tensor(out=tD[:, :, 0], in0=uim44[:, l], in1=hr, op=ALU.mult)
                    e.tensor_tensor(out=tA[:], in0=tA[:], in1=tB[:], op=ALU.subtract)
                    e.tensor_tensor(out=tB[:], in0=tC[:], in1=tD[:, :, 0], op=ALU.add)
                    for c in range(NCH):
                        qp = qpow[:, c]
                        e.tensor_tensor(out=tD[:, :, 0], in0=qp[:, :, 0], in1=tA[:], op=ALU.mult)
                        e.tensor_tensor(out=tC[:], in0=qp[:, :, 1], in1=tB[:], op=ALU.mult)
                        e.tensor_tensor(out=tD[:, :, 1], in0=qp[:, :, 0], in1=tB[:], op=ALU.mult)
                        e.tensor_tensor(out=tF[:, :, 0], in0=qp[:, :, 1], in1=tA[:], op=ALU.mult)
                        e.tensor_tensor(out=tD[:, :, 0], in0=tD[:, :, 0], in1=tC[:], op=ALU.subtract)
                        e.tensor_tensor(out=tD[:, :, 1], in0=tD[:, :, 1], in1=tF[:, :, 0], op=ALU.add)
                        for comp in range(2):
                            e.tensor_tensor(out=iot[:, :, c, comp], in0=iot[:, :, c, comp],
                                            in1=tD[:, :, comp], op=ALU.add)
                    e.tensor_tensor(out=iosc[:], in0=iot[:],
                                    in1=sclv[:].unsqueeze(2).unsqueeze(3).broadcast_to([128, NGD, NCH, 2]),
                                    op=ALU.mult)
                blk.vector(_comb)

                def _ctr(e):
                    e.wait_ge(SV.h, SV.n)
                    t12s = [ytmp[:, 0:CK * 2].rearrange("p (t x) -> p t x", x=2),
                            ytmp[:, CK * 2:CK * 4].rearrange("p (t x) -> p t x", x=2)]

                    def build(g):
                        e.tensor_tensor(out=t12s[g % 2], in0=rint[:, g],
                                        in1=dk4[:, g].unsqueeze(2).broadcast_to([128, CK, 2]),
                                        op=ALU.mult)

                    def stts(g):
                        for comp in range(2):
                            for c in range(NCH):
                                hs = h[:, g, c * CK:c * CK + CK]
                                e.scalar_tensor_tensor(out=hs, in0=t12s[g % 2][:, :, comp],
                                                       scalar=iosc[:, g, c, comp:comp + 1],
                                                       in1=hs, op0=ALU.mult, op1=ALU.add)
                    build(0)
                    build(1)
                    stts(0)
                    build(2)
                    stts(1)
                    build(3)
                    stts(2)
                    stts(3)
                blk.vector(_ctr)
                if l + 1 < NL:
                    tbl_done[l + 1] = emit_gp_tables(l + 1, SV.n)
                barrier(("vector", "scalar", "tensor"))

                attn_ln_stats()
                for g in range(NGD):
                    def _nr2(e, g=g):
                        if g == 0:
                            wait_all(e)
                        else:
                            e.wait_ge(SSm.h, SSm.n)
                        e.tensor_tensor(out=ytmp[:], in0=h[:, g, :], in1=rstdb[:], op=ALU.mult)
                        e.tensor_tensor(out=ytmp[:], in0=ytmp[:], in1=nub[:],
                                        op=ALU.subtract)
                    blk.vector(_nr2)

                    def _aff(e, g=g):
                        e.wait_ge(SV.h, SV.n)
                        e.activation(out=scrbf[:, g, :], in_=ytmp[:], func=AF.Identity,
                                     scale=ln2w[:, g:g + 1], bias=ln2b[:, g:g + 1]).then_inc(SSm.h, 1)
                    blk.scalar(_aff)
                    SSm.inc(1)
                barrier(("vector", "scalar", "tensor", "gpsimd"))

                for tt in range(NTT):
                    ss_f1 = SSm.n

                    def _mmf1(e, tt=tt, ss_f1=ss_f1):
                        if tt == 0:
                            wait_all(e)
                        else:
                            e.wait_ge(SSm.h, ss_f1)
                        for m in range(NGF):
                            bank = 2 * (m % 2)
                            if m >= 2:
                                e.wait_ge(SSm.h, ss_f1 + m - 1)
                            ins = None
                            for k in range(NGD):
                                ins = e.matmul(out=ps[bank][:, :], lhsT=w1b[:, k, m * 128:m * 128 + 128],
                                               rhs=scrbf[:, k, tt * TT:tt * TT + TT],
                                               start=(k == 0), stop=(k == NGD - 1))
                            ins.then_inc(ST.h, 1)
                    blk.tensor(_mmf1)
                    st_f1 = ST.n
                    ST.inc(NGF)

                    def _evf1(e, st_f1=st_f1):
                        for m in range(NGF):
                            bank = 2 * (m % 2)
                            e.wait_ge(ST.h, st_f1 + m + 1)
                            e.activation(out=midt[:, m, :], in_=ps[bank][:, :], func=AF.Silu,
                                         bias=b1v[:, m:m + 1]).then_inc(SSm.h, 1)
                    blk.scalar(_evf1)
                    SSm.inc(NGF)

                    sv_f2 = SV.n

                    def _mmf2(e, sv_f2=sv_f2):
                        e.wait_ge(SSm.h, SSm.n)
                        e.wait_ge(SV.h, sv_f2)
                        for m in range(NGD):
                            bank = 4 + 2 * (m % 2)
                            if m >= 2:
                                e.wait_ge(SV.h, sv_f2 + m - 1)
                            ins = None
                            for k in range(NGF):
                                ins = e.matmul(out=ps[bank][:, :], lhsT=w2b[:, k, m * 128:m * 128 + 128],
                                               rhs=midt[:, k, :], start=(k == 0), stop=(k == NGF - 1))
                            ins.then_inc(ST.h, 1)
                    blk.tensor(_mmf2)
                    st_f2 = ST.n
                    ST.inc(NGD)

                    def _evf2(e, tt=tt, st_f2=st_f2):
                        for m in range(NGD):
                            bank = 4 + 2 * (m % 2)
                            e.wait_ge(ST.h, st_f2 + m + 1)
                            hs = h[:, m, tt * TT:tt * TT + TT]
                            e.scalar_tensor_tensor(out=hs, in0=ps[bank][:, :], scalar=b2v[:, m:m + 1],
                                                   in1=hs, op0=ALU.add, op1=ALU.add)
                    blk.vector(_evf2)
                barrier()

                if debug:
                    def _dbg2(e, l=l):
                        wait_all(e)
                        dma_w(e, LD, dbg[f"h{l+1}"], h[:])
                    blk.sync(_dbg2)
                    barrier()

            attn_ln_stats()
            for g in range(NGD):
                def _nr4(e, g=g):
                    if g == 0:
                        wait_all(e)
                    else:
                        e.wait_ge(SSm.h, SSm.n)
                    e.tensor_tensor(out=ytmp[:], in0=h[:, g, :], in1=rstdb[:], op=ALU.mult)
                    e.tensor_tensor(out=ytmp[:], in0=ytmp[:], in1=nub[:],
                                    op=ALU.subtract)
                blk.vector(_nr4)

                def _aff4(e, g=g):
                    e.wait_ge(SV.h, SV.n)
                    e.activation(out=scrbf[:, g, :], in_=ytmp[:], func=AF.Identity,
                                 scale=lnlw[:, g:g + 1], bias=lnlb[:, g:g + 1]).then_inc(SSm.h, 1)
                blk.scalar(_aff4)
                SSm.inc(1)
            barrier()

            if debug:
                def _dbg3(e):
                    wait_all(e)
                    dma_w(e, LD, dbg["hn"], scrbf[:])
                blk.sync(_dbg3)
                barrier()

        if stages < 3:
            def _fin2(e):
                wait_all(e)
            for name in ("sync", "scalar", "vector", "gpsimd", "tensor"):
                getattr(blk, name)(_fin2)
            es2.close()
            return nc
        es2.close()
        with (
            nc.sbuf_tensor("xnf", [128, NGV, TOK], BF16) as xnf,
            nc.sbuf_tensor("wtb", [128, 2, NGV, 128], BF16) as wtb,
            nc.sbuf_tensor("woutm", [128, 2, NGD, 128], BF16) as woutm,
            nc.sbuf_tensor("btb", [64, VD], BF16) as btb,
            nc.sbuf_tensor("xm", [128, TOK], F32) as xm,
            nc.sbuf_tensor("outst", [128, 2, TOK], F32) as outst,
            nc.sbuf_tensor("biast", [128, NGV], F32) as biast,
            nc.sbuf_tensor("bias2", [128, NGV], F32) as bias2,
        ):
            def _l3(e):
                for i in range(NGV // 8):
                    dma_w(e, LD, xnf[:, i * 8:(i + 1) * 8, :],
                          dxn[i * 8:(i + 1) * 8].rearrange("g p t -> p g t"))
                dma_w(e, LD, biast[:], dbout)
                dma_w(e, LD, bias2[:], dbff)
            blk.sync(_l3)

            def _l3g(e):
                dma_w(e, GD, btb[:], dbT)
            blk.gpsimd(_l3g)

            def _bsum(e):
                wait_all(e)
                e.tensor_tensor(out=biast[:], in0=biast[:], in1=bias2[:], op=ALU.add)
            blk.vector(_bsum)
            barrier()

            st_mm = []
            for m in range(NGV):
                buf = m % 2
                ms = m * 128

                GWp = GW[buf]

                def _lw3(e, m=m, buf=buf, ms=ms, w=(st_mm[m - 2] if m >= 2 else 0), GWp=GWp):
                    if m >= 2:
                        e.wait_ge(ST.h, w)
                    e.dma_start(out=wtb[:, buf],
                                in_=dwff[:, ms:ms + 128].rearrange("(k p) f -> p k f", p=128)
                                ).then_inc(GWp.h, 16)
                    GWp.inc(16)
                    e.dma_start(out=woutm[:, buf],
                                in_=dwout[:, ms:ms + 128].rearrange("(k p) f -> p k f", p=128)
                                ).then_inc(GWp.h, 16)
                    GWp.inc(16)
                blk.gpsimd(_lw3)

                def _lx3(e, m=m, ms=ms):
                    if m >= 1:
                        e.wait_ge(SV.h, SV.n)
                    dma_w(e, LD, xm[:], dxT[ms:ms + 128, :])
                blk.sync(_lx3)

                gw_m = GW[buf].n
                sv_banks = SV.n

                def _mm5(e, m=m, buf=buf, gw_m=gw_m, sv_banks=sv_banks):
                    e.wait_ge(GW[buf].h, gw_m)
                    e.wait_ge(GD.h, GD.n)
                    if m >= 2:
                        e.wait_ge(SV.h, sv_banks - 4)
                    ins = None
                    for tt in range(NTT):
                        e.matmul(out=ps[4 * buf + tt][:, :], lhsT=btb[0:64, m * 128:m * 128 + 128],
                                 rhs=umask[0:64, tt * TT:tt * TT + TT], start=True, stop=False)
                    for k in range(NGD):
                        for tt in range(NTT):
                            e.matmul(out=ps[4 * buf + tt][:, :], lhsT=woutm[:, buf, k, :],
                                     rhs=scrbf[:, k, tt * TT:tt * TT + TT], start=False, stop=False)
                    for k in range(NGV):
                        for tt in range(NTT):
                            ins = e.matmul(out=ps[4 * buf + tt][:, :], lhsT=wtb[:, buf, k, :],
                                           rhs=xnf[:, k, tt * TT:tt * TT + TT], start=False,
                                           stop=(k == NGV - 1))
                    ins.then_inc(ST.h, 1)
                blk.tensor(_mm5)
                ST.inc(1)
                st_mm.append(ST.n)

                st_m = ST.n
                ld_m = LD.n

                def _ev5(e, m=m, buf=buf, st_m=st_m, ld_m=ld_m):
                    e.wait_ge(ST.h, st_m)
                    e.wait_ge(LD.h, ld_m)
                    for tt in range(NTT):
                        bank = 4 * buf + tt
                        e.scalar_tensor_tensor(out=outst[:, buf, tt * TT:tt * TT + TT],
                                               in0=ps[bank][:, :], scalar=biast[:, m:m + 1],
                                               in1=xm[:, tt * TT:tt * TT + TT],
                                               op0=ALU.add, op1=ALU.add)
                blk.vector(_ev5)

                sv_m = SV.n

                def _st5(e, buf=buf, ms=ms, sv_m=sv_m):
                    e.wait_ge(SV.h, sv_m)
                    dma_w(e, LD, dout[ms:ms + 128, :], outst[:, buf, :])
                blk.sync(_st5)

            def _fin(e):
                e.wait_ge(LD.h, LD.n)
            blk.sync(_fin)

            if debug:
                def _dbg4(e):
                    wait_all(e)
                    dma_w(e, LD, dbg["xn"], xnf[:])
                blk.sync(_dbg4)

    return nc


def _patm():
    import ml_dtypes
    p = np.zeros((4, 64), ml_dtypes.bfloat16)
    p[0, 0:8] = 1
    p[1, 8:16] = 1
    p[2, 16:32] = 1
    p[3, 32:64] = 1
    return p


def _prep(inputs):
    f = np.float32
    x = np.asarray(inputs["L0_attn_x"], f)[0]
    xT = np.ascontiguousarray(x.T)
    Win = np.asarray(inputs["L0_attn_Win"], f)
    A = np.asarray(inputs["L0_ffn_A"], f)
    gw = np.asarray(inputs["L0_ffn_gate_w"], f)
    com = {
        "winT": np.ascontiguousarray(Win.T),
        "agT": np.ascontiguousarray(np.concatenate([A.T, gw.T], axis=1)),
        "w1T": np.ascontiguousarray(np.asarray(inputs["L0_attn_W1"], f).transpose(0, 2, 1)),
        "w2T": np.ascontiguousarray(np.asarray(inputs["L0_attn_W2"], f).transpose(0, 2, 1)),
        "woutT": np.ascontiguousarray(np.asarray(inputs["L0_attn_Wout"], f).T),
        "wT": np.ascontiguousarray(np.asarray(inputs["L0_ffn_W"], f).T),
        "bT": np.ascontiguousarray(np.asarray(inputs["L0_ffn_B"], f).T),
        "bin_r": np.ascontiguousarray(np.asarray(inputs["L0_attn_bin"], f).reshape(NGD, 128).T),
        "lnlw_r": np.ascontiguousarray(np.asarray(inputs["L0_attn_lnl_w"], f).reshape(NGD, 128).T),
        "lnlb_r": np.ascontiguousarray(np.asarray(inputs["L0_attn_lnl_b"], f).reshape(NGD, 128).T),
        "bout_r": np.ascontiguousarray(np.asarray(inputs["L0_attn_bout"], f).reshape(NGV, 128).T),
        "bff_r": np.ascontiguousarray(np.asarray(inputs["L0_ffn_b"], f).reshape(NGV, 128).T),
        "gateb_r": np.ascontiguousarray(np.asarray(inputs["L0_ffn_gate_b"], f).reshape(1, 4)),
        "patm": _patm(),
    }
    parts = [np.asarray(inputs[key], f).reshape(NL, NGD, 128).transpose(0, 2, 1)
             for key in ("L0_attn_ln1_w", "L0_attn_ln1_b", "L0_attn_ln2_w",
                         "L0_attn_ln2_b", "L0_attn_sc_lin", "L0_attn_b2",
                         "L0_attn_phz_re", "L0_attn_phz_im", "L0_attn_lc_re",
                         "L0_attn_lc_im")]
    parts.append(np.asarray(inputs["L0_attn_b1"], f).reshape(NL, NGF, 128).transpose(0, 2, 1))
    com["lcon_r"] = np.ascontiguousarray(np.concatenate(parts, axis=2))
    in_maps = []
    for c in range(NCORE):
        mp = dict(com)
        mp["xT"] = np.ascontiguousarray(xT[:, c * TOK:(c + 1) * TOK])
        mp["mask8"] = np.ascontiguousarray(
            np.array([1.0] * c + [0.0] * (NCORE - c), f).reshape(1, 8))
        in_maps.append(mp)
    return in_maps


def kernel(**inputs):
    in_maps = _prep(inputs)
    nc = build_nc(debug=DEBUG, stages=STAGES)
    res = run_bass_kernel_spmd(nc, in_maps, core_ids=list(range(NCORE)))
    out = np.concatenate([np.asarray(res.results[c]["outT"]) for c in range(NCORE)], axis=1)
    if DEBUG:
        kernel._last = res
    return np.ascontiguousarray(out.T).reshape(1, 16384, 4096).astype(np.float32)
